# version 04
# baseline (speedup 1.0000x reference)
import numpy as np
import ml_dtypes
from contextlib import ExitStack

import concourse.bass as bass
import concourse.mybir as mybir
from concourse.bass_utils import run_bass_kernel_spmd

F32 = mybir.dt.float32
BF16 = mybir.dt.bfloat16
I32 = mybir.dt.int32
AF = mybir.ActivationFunctionType
ALU = mybir.AluOpType
AX = mybir.AxisListType
BF = ml_dtypes.bfloat16

D = 2048
NC = 16
T = 1024
SEQ = 2048
DH = 128
NEX = 18
ICH = 11
I_MOE = 1408
L1HCH = 44
L1HID = 5632
NB = 4
EPS_LN = 1e-5
EPS_RMS = 1e-6
SOFTCAP = 50.0
SC_ATT = DH ** -0.5

ENGINES = ("sync", "tensor", "vector", "scalar", "gpsimd")
N_DMA_SEM = 12
GROUPS = [[0, 1], [2, 3], [4, 5], [6, 7]]
LAUNCH_PLAN = [("a0", "f0", "a1", "f1")]
DEBUG_TAP = None
DEBUG_STOP = None


class Region:
    __slots__ = ("space", "lo", "hi")

    def __init__(self, space, lo=0, hi=1 << 62):
        self.space, self.lo, self.hi = space, lo, hi

    def sub(self, lo_b, nb):
        return Region(self.space, self.lo + lo_b, self.lo + lo_b + nb)

    def overlaps(self, o):
        return self.space == o.space and self.lo < o.hi and o.lo < self.hi


class Prog:
    def __init__(self):
        self.ops = {e: [] for e in ENGINES}
        self.counts = {}
        self.writers = []
        self.readers = []
        self.dma_rr = {}
        self.sem_names = set()
        self.final_tokens = []
        self.oplog = []
        self.n_ops = 0

    def _sem(self, name):
        self.sem_names.add(name)
        return name

    def op(self, eng, fn, reads=(), writes=(), dma=False, coll=False, tag=""):
        waits = {}

        def need(tok):
            s, v = tok
            if waits.get(s, 0) < v:
                waits[s] = v

        for r in reads:
            for reg, tok in self.writers:
                if r.overlaps(reg):
                    need(tok)
        for w in writes:
            for reg, tok in self.writers:
                if w.overlaps(reg):
                    need(tok)
            for reg, tok in self.readers:
                if w.overlaps(reg):
                    need(tok)
        if dma:
            i = self.dma_rr.get(eng, 0)
            self.dma_rr[eng] = i + 1
            sem = self._sem(f"dq_{eng}_{i % N_DMA_SEM}")
            inc = 16
        elif coll:
            sem = self._sem("cc")
            inc = 1
        else:
            sem = self._sem(f"tk_{eng}")
            inc = 1
        val = self.counts.get(sem, 0) + inc
        self.counts[sem] = val
        tok = (sem, val)
        self.ops[eng].append((fn, waits, sem, inc, tag))
        self.oplog.append((eng, tag, [(w.space, w.lo, w.hi) for w in writes],
                           [(r.space, r.lo, r.hi) for r in reads]))
        self.n_ops += 1
        def contained(g, w):
            return g.space == w.space and g.lo >= w.lo and g.hi <= w.hi

        for w in writes:
            self.writers = [(g, t) for g, t in self.writers
                            if not contained(g, w)]
            self.readers = [(g, t) for g, t in self.readers
                            if not contained(g, w)]
            self.writers.append((w, tok))
        for r in reads:
            kept = []
            for g, t in self.readers:
                if (g.space == r.space and g.lo == r.lo and g.hi == r.hi
                        and t[0] == tok[0]):
                    continue
                kept.append((g, t))
            kept.append((r, tok))
            self.readers = kept
        return tok

    def emit(self, nc):
        with ExitStack() as st:
            sems = {n: st.enter_context(nc.semaphore(n))
                    for n in sorted(self.sem_names)}
            with nc.Block() as block:
                for ename in ENGINES:
                    ops = self.ops[ename]
                    fin = self.final_tokens if ename == "sync" else []
                    if not ops and not fin:
                        continue

                    def mkbody(ops=ops, fin=fin):
                        def body(eng):
                            water = {}
                            for fn, waits, sem, inc, tag in ops:
                                for s in sorted(waits):
                                    v = waits[s]
                                    if water.get(s, 0) < v:
                                        eng.wait_ge(sems[s], v)
                                        water[s] = v
                                ins = fn(eng)
                                ins.then_inc(sems[sem], inc)
                            for s, v in fin:
                                if water.get(s, 0) < v:
                                    eng.wait_ge(sems[s], v)
                                    water[s] = v
                        return body

                    getattr(block, ename)(mkbody())


class Arena:
    def __init__(self, ap_u8, nbytes):
        self.ap = ap_u8
        self.nbytes = nbytes
        self.off = 0
        self.marks = []
        self.peak = 0

    def tile(self, cols, dtype, parts=128):
        esz = mybir.dt.size(dtype)
        nb = cols * esz
        nb_al = (nb + 31) & ~31
        assert self.off + nb_al <= self.nbytes, \
            f"SBUF arena overflow: {self.off}+{nb_al} > {self.nbytes}"
        lo = self.off
        self.off += nb_al
        self.peak = max(self.peak, self.off)
        ap = self.ap[:, lo:lo + nb].bitcast(dtype)
        if parts != 128:
            ap = ap[0:parts, :]
        return ap, Region("sbuf", lo, lo + nb)

    def push(self):
        self.marks.append(self.off)

    def pop(self):
        self.off = self.marks.pop()


class Rot:
    def __init__(self, A, n, cols, dtype, parts=128):
        self.items = [A.tile(cols, dtype, parts) for _ in range(n)]
        self.i = 0

    def nxt(self):
        it = self.items[self.i % len(self.items)]
        self.i += 1
        return it


def build_program(phases):
    nc = bass.Bass("TRN2", target_bir_lowering=False)
    P = Prog()
    dram = {}

    def din(name, shape, dtype):
        dram[name] = nc.dram_tensor(name, list(shape), dtype,
                                    kind="ExternalInput").ap()
        return dram[name]

    def dreg(name):
        return Region("dram:" + name)

    x_in = din("x", (D, T), F32)
    x_out = nc.dram_tensor("xout", [D, T], F32, kind="ExternalOutput").ap()
    x_spill = nc.dram_tensor("x_spill", [D, T], F32).ap()
    dbg_state = {}

    def debug_tap(tag, ap, R, n_ch, dtype):
        if DEBUG_TAP != tag or "done" in dbg_state:
            return
        dbg_state["done"] = True
        dbg = nc.dram_tensor("dbg", [n_ch * 128, T], dtype,
                             kind="ExternalOutput").ap()
        tok = P.op("sync", lambda e: e.dma_start(
            out=dbg.rearrange("(c p) t -> p c t", p=128),
            in_=ap.rearrange("p (c t) -> p c t", c=n_ch)),
            reads=[R], writes=[Region("dram:dbg")], dma=True)
        P.final_tokens.append(tok)

    need = set(phases)
    if need & {"a0", "a1"}:
        din("pmask", (128, 16 * T), BF16)
        din("identb", (128, 128), BF16)
    if "a0" in need:
        din("wqT0", (16, 128, D), F32)
        din("wkT0", (16, 128, D), F32)
        din("wvT0", (16, 128, D), F32)
        din("woT0", (16, 128, D), F32)
        din("qnw", (1, 128), F32)
        din("knw", (1, 128), F32)
        din("ropeHC", (128, 16), F32)
        din("ropeHS", (128, 16), F32)
    if "f0" in need:
        din("WgT", (128, 256), F32)
        din("identf", (128, 128), F32)
        din("esel", (16, 16 * 128), F32)
        din("W1T", (NEX, ICH, 128, D), F32)
        din("W3T", (NEX, ICH, 128, D), F32)
        din("W2T", (NEX, 16, 128, ICH * 128), F32)
    if "a1" in need:
        din("wq1", (16, 128, D), F32)
        din("wk1", (4, 128, D), F32)
        din("wv1", (4, 128, D), F32)
        din("wo1", (16, 128, D), F32)
        din("bq1", (128, 16), F32)
        din("bk1", (128, 4), F32)
        din("bv1", (128, 4), F32)
        din("bo1", (128, 16), F32)
    if "f1" in need:
        din("wgT1", (L1HCH, 128, D), I32)
        din("wuT1", (L1HCH, 128, D), I32)
        din("wdT1", (16, 128, L1HID), I32)
        din("sg1", (128, L1HCH), F32)
        din("su1", (128, L1HCH), F32)
        din("sd1", (128, 16), F32)

    cc = {}
    if "a0" in need:
        cc["k0i"] = nc.dram_tensor("k0i", [D, T], BF16).ap()
        cc["k0g"] = nc.dram_tensor("k0g", [2 * D, T], BF16).ap()
        cc["v0i"] = nc.dram_tensor("v0i", [T, D], BF16).ap()
        cc["v0g"] = nc.dram_tensor("v0g", [2 * T, D], BF16).ap()
    if "a1" in need:
        cc["k1i"] = nc.dram_tensor("k1i", [512, T], BF16).ap()
        cc["k1g"] = nc.dram_tensor("k1g", [1024, T], BF16).ap()
        cc["v1i"] = nc.dram_tensor("v1i", [T, 512], BF16).ap()
        cc["v1g"] = nc.dram_tensor("v1g", [2 * T, 512], BF16).ap()

    with ExitStack() as st:
        ARENA = 207 * 1024
        arena_t = st.enter_context(
            nc.sbuf_tensor("arena", [128, ARENA], mybir.dt.uint8))
        A = Arena(arena_t[:, :], ARENA)
        PBT = st.enter_context(nc.psum_tensor("pball", [128, 4096], F32))
        PBALL = PBT[:, :]
        PB = [PBALL[:, 512 * k:512 * (k + 1)] for k in range(8)]
        PBR = [Region(f"ps{k}") for k in range(8)]
        PBP = [PBALL[:, 1024 * p:1024 * (p + 1)] for p in range(4)]
        PBbf = [PB[k].bitcast(BF16) for k in range(8)]

        ones_col, R_onc = A.tile(1, BF16)
        onesf_row, R_onfr = A.tile(128, F32, parts=1)
        c128f_row, R_c128 = A.tile(128, F32, parts=1)
        rowbuf, R_rows = A.tile(2048, F32, parts=1)

        def rows(off, n):
            return rowbuf[0:1, off:off + n], R_rows.sub(4 * off, 4 * n)

        eps_ln_t, R_epl = A.tile(1, F32, parts=1)
        eps_rms_t, R_epr = A.tile(1, F32, parts=1)
        P.op("gpsimd", lambda e: e.memset(ones_col, 1.0), writes=[R_onc])
        P.op("gpsimd", lambda e: e.memset(onesf_row, 1.0), writes=[R_onfr])
        P.op("gpsimd", lambda e: e.memset(c128f_row, 128.0), writes=[R_c128])
        P.op("gpsimd", lambda e: e.memset(eps_ln_t, EPS_LN), writes=[R_epl])
        P.op("gpsimd", lambda e: e.memset(eps_rms_t, EPS_RMS), writes=[R_epr])

        def ln_stats_bc(chunk_src, n_ch, n_feat, eps, tag):
            bc_r, R_bcr = A.tile(T, F32)
            bc_m, R_bcm = A.tile(T, F32)
            A.push()
            sqr = Rot(A, 2, T, BF16)
            for c in range(n_ch):
                src_ap, src_regs = chunk_src(c)
                sq, R_sq = sqr.nxt()
                P.op("scalar", lambda e, src_ap=src_ap, sq=sq: e.activation(
                    out=sq, in_=src_ap, func=AF.Square),
                    reads=list(src_regs), writes=[R_sq], tag=tag + f".sq{c}")
                for tt in range(2):
                    P.op("tensor", lambda e, src_ap=src_ap, tt=tt, c=c: e.matmul(
                        PB[4 + tt][0:1, :], ones_col,
                        src_ap[:, tt * 512:tt * 512 + 512],
                        start=(c == 0), stop=(c == n_ch - 1)),
                        reads=list(src_regs) + [R_onc], writes=[PBR[4 + tt]],
                        tag=tag + f".s1_{c}{tt}")
                    P.op("tensor", lambda e, sq=sq, tt=tt, c=c: e.matmul(
                        PB[6 + tt][0:1, :], ones_col,
                        sq[:, tt * 512:tt * 512 + 512],
                        start=(c == 0), stop=(c == n_ch - 1)),
                        reads=[R_sq, R_onc], writes=[PBR[6 + tt]],
                        tag=tag + f".s2_{c}{tt}")
            A.pop()
            m_ap, m_reg = rows(0, T)
            v_ap, v_reg = rows(T, T)
            inv = 1.0 / n_feat
            for tt in range(2):
                P.op("vector", lambda e, tt=tt: e.tensor_scalar(
                    m_ap[:, tt * 512:tt * 512 + 512], PB[4 + tt][0:1, :], inv,
                    None, ALU.mult), reads=[PBR[4 + tt]], writes=[m_reg])
                P.op("vector", lambda e, tt=tt: e.tensor_scalar(
                    v_ap[:, tt * 512:tt * 512 + 512], PB[6 + tt][0:1, :], inv,
                    None, ALU.mult), reads=[PBR[6 + tt]], writes=[v_reg])
            P.op("scalar", lambda e: e.activation(
                out=PB[4][0:1, 0:512], in_=m_ap[:, 0:512], func=AF.Square),
                reads=[m_reg], writes=[PBR[4]])
            P.op("scalar", lambda e: e.activation(
                out=PB[5][0:1, 0:512], in_=m_ap[:, 512:1024], func=AF.Square),
                reads=[m_reg], writes=[PBR[5]])
            P.op("vector", lambda e: e.tensor_sub(
                v_ap[:, 0:512], v_ap[:, 0:512], PB[4][0:1, 0:512]),
                reads=[v_reg, PBR[4]], writes=[v_reg])
            P.op("vector", lambda e: e.tensor_sub(
                v_ap[:, 512:1024], v_ap[:, 512:1024], PB[5][0:1, 0:512]),
                reads=[v_reg, PBR[5]], writes=[v_reg])
            P.op("scalar", lambda e: e.activation(out=v_ap, in_=v_ap, func=AF.Ln,
                                                  bias=eps_ln_t),
                 reads=[v_reg, R_epl], writes=[v_reg])
            P.op("scalar", lambda e: e.activation(out=v_ap, in_=v_ap,
                                                  func=AF.Exp, scale=-0.5),
                 reads=[v_reg], writes=[v_reg])
            P.op("vector", lambda e: e.tensor_mul(m_ap, m_ap, v_ap),
                 reads=[m_reg, v_reg], writes=[m_reg])
            for tt in range(2):
                P.op("tensor", lambda e, tt=tt: e.matmul(
                    PB[4 + tt], onesf_row, v_ap[:, tt * 512:tt * 512 + 512],
                    start=True, stop=True),
                    reads=[v_reg, R_onfr], writes=[PBR[4 + tt]])
                P.op("tensor", lambda e, tt=tt: e.matmul(
                    PB[6 + tt], onesf_row, m_ap[:, tt * 512:tt * 512 + 512],
                    start=True, stop=True),
                    reads=[m_reg, R_onfr], writes=[PBR[6 + tt]])
                P.op("scalar", lambda e, tt=tt: e.copy(
                    bc_r[:, tt * 512:tt * 512 + 512], PB[4 + tt]),
                    reads=[PBR[4 + tt]], writes=[R_bcr])
                P.op("scalar", lambda e, tt=tt: e.copy(
                    bc_m[:, tt * 512:tt * 512 + 512], PB[6 + tt]),
                    reads=[PBR[6 + tt]], writes=[R_bcm])
            return bc_r, R_bcr, bc_m, R_bcm

        def emit_preln(src_ap, src_name, xn_bf, R_xn, tag, f32_cb=None):
            A.push()
            xcr = Rot(A, 3, T, F32)
            xbr = Rot(A, 2, T, BF16)

            def load_chunk(c, xc, R_xc):
                P.op("sync", lambda e, c=c, xc=xc: e.dma_start(
                    out=xc, in_=src_ap[c * 128:(c + 1) * 128, :]),
                    reads=[dreg(src_name)], writes=[R_xc], dma=True,
                    tag=tag + f".ldx{c}")

            chunk_cache = {}

            def chunk_src(c):
                xc, R_xc = xcr.nxt()
                load_chunk(c, xc, R_xc)
                xb, R_xb = xbr.nxt()
                P.op("vector", lambda e, xb=xb, xc=xc: e.tensor_copy(xb, xc),
                     reads=[R_xc], writes=[R_xb])
                return xb, [R_xb]

            bc_r, R_bcr, bc_m, R_bcm = ln_stats_bc(chunk_src, NC, D, EPS_LN, tag)
            tmp = Rot(A, 2, T, F32)
            for c in range(NC):
                xc, R_xc = xcr.nxt()
                load_chunk(c, xc, R_xc)
                tf, R_tf = tmp.nxt()
                R_nc = R_xn.sub(2 * c * T, 2 * T)
                P.op("vector", lambda e, xc=xc, tf=tf: e.tensor_mul(
                    tf, xc, bc_r), reads=[R_xc, R_bcr], writes=[R_tf])
                if f32_cb is None:
                    P.op("vector", lambda e, c=c, tf=tf: e.tensor_sub(
                        xn_bf[:, c * T:(c + 1) * T], tf, bc_m),
                        reads=[R_tf, R_bcm], writes=[R_nc])
                else:
                    P.op("vector", lambda e, tf=tf: e.tensor_sub(tf, tf, bc_m),
                         reads=[R_tf, R_bcm], writes=[R_tf])
                    P.op("gpsimd", lambda e, c=c, tf=tf: e.tensor_copy(
                        xn_bf[:, c * T:(c + 1) * T], tf),
                        reads=[R_tf], writes=[R_nc])
                    f32_cb(c, tf, R_tf)
            A.pop()

        def emit_postres(src_ap, src_name, dst_ap, dst_name, out_bf, R_out,
                         tag):
            A.push()

            def chunk_src(c):
                return (out_bf[:, c * T:(c + 1) * T],
                        [R_out.sub(2 * c * T, 2 * T)])

            bc_r, R_bcr, bc_m, R_bcm = ln_stats_bc(chunk_src, NC, D, EPS_LN,
                                                   tag)
            xcr = Rot(A, 3, T, F32)
            xnr = Rot(A, 3, T, F32)
            tmp = Rot(A, 2, T, F32)
            store_toks = []
            for c in range(NC):
                xc, R_xc = xcr.nxt()
                P.op("sync", lambda e, c=c, xc=xc: e.dma_start(
                    out=xc, in_=src_ap[c * 128:(c + 1) * 128, :]),
                    reads=[dreg(src_name)], writes=[R_xc], dma=True)
                tf, R_tf = tmp.nxt()
                P.op("vector", lambda e, c=c, tf=tf: e.tensor_mul(
                    tf, out_bf[:, c * T:(c + 1) * T], bc_r),
                    reads=[R_out.sub(2 * c * T, 2 * T), R_bcr], writes=[R_tf])
                P.op("vector", lambda e, tf=tf: e.tensor_sub(tf, tf, bc_m),
                     reads=[R_tf, R_bcm], writes=[R_tf])
                xo, R_xo = xnr.nxt()
                P.op("vector", lambda e, xo=xo, xc=xc, tf=tf: e.tensor_add(
                    xo, xc, tf), reads=[R_xc, R_tf], writes=[R_xo])
                tok = P.op("sync", lambda e, c=c, xo=xo: e.dma_start(
                    out=dst_ap[c * 128:(c + 1) * 128, :], in_=xo),
                    reads=[R_xo], writes=[dreg(dst_name) if False else
                                          Region("dram:" + dst_name,
                                                 c * 128 * T * 4,
                                                 (c + 1) * 128 * T * 4)],
                    dma=True)
                store_toks.append(tok)
            A.pop()
            return store_toks

        def emit_proj(w_hbm, wname, n_out_ch, src_bf, R_src, out_cb, tag,
                      in_ch=NC, nbuf=3):
            A.push()
            wt = Rot(A, nbuf, in_ch * 128, BF16)
            for o in range(n_out_ch):
                w, R_w = wt.nxt()

                def ldw(e, o=o, w=w):
                    return e.dma_start(out=w, in_=w_hbm[o])
                P.op("gpsimd", ldw, reads=[dreg(wname)], writes=[R_w], dma=True,
                     tag=tag + f".w{o}")
                p = o % 2
                for tt in range(2):
                    bank = p * 2 + tt

                    def mm(e, tt=tt, w=w, bank=bank):
                        ins = None
                        for c in range(in_ch):
                            ins = e.matmul(
                                PB[bank], w[:, c * 128:(c + 1) * 128],
                                src_bf[:, c * T + tt * 512: c * T + tt * 512 + 512],
                                start=(c == 0), stop=(c == in_ch - 1))
                        return ins
                    P.op("tensor", mm, reads=[R_w, R_src], writes=[PBR[bank]],
                         tag=tag + f".mm{o}_{tt}")
                out_cb(o, PBP[p], [PBR[p * 2], PBR[p * 2 + 1]])
            A.pop()

        def phase_attn(layer, first, last):
            is0 = (layer == "a0")
            kvh_n = 16 if is0 else 4
            kv_dim = kvh_n * 128
            kgname, vgname = ("k0g", "v0g") if is0 else ("k1g", "v1g")
            kiname, viname = ("k0i", "v0i") if is0 else ("k1i", "v1i")
            src_name = "x" if first else "x_spill"
            src_ap = x_in if first else x_spill
            dst_name = "xout" if last else "x_spill"
            dst_ap = x_out if last else x_spill

            A.push()
            q_bf, R_q = A.tile(16 * T, BF16)
            ao_bf, R_ao = A.tile(16 * T, BF16)
            idb_sb, R_idb = A.tile(128, BF16)
            P.op("sync", lambda e: e.dma_start(out=idb_sb, in_=dram["identb"]),
                 reads=[dreg("identb")], writes=[R_idb], dma=True)
            if is0:
                ropeC_sb, R_rc = A.tile(16, F32)
                ropeS_sb, R_rs = A.tile(16, F32)
                qnw_sb, R_qnw = A.tile(128, F32, parts=1)
                knw_sb, R_knw = A.tile(128, F32, parts=1)
                for nm, ap_, rg in (("ropeHC", ropeC_sb, R_rc),
                                    ("ropeHS", ropeS_sb, R_rs),
                                    ("qnw", qnw_sb, R_qnw),
                                    ("knw", knw_sb, R_knw)):
                    P.op("sync", lambda e, nm=nm, ap_=ap_: e.dma_start(
                        out=ap_, in_=dram[nm]),
                        reads=[dreg(nm)], writes=[rg], dma=True)
            else:
                bq_sb, R_bq = A.tile(16, F32)
                bk_sb, R_bk = A.tile(4, F32)
                bv_sb, R_bv = A.tile(4, F32)
                bo_sb, R_bo = A.tile(16, F32)
                for nm, ap_, rg in (("bq1", bq_sb, R_bq), ("bk1", bk_sb, R_bk),
                                    ("bv1", bv_sb, R_bv), ("bo1", bo_sb, R_bo)):
                    P.op("sync", lambda e, nm=nm, ap_=ap_: e.dma_start(
                        out=ap_, in_=dram[nm]),
                        reads=[dreg(nm)], writes=[rg], dma=True)

            A.push()
            xn_bf, R_xn = A.tile(NC * T, BF16)
            emit_preln(src_ap, src_name, xn_bf, R_xn, layer + ".ln")

            praw = Rot(A, 2, T, F32)
            sqr = Rot(A, 2, T, BF16)
            bctr0 = Rot(A, 2, T, F32)
            qnr = Rot(A, 2, T, F32)
            t0r = Rot(A, 2, T, F32)
            khead = Rot(A, 2, T, BF16)
            vhead = Rot(A, 2, T, BF16)
            vtmp = Rot(A, 2, T, BF16)

            def stage_k(o, kh, R_kh):
                P.op("sync", lambda e, o=o, kh=kh: e.dma_start(
                    out=cc[kiname][o * 128:(o + 1) * 128, :], in_=kh),
                    reads=[R_kh],
                    writes=[Region("dram:" + kiname, o * 128 * T * 2,
                                   (o + 1) * 128 * T * 2)], dma=True)

            def stage_v(o, vh, R_vh):
                P.op("sync", lambda e, o=o, vh=vh: e.dma_start(
                    out=cc[viname][:, o * 128:(o + 1) * 128].rearrange(
                        "(tc p) d -> p tc d", p=128),
                    in_=vh.rearrange("p (tc d) -> p tc d", d=128)),
                    reads=[R_vh],
                    writes=[Region("dram:" + viname, o, o + 1)], dma=True)

            def qk_post_l0(sink, nw_sb, R_nw):
                def cb(o, pp, pregs):
                    pr, R_pr = praw.nxt()
                    P.op("scalar", lambda e, pr=pr, pp=pp: e.copy(pr, pp),
                         reads=pregs, writes=[R_pr])
                    sqb, R_sqb = sqr.nxt()
                    P.op("scalar", lambda e, sqb=sqb, pr=pr: e.activation(
                        out=sqb, in_=pr, func=AF.Square),
                        reads=[R_pr], writes=[R_sqb])
                    roff = 1024 * (o % 2)
                    rs_ap, rs_reg = rows(roff, 1024)
                    for tt in range(2):
                        P.op("tensor", lambda e, sqb=sqb, tt=tt: e.matmul(
                            PB[4 + tt][0:1, :], ones_col,
                            sqb[:, tt * 512:tt * 512 + 512],
                            start=True, stop=True),
                            reads=[R_sqb, R_onc], writes=[PBR[4 + tt]])
                        P.op("scalar", lambda e, rs_ap=rs_ap, tt=tt:
                             e.activation(out=rs_ap[:, tt * 512:tt * 512 + 512],
                                          in_=PB[4 + tt][0:1, :], func=AF.Ln,
                                          scale=1.0 / DH, bias=eps_rms_t),
                             reads=[PBR[4 + tt], R_epr], writes=[rs_reg])
                    P.op("scalar", lambda e, rs_ap=rs_ap: e.activation(
                        out=rs_ap, in_=rs_ap, func=AF.Exp, scale=-0.5),
                        reads=[rs_reg], writes=[rs_reg])
                    for tt in range(2):
                        P.op("tensor", lambda e, rs_ap=rs_ap, tt=tt: e.matmul(
                            PB[4 + tt], nw_sb,
                            rs_ap[:, tt * 512:tt * 512 + 512],
                            start=True, stop=True),
                            reads=[rs_reg, R_nw], writes=[PBR[4 + tt]])
                    bct, R_bct = bctr0.nxt()
                    P.op("scalar", lambda e, bct=bct: e.copy(bct, PBP[2]),
                         reads=[PBR[4], PBR[5]], writes=[R_bct])
                    qn, R_qn = qnr.nxt()
                    P.op("vector", lambda e, qn=qn, pr=pr, bct=bct:
                         e.tensor_mul(qn, pr, bct),
                         reads=[R_pr, R_bct], writes=[R_qn])
                    t0, R_t0 = t0r.nxt()
                    P.op("vector", lambda e, t0=t0, qn=qn, o=o:
                         e.tensor_scalar(t0, qn, ropeC_sb[:, o:o + 1], None,
                                         ALU.mult),
                         reads=[R_qn, R_rc], writes=[R_t0])
                    P.op("vector", lambda e, qn=qn, o=o: e.tensor_scalar(
                        PBP[2], qn, ropeS_sb[:, o:o + 1], None, ALU.mult),
                        reads=[R_qn, R_rs], writes=[PBR[4], PBR[5]])
                    d_sl, R_dsl = sink(o)
                    P.op("vector", lambda e, d_sl=d_sl, t0=t0: e.tensor_sub(
                        d_sl[0:64, :], t0[0:64, :], PBP[2][64:128, :]),
                        reads=[R_t0, PBR[4], PBR[5]], writes=[R_dsl])
                    P.op("vector", lambda e, d_sl=d_sl, t0=t0: e.tensor_add(
                        d_sl[64:128, :], t0[64:128, :], PBP[2][0:64, :]),
                        reads=[R_t0, PBR[4], PBR[5]], writes=[R_dsl])
                return cb

            kstate = {}

            def k_sink(o):
                kstate["t"] = khead.nxt()
                return kstate["t"]

            def k_post_wrap(cb):
                def cb2(o, pp, pregs):
                    cb(o, pp, pregs)
                    kh, R_kh = kstate["t"]
                    stage_k(o, kh, R_kh)
                return cb2

            def bias_sink_cb(sink, b_sb, R_b, stage_fn=None):
                def cb(o, pp, pregs):
                    d_sl, R_dsl = sink(o)
                    if b_sb is None:
                        P.op("scalar", lambda e, d_sl=d_sl, pp=pp: e.copy(
                            d_sl, pp), reads=pregs, writes=[R_dsl])
                    else:
                        P.op("vector", lambda e, d_sl=d_sl, pp=pp, o=o:
                             e.tensor_scalar(d_sl, pp, b_sb[:, o:o + 1],
                                             None, ALU.add),
                             reads=pregs + [R_b], writes=[R_dsl])
                    if stage_fn is not None:
                        stage_fn(o)
                return cb

            if is0:
                emit_proj(dram["wkT0"], "wkT0", kvh_n, xn_bf, R_xn,
                          k_post_wrap(qk_post_l0(k_sink, knw_sb, R_knw)),
                          layer + ".k")
            else:
                emit_proj(dram["wk1"], "wk1", kvh_n, xn_bf, R_xn,
                          bias_sink_cb(k_sink, bk_sb, R_bk,
                                       lambda o: stage_k(*((o,) + kstate["t"]))),
                          layer + ".k")

            def v_evac_cb(o, pp, pregs):
                vt, R_vtt = vtmp.nxt()
                if is0:
                    P.op("scalar", lambda e, vt=vt, pp=pp: e.copy(vt, pp),
                         reads=pregs, writes=[R_vtt])
                else:
                    P.op("vector", lambda e, vt=vt, pp=pp, o=o:
                         e.tensor_scalar(vt, pp, bv_sb[:, o:o + 1], None,
                                         ALU.add),
                         reads=pregs + [R_bv], writes=[R_vtt])
                vh, R_vh = vhead.nxt()
                for tc in range(8):
                    bank2 = 4 + tc % 2
                    P.op("tensor", lambda e, vt=vt, tc=tc, bank2=bank2:
                         e.transpose(PBbf[bank2][:, 0:128],
                                     vt[:, tc * 128:tc * 128 + 128], idb_sb),
                         reads=[R_vtt, R_idb], writes=[PBR[bank2]])
                    P.op("vector", lambda e, vh=vh, tc=tc, bank2=bank2:
                         e.tensor_copy(vh[:, tc * 128:tc * 128 + 128],
                                       PBbf[bank2][:, 0:128]),
                         reads=[PBR[bank2]], writes=[R_vh])
                stage_v(o, vh, R_vh)

            if is0:
                emit_proj(dram["wvT0"], "wvT0", kvh_n, xn_bf, R_xn, v_evac_cb,
                          layer + ".v")
            else:
                emit_proj(dram["wv1"], "wv1", kvh_n, xn_bf, R_xn, v_evac_cb,
                          layer + ".v")

            P.op("gpsimd", lambda e: e.collective_compute(
                "AllGather", ALU.bypass, replica_groups=GROUPS,
                ins=[cc[kiname]], outs=[cc[kgname]]),
                reads=[Region("dram:" + kiname)], writes=[dreg(kgname)],
                coll=True)
            P.op("gpsimd", lambda e: e.collective_compute(
                "AllGather", ALU.bypass, replica_groups=GROUPS,
                ins=[cc[viname]], outs=[cc[vgname]]),
                reads=[Region("dram:" + viname)], writes=[dreg(vgname)],
                coll=True)

            def q_sink(o):
                sl = q_bf[:, o * T:(o + 1) * T]
                return sl, R_q.sub(2 * o * T, 2 * T)

            if is0:
                emit_proj(dram["wqT0"], "wqT0", 16, xn_bf, R_xn,
                          qk_post_l0(q_sink, qnw_sb, R_qnw), layer + ".q")
            else:
                emit_proj(dram["wq1"], "wq1", 16, xn_bf, R_xn,
                          bias_sink_cb(q_sink, bq_sb, R_bq), layer + ".q")
            A.pop()

            A.push()
            mask_sb, R_mask = A.tile(16 * T, BF16)
            P.op("sync", lambda e: e.dma_start(out=mask_sb, in_=dram["pmask"]),
                reads=[dreg("pmask")], writes=[R_mask], dma=True)
            GH = 4
            kbuf = Rot(A, 2, GH * SEQ, BF16)
            vbuf = Rot(A, 2, GH * SEQ, BF16)
            ptr = Rot(A, 6, T, BF16)
            utr = Rot(A, 3, T, F32)
            bctr = Rot(A, 2, T, F32)

            def load_group(g):
                kt, R_kt = kbuf.nxt()
                vt, R_vt2 = vbuf.nxt()
                for hl in range(GH):
                    h = g * GH + hl
                    kvh = h * kvh_n // 16
                    for r in range(2):
                        P.op("sync", lambda e, hl=hl, r=r, kvh=kvh, kt=kt:
                             e.dma_start(
                                 out=kt[:, hl * SEQ + r * T: hl * SEQ + r * T + T],
                                 in_=cc[kgname][r * kv_dim + 128 * kvh:
                                                r * kv_dim + 128 * kvh + 128, :]),
                             reads=[dreg(kgname)], writes=[R_kt], dma=True)
                        P.op("sync", lambda e, hl=hl, r=r, kvh=kvh, vt=vt:
                             e.dma_start(
                                 out=vt[:, hl * SEQ + r * T: hl * SEQ + r * T + T]
                                 .rearrange("p (tc d) -> p tc d", d=128),
                                 in_=cc[vgname][r * T:(r + 1) * T,
                                                128 * kvh:128 * kvh + 128]
                                 .rearrange("(tc p) d -> p tc d", p=128)),
                             reads=[dreg(vgname)], writes=[R_vt2], dma=True)
                return kt, R_kt, vt, R_vt2

            cur = load_group(0)
            for g in range(4):
                kt, R_kt, vt, R_vt2 = cur
                nxt = load_group(g + 1) if g < 3 else None
                for hl in range(GH):
                    h = g * GH + hl
                    ptiles = {}

                    def emit_sums_O(j):
                        pt, R_pt = ptiles.pop(j)
                        for tt in range(2):
                            P.op("tensor", lambda e, pt=pt, j=j, tt=tt:
                                 e.matmul(PB[4 + tt][0:1, :], ones_col,
                                          pt[:, tt * 512:tt * 512 + 512],
                                          start=(j == 0), stop=(j == 15)),
                                 reads=[R_pt, R_onc], writes=[PBR[4 + tt]])
                            vsl = vt[:, hl * SEQ + j * 128:
                                     hl * SEQ + j * 128 + 128]
                            P.op("tensor", lambda e, vsl=vsl, pt=pt, j=j, tt=tt:
                                 e.matmul(PB[6 + tt], vsl,
                                          pt[:, tt * 512:tt * 512 + 512],
                                          start=(j == 0), stop=(j == 15)),
                                 reads=[R_vt2, R_pt], writes=[PBR[6 + tt]])

                    R_qsl = R_q.sub(2 * h * T, 2 * T)
                    for j in range(16):
                        sp = j % 2
                        ksl = kt[:, hl * SEQ + j * 128: hl * SEQ + j * 128 + 128]
                        for tt in range(2):
                            qsl = q_bf[:, h * T + tt * 512: h * T + tt * 512 + 512]
                            P.op("tensor", lambda e, ksl=ksl, qsl=qsl, sp=sp,
                                 tt=tt: e.matmul(PB[sp * 2 + tt], ksl, qsl,
                                                 start=True, stop=True),
                                 reads=[R_kt, R_qsl], writes=[PBR[sp * 2 + tt]])
                        pt, R_pt = ptr.nxt()
                        ptiles[j] = (pt, R_pt)
                        if is0:
                            ut, R_ut = utr.nxt()
                            P.op("scalar", lambda e, sp=sp, ut=ut: e.activation(
                                out=ut, in_=PBP[sp], func=AF.Tanh,
                                scale=SC_ATT / SOFTCAP),
                                reads=[PBR[sp * 2], PBR[sp * 2 + 1]],
                                writes=[R_ut])
                            P.op("scalar", lambda e, ut=ut, pt=pt: e.activation(
                                out=pt, in_=ut, func=AF.Exp, scale=SOFTCAP),
                                reads=[R_ut], writes=[R_pt])
                        else:
                            P.op("scalar", lambda e, sp=sp, pt=pt: e.activation(
                                out=pt, in_=PBP[sp], func=AF.Exp, scale=SC_ATT),
                                reads=[PBR[sp * 2], PBR[sp * 2 + 1]],
                                writes=[R_pt])
                        msl = mask_sb[:, j * T:(j + 1) * T]
                        P.op("vector", lambda e, pt=pt, msl=msl: e.tensor_mul(
                            pt, pt, msl), reads=[R_mask, R_pt], writes=[R_pt])
                        if j >= 3:
                            emit_sums_O(j - 3)
                    emit_sums_O(13)
                    emit_sums_O(14)
                    emit_sums_O(15)
                    roff = T + 0
                    inv_ap, inv_reg = rows(roff, 1024)
                    for tt in range(2):
                        P.op("vector", lambda e, tt=tt, inv_ap=inv_ap:
                             e.reciprocal(inv_ap[:, tt * 512:tt * 512 + 512],
                                          PB[4 + tt][0:1, :]),
                             reads=[PBR[4 + tt]], writes=[inv_reg])
                        P.op("tensor", lambda e, tt=tt, inv_ap=inv_ap: e.matmul(
                            PB[4 + tt], onesf_row,
                            inv_ap[:, tt * 512:tt * 512 + 512],
                            start=True, stop=True),
                            reads=[inv_reg, R_onfr], writes=[PBR[4 + tt]])
                    bct, R_bct = bctr.nxt()
                    P.op("scalar", lambda e, bct=bct: e.copy(bct, PBP[2]),
                         reads=[PBR[4], PBR[5]], writes=[R_bct])
                    osl = ao_bf[:, h * T:(h + 1) * T]
                    R_osl = R_ao.sub(2 * h * T, 2 * T)
                    P.op("vector", lambda e, osl=osl, bct=bct: e.tensor_mul(
                        osl, PBP[3], bct),
                        reads=[PBR[6], PBR[7], R_bct], writes=[R_osl])
                cur = nxt
            A.pop()

            aoo_bf, R_aoo = A.tile(NC * T, BF16)

            def o_sink(o):
                sl = aoo_bf[:, o * T:(o + 1) * T]
                return sl, R_aoo.sub(2 * o * T, 2 * T)

            if is0:
                emit_proj(dram["woT0"], "woT0", NC, ao_bf, R_ao,
                          bias_sink_cb(o_sink, None, None), layer + ".o",
                          in_ch=16)
            else:
                emit_proj(dram["wo1"], "wo1", NC, ao_bf, R_ao,
                          bias_sink_cb(o_sink, bo_sb, R_bo), layer + ".o",
                          in_ch=16)
            emit_postres(src_ap, src_name, dst_ap, dst_name, aoo_bf, R_aoo,
                         layer + ".post")
            A.pop()

        def phase_f0(first, last):
            src_name = "x" if first else "x_spill"
            src_ap = x_in if first else x_spill
            dst_name = "xout" if last else "x_spill"
            dst_ap = x_out if last else x_spill

            A.push()
            y_sb, R_y = A.tile(NC * T, F32)
            A.push()
            hp_bf, R_hp = A.tile(ICH * T, BF16)
            xn_bf, R_xn = A.tile(NC * T, BF16)
            wg_sb, R_wg = A.tile(16 * 16, F32)
            idf_t, R_idft = A.tile(128, F32)
            P.op("sync", lambda e: e.dma_start(out=wg_sb, in_=dram["WgT"]),
                reads=[dreg("WgT")], writes=[R_wg], dma=True)
            P.op("sync", lambda e: e.dma_start(out=idf_t, in_=dram["identf"]),
                 reads=[dreg("identf")], writes=[R_idft], dma=True)

            def router_cb(c, tf, R_tf):
                for tt in range(2):
                    P.op("tensor", lambda e, c=c, tf=tf, tt=tt: e.matmul(
                        PB[2 + tt][0:16, :], wg_sb[:, c * 16:(c + 1) * 16],
                        tf[:, tt * 512:tt * 512 + 512],
                        start=(c == 0), stop=(c == NC - 1)),
                        reads=[R_tf, R_wg], writes=[PBR[2 + tt]])

            emit_preln(src_ap, src_name, xn_bf, R_xn, "f0.ln",
                       f32_cb=router_cb)

            lg_sb, R_lg = A.tile(T, F32, parts=16)
            for tt in range(2):
                P.op("scalar", lambda e, tt=tt: e.copy(
                    lg_sb[:, tt * 512:tt * 512 + 512], PB[2 + tt][0:16, :]),
                    reads=[PBR[2 + tt]], writes=[R_lg])
            ltm, R_ltm = A.tile(8 * 16, F32)
            for tc in range(8):
                bank = tc % 2
                P.op("tensor", lambda e, tc=tc, bank=bank: e.transpose(
                    PB[bank][:, 0:16], lg_sb[:, tc * 128:(tc + 1) * 128],
                    idf_t[0:16, 0:16]),
                    reads=[R_lg, R_idft], writes=[PBR[bank]])
                P.op("vector", lambda e, tc=tc, bank=bank: e.tensor_copy(
                    ltm[:, tc * 16:(tc + 1) * 16], PB[bank][:, 0:16]),
                    reads=[PBR[bank]], writes=[R_ltm])

            cmb_tm, R_cmb = A.tile(8 * 16, F32)
            sc_t, R_sct = A.tile(16, F32)
            gs_t, R_gst = A.tile(4, F32)
            gt_t, R_gt = A.tile(16, F32)
            r1_t, R_r1 = A.tile(1, F32)
            r2_t, R_r2 = A.tile(1, F32)
            ex_t, R_ex = A.tile(16, F32)
            for tc in range(8):
                lt = ltm[:, tc * 16:(tc + 1) * 16]
                R_lt = R_ltm.sub(4 * tc * 16, 4 * 16)
                P.op("scalar", lambda e, lt=lt: e.activation(
                    out=ex_t, in_=lt, func=AF.Exp),
                    reads=[R_lt], writes=[R_ex])
                P.op("vector", lambda e: e.reduce_sum(r1_t, ex_t, axis=AX.X),
                     reads=[R_ex], writes=[R_r1])
                P.op("vector", lambda e: e.reciprocal(r2_t, r1_t),
                     reads=[R_r1], writes=[R_r2])
                P.op("vector", lambda e: e.tensor_scalar(
                    sc_t, ex_t, r2_t, None, ALU.mult),
                    reads=[R_ex, R_r2], writes=[R_sct])
                for gg in range(4):
                    P.op("vector", lambda e, gg=gg: e.reduce_max(
                        gs_t[:, gg:gg + 1], sc_t[:, gg * 4:(gg + 1) * 4],
                        axis=AX.X), reads=[R_sct], writes=[R_gst])
                for gg in range(4):
                    P.op("vector", lambda e, gg=gg: e.tensor_scalar(
                        gt_t[:, 0:4], gs_t, gs_t[:, gg:gg + 1], None,
                        ALU.is_gt), reads=[R_gst], writes=[R_gt])
                    P.op("vector", lambda e: e.reduce_sum(
                        r1_t, gt_t[:, 0:4], axis=AX.X),
                        reads=[R_gt], writes=[R_r1])
                    P.op("vector", lambda e: e.tensor_scalar(
                        r1_t, r1_t, 1.5, None, ALU.is_le),
                        reads=[R_r1], writes=[R_r1])
                    P.op("vector", lambda e: e.tensor_scalar(
                        r2_t, r1_t, 1e9, -1e9, ALU.mult, ALU.add),
                        reads=[R_r1], writes=[R_r2])
                    P.op("vector", lambda e, gg=gg: e.tensor_scalar(
                        ex_t[:, gg * 4:(gg + 1) * 4],
                        sc_t[:, gg * 4:(gg + 1) * 4], r1_t, r2_t,
                        ALU.mult, ALU.add),
                        reads=[R_sct, R_r1, R_r2], writes=[R_ex])
                for ee in range(16):
                    P.op("vector", lambda e, ee=ee: e.tensor_scalar(
                        gt_t, ex_t, ex_t[:, ee:ee + 1], None, ALU.is_gt),
                        reads=[R_ex], writes=[R_gt])
                    P.op("vector", lambda e: e.reduce_sum(
                        r1_t, gt_t, axis=AX.X), reads=[R_gt], writes=[R_r1])
                    P.op("vector", lambda e: e.tensor_scalar(
                        r1_t, r1_t, 3.5, None, ALU.is_le),
                        reads=[R_r1], writes=[R_r1])
                    dsl = cmb_tm[:, tc * 16 + ee: tc * 16 + ee + 1]
                    R_dsl = R_cmb.sub(4 * (tc * 16 + ee), 4)
                    P.op("vector", lambda e, dsl=dsl, ee=ee: e.tensor_mul(
                        dsl, sc_t[:, ee:ee + 1], r1_t),
                        reads=[R_sct, R_r1], writes=[R_dsl])
            cmb_fm, R_cfm = A.tile(T, F32, parts=16)
            for tc in range(8):
                bank = tc % 2
                P.op("tensor", lambda e, tc=tc, bank=bank: e.transpose(
                    PB[bank][0:16, 0:128], cmb_tm[:, tc * 16:(tc + 1) * 16],
                    idf_t),
                    reads=[R_cmb, R_idft], writes=[PBR[bank]])
                P.op("vector", lambda e, tc=tc, bank=bank: e.tensor_copy(
                    cmb_fm[:, tc * 128:(tc + 1) * 128], PB[bank][0:16, 0:128]),
                    reads=[PBR[bank]], writes=[R_cfm])
            esel_sb, R_esel = A.tile(16 * 128, F32, parts=16)
            P.op("sync", lambda e: e.dma_start(out=esel_sb, in_=dram["esel"]),
                 reads=[dreg("esel")], writes=[R_esel], dma=True)

            w13 = Rot(A, 4, NC * 128, BF16)
            w2r = Rot(A, 2, ICH * 128, BF16)
            bce, R_bce = A.tile(T, F32)
            sar = Rot(A, 2, 512, F32)
            tbr = Rot(A, 2, 512, F32)

            for ex in range(NEX):
                routed = ex < 16
                if routed:
                    for tt in range(2):
                        P.op("tensor", lambda e, ex=ex, tt=tt: e.matmul(
                            PB[6 + tt], esel_sb[:, ex * 128:(ex + 1) * 128],
                            cmb_fm[:, tt * 512:tt * 512 + 512],
                            start=True, stop=True),
                            reads=[R_cfm, R_esel], writes=[PBR[6 + tt]])
                        P.op("scalar", lambda e, tt=tt: e.copy(
                            bce[:, tt * 512:tt * 512 + 512], PB[6 + tt]),
                            reads=[PBR[6 + tt]], writes=[R_bce])
                for ic in range(ICH):
                    bs = 0 if ic % 2 == 0 else 4
                    w1, R_w1 = w13.nxt()
                    w3, R_w3 = w13.nxt()
                    P.op("gpsimd", lambda e, ex=ex, ic=ic, w1=w1: e.dma_start(
                        out=w1, in_=dram["W1T"][ex, ic]),
                        reads=[dreg("W1T")], writes=[R_w1], dma=True)
                    P.op("gpsimd", lambda e, ex=ex, ic=ic, w3=w3: e.dma_start(
                        out=w3, in_=dram["W3T"][ex, ic]),
                        reads=[dreg("W3T")], writes=[R_w3], dma=True)
                    for tt in range(2):
                        for mi, (w, R_w) in enumerate(((w1, R_w1), (w3, R_w3))):
                            bank = bs + mi * 2 + tt

                            def mm(e, w=w, tt=tt, bank=bank):
                                ins = None
                                for c in range(NC):
                                    ins = e.matmul(
                                        PB[bank], w[:, c * 128:(c + 1) * 128],
                                        xn_bf[:, c * T + tt * 512:
                                              c * T + tt * 512 + 512],
                                        start=(c == 0), stop=(c == NC - 1))
                                return ins
                            P.op("tensor", mm, reads=[R_w, R_xn],
                                 writes=[PBR[bank]],
                                 tag=f"f0.e{ex}i{ic}t{tt}m{mi}")
                        sa, R_sa = sar.nxt()
                        P.op("scalar", lambda e, sa=sa, bs=bs, tt=tt:
                             e.activation(out=sa, in_=PB[bs + tt],
                                          func=AF.Silu),
                             reads=[PBR[bs + tt]], writes=[R_sa])
                        hsl = hp_bf[:, ic * T + tt * 512: ic * T + tt * 512 + 512]
                        R_hsl = R_hp.sub(2 * (ic * T + tt * 512), 2 * 512)
                        if routed:
                            tb, R_tb = tbr.nxt()
                            P.op("vector", lambda e, tb=tb, bs=bs, tt=tt:
                                 e.tensor_mul(tb, PB[bs + 2 + tt],
                                              bce[:, tt * 512:tt * 512 + 512]),
                                 reads=[PBR[bs + 2 + tt], R_bce],
                                 writes=[R_tb])
                            P.op("vector", lambda e, hsl=hsl, tb=tb, sa=sa:
                                 e.tensor_mul(hsl, tb, sa),
                                 reads=[R_tb, R_sa], writes=[R_hsl])
                        else:
                            P.op("vector", lambda e, hsl=hsl, bs=bs, tt=tt,
                                 sa=sa: e.tensor_mul(hsl, PB[bs + 2 + tt], sa),
                                 reads=[PBR[bs + 2 + tt], R_sa],
                                 writes=[R_hsl])
                for o in range(NC):
                    w2, R_w2 = w2r.nxt()
                    P.op("gpsimd", lambda e, ex=ex, o=o, w2=w2: e.dma_start(
                        out=w2, in_=dram["W2T"][ex, o]),
                        reads=[dreg("W2T")], writes=[R_w2], dma=True)
                    for tt in range(2):
                        bank = (o * 2 + tt) % 4

                        def mm2(e, w2=w2, tt=tt, bank=bank):
                            ins = None
                            for ci in range(ICH):
                                ins = e.matmul(
                                    PB[bank], w2[:, ci * 128:(ci + 1) * 128],
                                    hp_bf[:, ci * T + tt * 512:
                                          ci * T + tt * 512 + 512],
                                    start=(ci == 0), stop=(ci == ICH - 1))
                            return ins
                        P.op("tensor", mm2, reads=[R_w2, R_hp],
                             writes=[PBR[bank]], tag=f"f0.w2e{ex}o{o}t{tt}")
                        ysl = y_sb[:, o * T + tt * 512: o * T + tt * 512 + 512]
                        R_ysl = R_y.sub(4 * (o * T + tt * 512), 4 * 512)
                        if ex == 0:
                            P.op("vector", lambda e, ysl=ysl, bank=bank:
                                 e.tensor_copy(ysl, PB[bank]),
                                 reads=[PBR[bank]], writes=[R_ysl])
                        else:
                            P.op("vector", lambda e, ysl=ysl, bank=bank:
                                 e.tensor_add(ysl, ysl, PB[bank]),
                                 reads=[PBR[bank], R_ysl], writes=[R_ysl])
            A.pop()

            ybr = Rot(A, 2, T, BF16)
            ycast = {}

            def ychunk(c):
                yb, R_yb = ybr.nxt()
                P.op("vector", lambda e, c=c, yb=yb: e.tensor_copy(
                    yb, y_sb[:, c * T:(c + 1) * T]),
                    reads=[R_y.sub(4 * c * T, 4 * T)], writes=[R_yb])
                return yb, [R_yb]

            bc_r, R_bcr, bc_m, R_bcm = ln_stats_bc(ychunk, NC, D, EPS_LN,
                                                   "f0.post")
            xcr = Rot(A, 3, T, F32)
            xnr = Rot(A, 3, T, F32)
            tmp = Rot(A, 2, T, F32)
            for c in range(NC):
                xc, R_xc = xcr.nxt()
                P.op("sync", lambda e, c=c, xc=xc: e.dma_start(
                    out=xc, in_=src_ap[c * 128:(c + 1) * 128, :]),
                    reads=[dreg(src_name)], writes=[R_xc], dma=True)
                tf, R_tf = tmp.nxt()
                P.op("vector", lambda e, c=c, tf=tf: e.tensor_mul(
                    tf, y_sb[:, c * T:(c + 1) * T], bc_r),
                    reads=[R_y.sub(4 * c * T, 4 * T), R_bcr], writes=[R_tf])
                P.op("vector", lambda e, tf=tf: e.tensor_sub(tf, tf, bc_m),
                     reads=[R_tf, R_bcm], writes=[R_tf])
                xo, R_xo = xnr.nxt()
                P.op("vector", lambda e, xo=xo, xc=xc, tf=tf: e.tensor_add(
                    xo, xc, tf), reads=[R_xc, R_tf], writes=[R_xo])
                P.op("sync", lambda e, c=c, xo=xo: e.dma_start(
                    out=dst_ap[c * 128:(c + 1) * 128, :], in_=xo),
                    reads=[R_xo],
                    writes=[Region("dram:" + dst_name, c * 128 * T * 4,
                                   (c + 1) * 128 * T * 4)], dma=True)
            A.pop()

        def phase_f1(first, last):
            src_name = "x" if first else "x_spill"
            src_ap = x_in if first else x_spill
            dst_name = "xout" if last else "x_spill"
            dst_ap = x_out if last else x_spill

            A.push()
            d_bf, R_d = A.tile(NC * T, BF16)
            sg_sb, R_sg = A.tile(L1HCH, F32)
            su_sb, R_su = A.tile(L1HCH, F32)
            sd_sb, R_sd = A.tile(16, F32)
            for nm, ap_, rg in (("sg1", sg_sb, R_sg), ("su1", su_sb, R_su),
                                ("sd1", sd_sb, R_sd)):
                P.op("sync", lambda e, nm=nm, ap_=ap_: e.dma_start(
                    out=ap_, in_=dram[nm]),
                    reads=[dreg(nm)], writes=[rg], dma=True)
            A.push()
            hp_bf, R_hp = A.tile(L1HCH * T, BF16)
            A.push()
            xn_bf, R_xn = A.tile(NC * T, BF16)
            emit_preln(src_ap, src_name, xn_bf, R_xn, "f1.ln")
            debug_tap("f1.xn", xn_bf, R_xn, NC, BF16)

            bcx, R_bcx = A.tile(T, F32)
            for tt in range(2):
                def fcs(e, tt=tt):
                    ins = None
                    for c in range(NC):
                        ins = e.matmul(
                            PB[6 + tt][0:1, :], ones_col,
                            xn_bf[:, c * T + tt * 512: c * T + tt * 512 + 512],
                            start=(c == 0), stop=(c == NC - 1))
                    return ins
                P.op("tensor", fcs, reads=[R_xn, R_onc], writes=[PBR[6 + tt]])
                csr, R_csr = rows(tt * 512, 512)
                P.op("scalar", lambda e, csr=csr, tt=tt: e.copy(
                    csr, PB[6 + tt][0:1, :]),
                    reads=[PBR[6 + tt]], writes=[R_csr])
                P.op("tensor", lambda e, csr=csr, tt=tt: e.matmul(
                    PB[6 + tt], c128f_row, csr, start=True, stop=True),
                    reads=[R_csr, R_c128], writes=[PBR[6 + tt]])
                P.op("scalar", lambda e, tt=tt: e.copy(
                    bcx[:, tt * 512:tt * 512 + 512], PB[6 + tt]),
                    reads=[PBR[6 + tt]], writes=[R_bcx])

            wgr = Rot(A, 2, NC * 128, BF16)
            wur = Rot(A, 2, NC * 128, BF16)
            tgr = Rot(A, 2, 512, F32)
            tur = Rot(A, 2, 512, F32)
            for o in range(L1HCH):
                wg_, R_wg_ = wgr.nxt()
                wu_, R_wu_ = wur.nxt()
                P.op("gpsimd", lambda e, o=o, wg_=wg_: e.dma_start(
                    out=wg_, in_=dram["wgT1"][o]),
                    reads=[dreg("wgT1")], writes=[R_wg_], dma=True)
                P.op("gpsimd", lambda e, o=o, wu_=wu_: e.dma_start(
                    out=wu_, in_=dram["wuT1"][o]),
                    reads=[dreg("wuT1")], writes=[R_wu_], dma=True)
                for tt in range(2):
                    bs = (o % 2) * 4
                    for mi, (w, R_w) in enumerate(((wg_, R_wg_), (wu_, R_wu_))):
                        bank = bs + mi * 2 + tt

                        def mm(e, w=w, tt=tt, bank=bank):
                            ins = None
                            for c in range(NC):
                                ins = e.matmul(
                                    PB[bank], w[:, c * 128:(c + 1) * 128],
                                    xn_bf[:, c * T + tt * 512:
                                          c * T + tt * 512 + 512],
                                    start=(c == 0), stop=(c == NC - 1))
                            return ins
                        P.op("tensor", mm, reads=[R_w, R_xn],
                             writes=[PBR[bank]], tag=f"f1.gu{o}t{tt}m{mi}")
                    tg, R_tg = tgr.nxt()
                    tu, R_tu = tur.nxt()
                    P.op("vector", lambda e, tg=tg, bs=bs, tt=tt: e.tensor_sub(
                        tg, PB[bs + tt], bcx[:, tt * 512:tt * 512 + 512]),
                        reads=[PBR[bs + tt], R_bcx], writes=[R_tg])
                    P.op("scalar", lambda e, tg=tg, o=o: e.activation(
                        out=tg, in_=tg, func=AF.Silu, scale=sg_sb[:, o:o + 1]),
                        reads=[R_tg, R_sg], writes=[R_tg])
                    P.op("vector", lambda e, tu=tu, bs=bs, tt=tt: e.tensor_sub(
                        tu, PB[bs + 2 + tt], bcx[:, tt * 512:tt * 512 + 512]),
                        reads=[PBR[bs + 2 + tt], R_bcx], writes=[R_tu])
                    P.op("vector", lambda e, tu=tu, o=o: e.tensor_scalar(
                        tu, tu, su_sb[:, o:o + 1], None, ALU.mult),
                        reads=[R_tu, R_su], writes=[R_tu])
                    hsl = hp_bf[:, o * T + tt * 512: o * T + tt * 512 + 512]
                    R_hsl = R_hp.sub(2 * (o * T + tt * 512), 2 * 512)
                    P.op("vector", lambda e, hsl=hsl, tg=tg, tu=tu:
                         e.tensor_mul(hsl, tg, tu),
                         reads=[R_tg, R_tu], writes=[R_hsl])
                    if DEBUG_TAP in (f"f1.tg{o}_{tt}", f"f1.tu{o}_{tt}"):
                        dsrc, drg = (tg, R_tg) if "tg" in DEBUG_TAP else (tu, R_tu)
                        dbg2 = nc.dram_tensor("dbg", [128, 512], F32,
                                              kind="ExternalOutput").ap()
                        tok = P.op("sync", lambda e, dsrc=dsrc: e.dma_start(
                            out=dbg2, in_=dsrc), reads=[drg],
                            writes=[Region("dram:dbg")], dma=True)
                        P.final_tokens.append(tok)
                    if DEBUG_TAP == f"f1.hp{o}_{tt}":
                        dbg3 = nc.dram_tensor("dbg", [128, 512], BF16,
                                              kind="ExternalOutput").ap()
                        tok = P.op("sync", lambda e, hsl=hsl: e.dma_start(
                            out=dbg3, in_=hsl), reads=[R_hsl],
                            writes=[Region("dram:dbg")], dma=True)
                        P.final_tokens.append(tok)
            debug_tap("f1.bcx", bcx, R_bcx, 1, F32)
            A.pop()
            debug_tap("f1.hp", hp_bf, R_hp, L1HCH, BF16)
            if DEBUG_STOP == "gate":
                return
            if DEBUG_TAP == "f1.hptail":
                dbg4 = nc.dram_tensor("dbg", [4 * 128, T], BF16,
                                      kind="ExternalOutput").ap()
                tok = P.op("sync", lambda e: e.dma_start(
                    out=dbg4.rearrange("(c p) t -> p c t", p=128),
                    in_=hp_bf[:, 40 * T:44 * T].rearrange(
                        "p (c t) -> p c t", c=4)),
                    reads=[R_hp], writes=[Region("dram:dbg")], dma=True)
                P.final_tokens.append(tok)

            bch, R_bch = A.tile(T, F32)
            for tt in ([] if DEBUG_STOP in ("down_nobch",) else range(2)):
                def fcs2(e, tt=tt):
                    ins = None
                    for c in range(L1HCH):
                        ins = e.matmul(
                            PB[6 + tt][0:1, :], ones_col,
                            hp_bf[:, c * T + tt * 512: c * T + tt * 512 + 512],
                            start=(c == 0), stop=(c == L1HCH - 1))
                    return ins
                P.op("tensor", fcs2, reads=[R_hp, R_onc], writes=[PBR[6 + tt]])
                csr, R_csr = rows(tt * 512, 512)
                P.op("scalar", lambda e, csr=csr, tt=tt: e.copy(
                    csr, PB[6 + tt][0:1, :]),
                    reads=[PBR[6 + tt]], writes=[R_csr])
                P.op("tensor", lambda e, csr=csr, tt=tt: e.matmul(
                    PB[6 + tt], c128f_row, csr, start=True, stop=True),
                    reads=[R_csr, R_c128], writes=[PBR[6 + tt]])
                P.op("scalar", lambda e, tt=tt: e.copy(
                    bch[:, tt * 512:tt * 512 + 512], PB[6 + tt]),
                    reads=[PBR[6 + tt]], writes=[R_bch])

            if DEBUG_STOP == "bch":
                return
            debug_tap("f1.bch", bch, R_bch, 1, F32)
            if DEBUG_STOP == "down_nobch":
                wdr0 = Rot(A, 2, L1HCH * 128, BF16)
                for o in range(16):
                    wd_, R_wd_ = wdr0.nxt()
                    P.op("gpsimd", lambda e, o=o, wd_=wd_: e.dma_start(
                        out=wd_, in_=dram["wdT1"][o]),
                        reads=[dreg("wdT1")], writes=[R_wd_], dma=True)
                return
            if DEBUG_STOP == "down_fresh":
                A.off = 183 * 1024
                wdrf = Rot(A, 2, L1HCH * 128, BF16)
                for o in range(1):
                    wd_, R_wd_ = wdrf.nxt()
                    P.op("gpsimd", lambda e, o=o, wd_=wd_: e.dma_start(
                        out=wd_, in_=dram["wdT1"][o]),
                        reads=[dreg("wdT1")], writes=[R_wd_], dma=True)
                return
            wdr = Rot(A, 2, L1HCH * 128, BF16)
            if DEBUG_STOP in ("down_dma1", "down_dmaonly2"):
                nloads = 1 if DEBUG_STOP == "down_dma1" else 16
                for o in range(nloads):
                    wd_, R_wd_ = wdr.nxt()
                    P.op("gpsimd", lambda e, o=o, wd_=wd_: e.dma_start(
                        out=wd_, in_=dram["wdT1"][o]),
                        reads=[dreg("wdT1")], writes=[R_wd_], dma=True)
                return
            tdr = Rot(A, 2, 512, F32)
            for o in range(NC):
                wd_, R_wd_ = wdr.nxt()
                for q4 in range(4):
                    qs = L1HID // 4
                    P.op("gpsimd", lambda e, o=o, wd_=wd_, q4=q4, qs=qs:
                         e.dma_start(out=wd_[:, q4 * qs:(q4 + 1) * qs],
                                     in_=dram["wdT1"][o][:, q4 * qs:(q4 + 1) * qs]),
                         reads=[dreg("wdT1")], writes=[R_wd_], dma=True)
                for tt in range(2):
                    if DEBUG_STOP == "down_dmaonly":
                        continue
                    bank = (o * 2 + tt) % 4

                    def mmd(e, wd_=wd_, tt=tt, bank=bank):
                        ins = None
                        for c in range(L1HCH):
                            ins = e.matmul(
                                PB[bank], wd_[:, c * 128:(c + 1) * 128],
                                hp_bf[:, c * T + tt * 512:
                                      c * T + tt * 512 + 512],
                                start=(c == 0), stop=(c == L1HCH - 1))
                        return ins
                    P.op("tensor", mmd, reads=[R_wd_, R_hp],
                         writes=[PBR[bank]], tag=f"f1.d{o}t{tt}")
                    td, R_td = tdr.nxt()
                    P.op("vector", lambda e, td=td, bank=bank, tt=tt:
                         e.tensor_sub(td, PB[bank],
                                      bch[:, tt * 512:tt * 512 + 512]),
                         reads=[PBR[bank], R_bch], writes=[R_td])
                    dsl = d_bf[:, o * T + tt * 512: o * T + tt * 512 + 512]
                    R_dsl = R_d.sub(2 * (o * T + tt * 512), 2 * 512)
                    P.op("vector", lambda e, dsl=dsl, td=td, o=o:
                         e.tensor_scalar(dsl, td, sd_sb[:, o:o + 1], None,
                                         ALU.mult),
                         reads=[R_td, R_sd], writes=[R_dsl])
            A.pop()
            debug_tap("f1.d", d_bf, R_d, NC, BF16)
            if DEBUG_STOP == "down":
                return

            emit_postres(src_ap, src_name, dst_ap, dst_name, d_bf, R_d,
                         "f1.post")
            A.pop()

        for pi, ph in enumerate(phases):
            first = (pi == 0)
            last = (pi == len(phases) - 1)
            if ph in ("a0", "a1"):
                phase_attn(ph, first, last)
            elif ph == "f0":
                phase_f0(first, last)
            elif ph == "f1":
                phase_f1(first, last)

        for reg, tok in P.writers:
            if reg.space == "dram:xout":
                P.final_tokens.append(tok)

        P.emit(nc)
    return nc, sorted(dram.keys()), A.peak, P.n_ops, P


def _to_native(WT, in_ch, n_out):
    W = np.asarray(WT)
    return np.ascontiguousarray(
        W.reshape(in_ch, 128, n_out, 128).transpose(2, 1, 0, 3)
        .reshape(n_out, 128, in_ch * 128))


def _prep_host(inp):
    H = {}
    f32 = np.float32

    def ct(a):
        return np.ascontiguousarray(np.asarray(a, dtype=f32))

    H["wqT0"] = ct(np.asarray(inp["L0_attn_wq"], f32).T)
    H["wkT0"] = ct(np.asarray(inp["L0_attn_wk"], f32).T)
    H["wvT0"] = ct(np.asarray(inp["L0_attn_wv"], f32).T)
    H["woT0"] = ct(np.asarray(inp["L0_attn_wo"], f32).T)
    H["qnw"] = ct(inp["L0_attn_q_norm_w"]).reshape(1, 128)
    H["knw"] = ct(inp["L0_attn_k_norm_w"]).reshape(1, 128)
    cos16 = ct(inp["L0_attn_rope_cos"])[0:16, :].T
    sin16 = ct(inp["L0_attn_rope_sin"])[0:16, :].T
    H["ropeHC"] = np.ascontiguousarray(np.concatenate([cos16, cos16], 0))
    H["ropeHS"] = np.ascontiguousarray(np.concatenate([sin16, sin16], 0))
    perm = np.concatenate([np.arange(0, 128, 2), np.arange(1, 128, 2)])
    full_perm = np.concatenate([h * 128 + perm for h in range(16)])
    H["wqT0"] = _to_native(H["wqT0"][:, full_perm], NC, 16)
    H["wkT0"] = _to_native(H["wkT0"][:, full_perm], NC, 16)
    H["wvT0"] = _to_native(H["wvT0"], NC, 16)
    H["woT0"] = _to_native(H["woT0"], 16, NC)
    H["qnw"] = np.ascontiguousarray(H["qnw"][:, perm])
    H["knw"] = np.ascontiguousarray(H["knw"][:, perm])

    wgt = ct(np.asarray(inp["L0_ffn_Wg"], f32).T)
    H["WgT"] = np.ascontiguousarray(
        wgt.reshape(NC, 128, 16).transpose(1, 0, 2).reshape(128, NC * 16))
    W1 = np.asarray(inp["L0_ffn_W1"], f32)
    W3 = np.asarray(inp["L0_ffn_W3"], f32)
    W2 = np.asarray(inp["L0_ffn_W2"], f32)
    W1T = np.empty((NEX, D, I_MOE), f32)
    W3T = np.empty((NEX, D, I_MOE), f32)
    W2T = np.empty((NEX, I_MOE, D), f32)
    for e in range(16):
        W1T[e] = W1[e].T
        W3T[e] = W3[e].T
        W2T[e] = W2[e].T
    Ws1T = np.asarray(inp["L0_ffn_Ws1"], f32).T
    Ws3T = np.asarray(inp["L0_ffn_Ws3"], f32).T
    Ws2T = np.asarray(inp["L0_ffn_Ws2"], f32).T
    W1T[16] = Ws1T[:, 0:I_MOE]
    W1T[17] = Ws1T[:, I_MOE:]
    W3T[16] = Ws3T[:, 0:I_MOE]
    W3T[17] = Ws3T[:, I_MOE:]
    W2T[16] = Ws2T[0:I_MOE, :]
    W2T[17] = Ws2T[I_MOE:, :]
    W1Tn = np.empty((NEX, ICH, 128, D), f32)
    W3Tn = np.empty((NEX, ICH, 128, D), f32)
    W2Tn = np.empty((NEX, 16, 128, ICH * 128), f32)
    for e in range(NEX):
        W1Tn[e] = _to_native(W1T[e], NC, ICH)
        W3Tn[e] = _to_native(W3T[e], NC, ICH)
        W2Tn[e] = _to_native(W2T[e], ICH, NC)
    H["W1T"], H["W3T"], H["W2T"] = W1Tn, W3Tn, W2Tn
    H["identf"] = np.eye(128, dtype=f32)
    H["identb"] = np.eye(128, dtype=BF)
    esel = np.zeros((16, 16 * 128), f32)
    for e in range(16):
        esel[e, e * 128:(e + 1) * 128] = 1.0
    H["esel"] = esel

    H["wq1"] = _to_native(ct(inp["L1_attn_wq"]), NC, 16)
    H["wk1"] = _to_native(ct(inp["L1_attn_wk"]), NC, 4)
    H["wv1"] = _to_native(ct(inp["L1_attn_wv"]), NC, 4)
    H["wo1"] = _to_native(ct(inp["L1_attn_wo"]), 16, NC)
    H["bq1"] = ct(inp["L1_attn_bq"]).reshape(16, 128).T.copy()
    H["bk1"] = ct(inp["L1_attn_bk"]).reshape(4, 128).T.copy()
    H["bv1"] = ct(inp["L1_attn_bv"]).reshape(4, 128).T.copy()
    H["bo1"] = ct(inp["L1_attn_bo"]).reshape(16, 128).T.copy()

    H["wgT1"] = _to_native(np.asarray(inp["L1_ffn_wq_gate"], np.int32).T,
                           NC, L1HCH)
    H["wuT1"] = _to_native(np.asarray(inp["L1_ffn_wq_up"], np.int32).T,
                           NC, L1HCH)
    H["wdT1"] = _to_native(np.asarray(inp["L1_ffn_wq_down"], np.int32).T,
                           L1HCH, NC)
    H["sg1"] = ct(inp["L1_ffn_s_gate"]).reshape(L1HCH, 128).T.copy()
    H["su1"] = ct(inp["L1_ffn_s_up"]).reshape(L1HCH, 128).T.copy()
    H["sd1"] = ct(inp["L1_ffn_s_down"]).reshape(16, 128).T.copy()
    return H


def _prep_core(inp, H, c):
    b, hf = c // 2, c % 2
    lo = hf * T
    per = {}
    x = np.asarray(inp["L0_attn_x"], np.float32)
    per["x"] = np.ascontiguousarray(x[b].T[:, lo:lo + T])
    tk = np.arange(SEQ).reshape(16, 128)
    tq = lo + np.arange(T)
    m = (tk[:, :, None] <= tq[None, None, :]).astype(BF)
    per["pmask"] = np.ascontiguousarray(m.transpose(1, 0, 2).reshape(128, 16 * T))
    return per


_CACHE = {}


def kernel(**inputs):
    H = _prep_host(inputs)
    per_core = [_prep_core(inputs, H, c) for c in range(8)]
    x_cur = [per_core[c]["x"] for c in range(8)]

    for phases in LAUNCH_PLAN:
        if phases not in _CACHE:
            _CACHE[phases] = build_program(phases)
        nc, names, peak, nops = _CACHE[phases][:4]
        in_maps = []
        for c in range(8):
            m = {}
            for n in names:
                if n == "x":
                    m[n] = x_cur[c]
                elif n in per_core[c]:
                    m[n] = per_core[c][n]
                else:
                    m[n] = H[n]
            in_maps.append(m)
        res = run_bass_kernel_spmd(nc, in_maps, core_ids=list(range(8)))
        x_cur = [np.asarray(res.results[c]["xout"]) for c in range(8)]

    out = np.empty((NB, SEQ, D), np.float32)
    for c in range(8):
        b, hf = c // 2, c % 2
        out[b, hf * T:(hf + 1) * T, :] = x_cur[c].T
    return out


# version 05
# speedup vs baseline: 1.0136x; 1.0136x over previous
import numpy as np
import ml_dtypes
from contextlib import ExitStack

import concourse.bass as bass
import concourse.mybir as mybir
from concourse.bass_utils import run_bass_kernel_spmd

F32 = mybir.dt.float32
BF16 = mybir.dt.bfloat16
I32 = mybir.dt.int32
AF = mybir.ActivationFunctionType
ALU = mybir.AluOpType
AX = mybir.AxisListType
BF = ml_dtypes.bfloat16

D = 2048
NC = 16
T = 1024
SEQ = 2048
DH = 128
NEX = 18
ICH = 11
I_MOE = 1408
L1HCH = 44
L1HID = 5632
NB = 4
EPS_LN = 1e-5
EPS_RMS = 1e-6
SOFTCAP = 50.0
SC_ATT = DH ** -0.5

ENGINES = ("sync", "tensor", "vector", "scalar", "gpsimd")
N_DMA_SEM = 12
GROUPS = [[0, 1], [2, 3], [4, 5], [6, 7]]
LAUNCH_PLAN = [("a0", "f0", "a1", "f1")]
DEBUG_TAP = None
DEBUG_STOP = None


class Region:
    __slots__ = ("space", "lo", "hi")

    def __init__(self, space, lo=0, hi=1 << 62):
        self.space, self.lo, self.hi = space, lo, hi

    def sub(self, lo_b, nb):
        return Region(self.space, self.lo + lo_b, self.lo + lo_b + nb)

    def overlaps(self, o):
        return self.space == o.space and self.lo < o.hi and o.lo < self.hi


class Prog:
    def __init__(self):
        self.ops = {e: [] for e in ENGINES}
        self.counts = {}
        self.writers = []
        self.readers = []
        self.dma_rr = {}
        self.sem_names = set()
        self.final_tokens = []
        self.oplog = []
        self.n_ops = 0

    def _sem(self, name):
        self.sem_names.add(name)
        return name

    def op(self, eng, fn, reads=(), writes=(), dma=False, coll=False, tag=""):
        waits = {}

        def need(tok):
            s, v = tok
            if waits.get(s, 0) < v:
                waits[s] = v

        for r in reads:
            for reg, tok in self.writers:
                if r.overlaps(reg):
                    need(tok)
        for w in writes:
            for reg, tok in self.writers:
                if w.overlaps(reg):
                    need(tok)
            for reg, tok in self.readers:
                if w.overlaps(reg):
                    need(tok)
        if dma:
            i = self.dma_rr.get(eng, 0)
            self.dma_rr[eng] = i + 1
            sem = self._sem(f"dq_{eng}_{i % N_DMA_SEM}")
            inc = 16
        elif coll:
            sem = self._sem("cc")
            inc = 1
        else:
            sem = self._sem(f"tk_{eng}")
            inc = 1
        val = self.counts.get(sem, 0) + inc
        self.counts[sem] = val
        tok = (sem, val)
        self.ops[eng].append((fn, waits, sem, inc, tag))
        self.oplog.append((eng, tag, [(w.space, w.lo, w.hi) for w in writes],
                           [(r.space, r.lo, r.hi) for r in reads]))
        self.n_ops += 1
        def contained(g, w):
            return g.space == w.space and g.lo >= w.lo and g.hi <= w.hi

        for w in writes:
            self.writers = [(g, t) for g, t in self.writers
                            if not contained(g, w)]
            self.readers = [(g, t) for g, t in self.readers
                            if not contained(g, w)]
            self.writers.append((w, tok))
        for r in reads:
            kept = []
            for g, t in self.readers:
                if (g.space == r.space and g.lo == r.lo and g.hi == r.hi
                        and t[0] == tok[0]):
                    continue
                kept.append((g, t))
            kept.append((r, tok))
            self.readers = kept
        return tok

    def emit(self, nc):
        with ExitStack() as st:
            sems = {n: st.enter_context(nc.semaphore(n))
                    for n in sorted(self.sem_names)}
            with nc.Block() as block:
                for ename in ENGINES:
                    ops = self.ops[ename]
                    fin = self.final_tokens if ename == "sync" else []
                    if not ops and not fin:
                        continue

                    def mkbody(ops=ops, fin=fin):
                        def body(eng):
                            water = {}
                            for fn, waits, sem, inc, tag in ops:
                                for s in sorted(waits):
                                    v = waits[s]
                                    if water.get(s, 0) < v:
                                        eng.wait_ge(sems[s], v)
                                        water[s] = v
                                ins = fn(eng)
                                ins.then_inc(sems[sem], inc)
                            for s, v in fin:
                                if water.get(s, 0) < v:
                                    eng.wait_ge(sems[s], v)
                                    water[s] = v
                        return body

                    getattr(block, ename)(mkbody())


class Arena:
    def __init__(self, ap_u8, nbytes):
        self.ap = ap_u8
        self.nbytes = nbytes
        self.off = 0
        self.marks = []
        self.peak = 0

    def tile(self, cols, dtype, parts=128):
        esz = mybir.dt.size(dtype)
        nb = cols * esz
        nb_al = (nb + 31) & ~31
        assert self.off + nb_al <= self.nbytes, \
            f"SBUF arena overflow: {self.off}+{nb_al} > {self.nbytes}"
        lo = self.off
        self.off += nb_al
        self.peak = max(self.peak, self.off)
        ap = self.ap[:, lo:lo + nb].bitcast(dtype)
        if parts != 128:
            ap = ap[0:parts, :]
        return ap, Region("sbuf", lo, lo + nb)

    def push(self):
        self.marks.append(self.off)

    def pop(self):
        self.off = self.marks.pop()


class Rot:
    def __init__(self, A, n, cols, dtype, parts=128):
        self.items = [A.tile(cols, dtype, parts) for _ in range(n)]
        self.i = 0

    def nxt(self):
        it = self.items[self.i % len(self.items)]
        self.i += 1
        return it


def build_program(phases):
    nc = bass.Bass("TRN2", target_bir_lowering=False)
    P = Prog()
    dram = {}

    def din(name, shape, dtype):
        dram[name] = nc.dram_tensor(name, list(shape), dtype,
                                    kind="ExternalInput").ap()
        return dram[name]

    def dreg(name):
        return Region("dram:" + name)

    x_in = din("x", (D, T), F32)
    x_out = nc.dram_tensor("xout", [D, T], F32, kind="ExternalOutput").ap()
    x_spill = nc.dram_tensor("x_spill", [D, T], F32).ap()
    dbg_state = {}

    def debug_tap(tag, ap, R, n_ch, dtype):
        if DEBUG_TAP != tag or "done" in dbg_state:
            return
        dbg_state["done"] = True
        dbg = nc.dram_tensor("dbg", [n_ch * 128, T], dtype,
                             kind="ExternalOutput").ap()
        tok = P.op("sync", lambda e: e.dma_start(
            out=dbg.rearrange("(c p) t -> p c t", p=128),
            in_=ap.rearrange("p (c t) -> p c t", c=n_ch)),
            reads=[R], writes=[Region("dram:dbg")], dma=True)
        P.final_tokens.append(tok)

    need = set(phases)
    if need & {"a0", "a1"}:
        din("pmask", (128, 16 * T), BF16)
        din("identb", (128, 128), BF16)
    if "a0" in need:
        din("wqT0", (16, 128, D), F32)
        din("wkT0", (16, 128, D), F32)
        din("wvT0", (16, 128, D), F32)
        din("woT0", (16, 128, D), F32)
        din("qnw", (1, 128), F32)
        din("knw", (1, 128), F32)
        din("ropeHC", (128, 16), F32)
        din("ropeHS", (128, 16), F32)
    if "f0" in need:
        din("WgT", (128, 256), F32)
        din("identf", (128, 128), F32)
        din("esel", (16, 16 * 128), F32)
        din("W1T", (NEX, ICH, 128, D), F32)
        din("W3T", (NEX, ICH, 128, D), F32)
        din("W2T", (NEX, 16, 128, ICH * 128), F32)
    if "a1" in need:
        din("wq1", (16, 128, D), F32)
        din("wk1", (4, 128, D), F32)
        din("wv1", (4, 128, D), F32)
        din("wo1", (16, 128, D), F32)
        din("bq1", (128, 16), F32)
        din("bk1", (128, 4), F32)
        din("bv1", (128, 4), F32)
        din("bo1", (128, 16), F32)
    if "f1" in need:
        din("wgT1", (L1HCH, 128, D), I32)
        din("wuT1", (L1HCH, 128, D), I32)
        din("wdT1", (16, 128, L1HID), I32)
        din("sg1", (128, L1HCH), F32)
        din("su1", (128, L1HCH), F32)
        din("sd1", (128, 16), F32)

    cc = {}
    if "a0" in need:
        cc["k0i"] = nc.dram_tensor("k0i", [D, T], BF16).ap()
        cc["k0g"] = nc.dram_tensor("k0g", [2 * D, T], BF16).ap()
        cc["v0i"] = nc.dram_tensor("v0i", [T, D], BF16).ap()
        cc["v0g"] = nc.dram_tensor("v0g", [2 * T, D], BF16).ap()
    if "a1" in need:
        cc["k1i"] = nc.dram_tensor("k1i", [512, T], BF16).ap()
        cc["k1g"] = nc.dram_tensor("k1g", [1024, T], BF16).ap()
        cc["v1i"] = nc.dram_tensor("v1i", [T, 512], BF16).ap()
        cc["v1g"] = nc.dram_tensor("v1g", [2 * T, 512], BF16).ap()

    with ExitStack() as st:
        ARENA = 207 * 1024
        arena_t = st.enter_context(
            nc.sbuf_tensor("arena", [128, ARENA], mybir.dt.uint8))
        A = Arena(arena_t[:, :], ARENA)
        PBT = st.enter_context(nc.psum_tensor("pball", [128, 4096], F32))
        PBALL = PBT[:, :]
        PB = [PBALL[:, 512 * k:512 * (k + 1)] for k in range(8)]
        PBR = [Region(f"ps{k}") for k in range(8)]
        PBP = [PBALL[:, 1024 * p:1024 * (p + 1)] for p in range(4)]
        PBbf = [PB[k].bitcast(BF16) for k in range(8)]

        ones_col, R_onc = A.tile(1, BF16)
        onesf_row, R_onfr = A.tile(128, F32, parts=1)
        c128f_row, R_c128 = A.tile(128, F32, parts=1)
        rowbuf, R_rows = A.tile(2048, F32, parts=1)

        def rows(off, n):
            return rowbuf[0:1, off:off + n], R_rows.sub(4 * off, 4 * n)

        eps_ln_t, R_epl = A.tile(1, F32, parts=1)
        eps_rms_t, R_epr = A.tile(1, F32, parts=1)
        P.op("gpsimd", lambda e: e.memset(ones_col, 1.0), writes=[R_onc])
        P.op("gpsimd", lambda e: e.memset(onesf_row, 1.0), writes=[R_onfr])
        P.op("gpsimd", lambda e: e.memset(c128f_row, 128.0), writes=[R_c128])
        P.op("gpsimd", lambda e: e.memset(eps_ln_t, EPS_LN), writes=[R_epl])
        P.op("gpsimd", lambda e: e.memset(eps_rms_t, EPS_RMS), writes=[R_epr])

        def ln_stats_bc(chunk_src, n_ch, n_feat, eps, tag):
            bc_r, R_bcr = A.tile(T, F32)
            bc_m, R_bcm = A.tile(T, F32)
            A.push()
            sqr = Rot(A, 2, T, BF16)
            for c in range(n_ch):
                src_ap, src_regs = chunk_src(c)
                sq, R_sq = sqr.nxt()
                P.op("scalar", lambda e, src_ap=src_ap, sq=sq: e.activation(
                    out=sq, in_=src_ap, func=AF.Square),
                    reads=list(src_regs), writes=[R_sq], tag=tag + f".sq{c}")
                for tt in range(2):
                    P.op("tensor", lambda e, src_ap=src_ap, tt=tt, c=c: e.matmul(
                        PB[4 + tt][0:1, :], ones_col,
                        src_ap[:, tt * 512:tt * 512 + 512],
                        start=(c == 0), stop=(c == n_ch - 1)),
                        reads=list(src_regs) + [R_onc], writes=[PBR[4 + tt]],
                        tag=tag + f".s1_{c}{tt}")
                    P.op("tensor", lambda e, sq=sq, tt=tt, c=c: e.matmul(
                        PB[6 + tt][0:1, :], ones_col,
                        sq[:, tt * 512:tt * 512 + 512],
                        start=(c == 0), stop=(c == n_ch - 1)),
                        reads=[R_sq, R_onc], writes=[PBR[6 + tt]],
                        tag=tag + f".s2_{c}{tt}")
            A.pop()
            m_ap, m_reg = rows(0, T)
            v_ap, v_reg = rows(T, T)
            inv = 1.0 / n_feat
            for tt in range(2):
                P.op("vector", lambda e, tt=tt: e.tensor_scalar(
                    m_ap[:, tt * 512:tt * 512 + 512], PB[4 + tt][0:1, :], inv,
                    None, ALU.mult), reads=[PBR[4 + tt]], writes=[m_reg])
                P.op("vector", lambda e, tt=tt: e.tensor_scalar(
                    v_ap[:, tt * 512:tt * 512 + 512], PB[6 + tt][0:1, :], inv,
                    None, ALU.mult), reads=[PBR[6 + tt]], writes=[v_reg])
            P.op("scalar", lambda e: e.activation(
                out=PB[4][0:1, 0:512], in_=m_ap[:, 0:512], func=AF.Square),
                reads=[m_reg], writes=[PBR[4]])
            P.op("scalar", lambda e: e.activation(
                out=PB[5][0:1, 0:512], in_=m_ap[:, 512:1024], func=AF.Square),
                reads=[m_reg], writes=[PBR[5]])
            P.op("vector", lambda e: e.tensor_sub(
                v_ap[:, 0:512], v_ap[:, 0:512], PB[4][0:1, 0:512]),
                reads=[v_reg, PBR[4]], writes=[v_reg])
            P.op("vector", lambda e: e.tensor_sub(
                v_ap[:, 512:1024], v_ap[:, 512:1024], PB[5][0:1, 0:512]),
                reads=[v_reg, PBR[5]], writes=[v_reg])
            P.op("scalar", lambda e: e.activation(out=v_ap, in_=v_ap, func=AF.Ln,
                                                  bias=eps_ln_t),
                 reads=[v_reg, R_epl], writes=[v_reg])
            P.op("scalar", lambda e: e.activation(out=v_ap, in_=v_ap,
                                                  func=AF.Exp, scale=-0.5),
                 reads=[v_reg], writes=[v_reg])
            P.op("vector", lambda e: e.tensor_mul(m_ap, m_ap, v_ap),
                 reads=[m_reg, v_reg], writes=[m_reg])
            for tt in range(2):
                P.op("tensor", lambda e, tt=tt: e.matmul(
                    PB[4 + tt], onesf_row, v_ap[:, tt * 512:tt * 512 + 512],
                    start=True, stop=True),
                    reads=[v_reg, R_onfr], writes=[PBR[4 + tt]])
                P.op("tensor", lambda e, tt=tt: e.matmul(
                    PB[6 + tt], onesf_row, m_ap[:, tt * 512:tt * 512 + 512],
                    start=True, stop=True),
                    reads=[m_reg, R_onfr], writes=[PBR[6 + tt]])
                P.op("scalar", lambda e, tt=tt: e.copy(
                    bc_r[:, tt * 512:tt * 512 + 512], PB[4 + tt]),
                    reads=[PBR[4 + tt]], writes=[R_bcr])
                P.op("scalar", lambda e, tt=tt: e.copy(
                    bc_m[:, tt * 512:tt * 512 + 512], PB[6 + tt]),
                    reads=[PBR[6 + tt]], writes=[R_bcm])
            return bc_r, R_bcr, bc_m, R_bcm

        def emit_preln(src_ap, src_name, xn_bf, R_xn, tag, f32_cb=None):
            A.push()
            xcr = Rot(A, 3, T, F32)
            xbr = Rot(A, 2, T, BF16)

            def load_chunk(c, xc, R_xc):
                P.op("sync", lambda e, c=c, xc=xc: e.dma_start(
                    out=xc, in_=src_ap[c * 128:(c + 1) * 128, :]),
                    reads=[Region("dram:" + src_name, c * 128 * T * 4,
                                  (c + 1) * 128 * T * 4)],
                    writes=[R_xc], dma=True, tag=tag + f".ldx{c}")

            chunk_cache = {}

            def chunk_src(c):
                xc, R_xc = xcr.nxt()
                load_chunk(c, xc, R_xc)
                xb, R_xb = xbr.nxt()
                P.op("vector", lambda e, xb=xb, xc=xc: e.tensor_copy(xb, xc),
                     reads=[R_xc], writes=[R_xb])
                return xb, [R_xb]

            bc_r, R_bcr, bc_m, R_bcm = ln_stats_bc(chunk_src, NC, D, EPS_LN, tag)
            tmp = Rot(A, 2, T, F32)
            for c in range(NC):
                xc, R_xc = xcr.nxt()
                load_chunk(c, xc, R_xc)
                tf, R_tf = tmp.nxt()
                R_nc = R_xn.sub(2 * c * T, 2 * T)
                P.op("vector", lambda e, xc=xc, tf=tf: e.tensor_mul(
                    tf, xc, bc_r), reads=[R_xc, R_bcr], writes=[R_tf])
                if f32_cb is None:
                    P.op("vector", lambda e, c=c, tf=tf: e.tensor_sub(
                        xn_bf[:, c * T:(c + 1) * T], tf, bc_m),
                        reads=[R_tf, R_bcm], writes=[R_nc])
                else:
                    P.op("vector", lambda e, tf=tf: e.tensor_sub(tf, tf, bc_m),
                         reads=[R_tf, R_bcm], writes=[R_tf])
                    P.op("gpsimd", lambda e, c=c, tf=tf: e.tensor_copy(
                        xn_bf[:, c * T:(c + 1) * T], tf),
                        reads=[R_tf], writes=[R_nc])
                    f32_cb(c, tf, R_tf)
            A.pop()

        def emit_postres(src_ap, src_name, dst_ap, dst_name, out_bf, R_out,
                         tag):
            A.push()

            def chunk_src(c):
                return (out_bf[:, c * T:(c + 1) * T],
                        [R_out.sub(2 * c * T, 2 * T)])

            bc_r, R_bcr, bc_m, R_bcm = ln_stats_bc(chunk_src, NC, D, EPS_LN,
                                                   tag)
            xcr = Rot(A, 3, T, F32)
            xnr = Rot(A, 3, T, F32)
            tmp = Rot(A, 2, T, F32)
            store_toks = []
            for c in range(NC):
                xc, R_xc = xcr.nxt()
                P.op("sync", lambda e, c=c, xc=xc: e.dma_start(
                    out=xc, in_=src_ap[c * 128:(c + 1) * 128, :]),
                    reads=[Region("dram:" + src_name, c * 128 * T * 4,
                                  (c + 1) * 128 * T * 4)],
                    writes=[R_xc], dma=True)
                tf, R_tf = tmp.nxt()
                P.op("vector", lambda e, c=c, tf=tf: e.tensor_mul(
                    tf, out_bf[:, c * T:(c + 1) * T], bc_r),
                    reads=[R_out.sub(2 * c * T, 2 * T), R_bcr], writes=[R_tf])
                P.op("vector", lambda e, tf=tf: e.tensor_sub(tf, tf, bc_m),
                     reads=[R_tf, R_bcm], writes=[R_tf])
                xo, R_xo = xnr.nxt()
                P.op("vector", lambda e, xo=xo, xc=xc, tf=tf: e.tensor_add(
                    xo, xc, tf), reads=[R_xc, R_tf], writes=[R_xo])
                tok = P.op("sync", lambda e, c=c, xo=xo: e.dma_start(
                    out=dst_ap[c * 128:(c + 1) * 128, :], in_=xo),
                    reads=[R_xo], writes=[dreg(dst_name) if False else
                                          Region("dram:" + dst_name,
                                                 c * 128 * T * 4,
                                                 (c + 1) * 128 * T * 4)],
                    dma=True)
                store_toks.append(tok)
            A.pop()
            return store_toks

        def emit_proj(w_hbm, wname, n_out_ch, src_bf, R_src, out_cb, tag,
                      in_ch=NC, nbuf=3):
            A.push()
            wt = Rot(A, nbuf, in_ch * 128, BF16)
            for o in range(n_out_ch):
                w, R_w = wt.nxt()

                def ldw(e, o=o, w=w):
                    return e.dma_start(out=w, in_=w_hbm[o])
                P.op("gpsimd", ldw, reads=[dreg(wname)], writes=[R_w], dma=True,
                     tag=tag + f".w{o}")
                p = o % 2
                for tt in range(2):
                    bank = p * 2 + tt

                    def mm(e, tt=tt, w=w, bank=bank):
                        ins = None
                        for c in range(in_ch):
                            ins = e.matmul(
                                PB[bank], w[:, c * 128:(c + 1) * 128],
                                src_bf[:, c * T + tt * 512: c * T + tt * 512 + 512],
                                start=(c == 0), stop=(c == in_ch - 1))
                        return ins
                    P.op("tensor", mm, reads=[R_w, R_src], writes=[PBR[bank]],
                         tag=tag + f".mm{o}_{tt}")
                out_cb(o, PBP[p], [PBR[p * 2], PBR[p * 2 + 1]])
            A.pop()

        def phase_attn(layer, first, last):
            is0 = (layer == "a0")
            kvh_n = 16 if is0 else 4
            kv_dim = kvh_n * 128
            kgname, vgname = ("k0g", "v0g") if is0 else ("k1g", "v1g")
            kiname, viname = ("k0i", "v0i") if is0 else ("k1i", "v1i")
            src_name = "x" if first else "x_spill"
            src_ap = x_in if first else x_spill
            dst_name = "xout" if last else "x_spill"
            dst_ap = x_out if last else x_spill

            A.push()
            q_bf, R_q = A.tile(16 * T, BF16)
            ao_bf, R_ao = A.tile(16 * T, BF16)
            idb_sb, R_idb = A.tile(128, BF16)
            P.op("sync", lambda e: e.dma_start(out=idb_sb, in_=dram["identb"]),
                 reads=[dreg("identb")], writes=[R_idb], dma=True)
            if is0:
                ropeC_sb, R_rc = A.tile(16, F32)
                ropeS_sb, R_rs = A.tile(16, F32)
                qnw_sb, R_qnw = A.tile(128, F32, parts=1)
                knw_sb, R_knw = A.tile(128, F32, parts=1)
                for nm, ap_, rg in (("ropeHC", ropeC_sb, R_rc),
                                    ("ropeHS", ropeS_sb, R_rs),
                                    ("qnw", qnw_sb, R_qnw),
                                    ("knw", knw_sb, R_knw)):
                    P.op("sync", lambda e, nm=nm, ap_=ap_: e.dma_start(
                        out=ap_, in_=dram[nm]),
                        reads=[dreg(nm)], writes=[rg], dma=True)
            else:
                bq_sb, R_bq = A.tile(16, F32)
                bk_sb, R_bk = A.tile(4, F32)
                bv_sb, R_bv = A.tile(4, F32)
                bo_sb, R_bo = A.tile(16, F32)
                for nm, ap_, rg in (("bq1", bq_sb, R_bq), ("bk1", bk_sb, R_bk),
                                    ("bv1", bv_sb, R_bv), ("bo1", bo_sb, R_bo)):
                    P.op("sync", lambda e, nm=nm, ap_=ap_: e.dma_start(
                        out=ap_, in_=dram[nm]),
                        reads=[dreg(nm)], writes=[rg], dma=True)

            A.push()
            xn_bf, R_xn = A.tile(NC * T, BF16)
            emit_preln(src_ap, src_name, xn_bf, R_xn, layer + ".ln")

            praw = Rot(A, 2, T, F32)
            sqr = Rot(A, 2, T, BF16)
            bctr0 = Rot(A, 2, T, F32)
            qnr = Rot(A, 2, T, F32)
            t0r = Rot(A, 2, T, F32)
            khead = Rot(A, 2, T, BF16)
            vhead = Rot(A, 2, T, BF16)
            vtmp = Rot(A, 2, T, BF16)

            def stage_k(o, kh, R_kh):
                P.op("sync", lambda e, o=o, kh=kh: e.dma_start(
                    out=cc[kiname][o * 128:(o + 1) * 128, :], in_=kh),
                    reads=[R_kh],
                    writes=[Region("dram:" + kiname, o * 128 * T * 2,
                                   (o + 1) * 128 * T * 2)], dma=True)

            def stage_v(o, vh, R_vh):
                P.op("sync", lambda e, o=o, vh=vh: e.dma_start(
                    out=cc[viname][:, o * 128:(o + 1) * 128].rearrange(
                        "(tc p) d -> p tc d", p=128),
                    in_=vh.rearrange("p (tc d) -> p tc d", d=128)),
                    reads=[R_vh],
                    writes=[Region("dram:" + viname, o, o + 1)], dma=True)

            def qk_post_l0(sink, nw_sb, R_nw):
                def cb(o, pp, pregs):
                    pr, R_pr = praw.nxt()
                    P.op("scalar", lambda e, pr=pr, pp=pp: e.copy(pr, pp),
                         reads=pregs, writes=[R_pr])
                    sqb, R_sqb = sqr.nxt()
                    P.op("scalar", lambda e, sqb=sqb, pr=pr: e.activation(
                        out=sqb, in_=pr, func=AF.Square),
                        reads=[R_pr], writes=[R_sqb])
                    roff = 1024 * (o % 2)
                    rs_ap, rs_reg = rows(roff, 1024)
                    for tt in range(2):
                        P.op("tensor", lambda e, sqb=sqb, tt=tt: e.matmul(
                            PB[4 + tt][0:1, :], ones_col,
                            sqb[:, tt * 512:tt * 512 + 512],
                            start=True, stop=True),
                            reads=[R_sqb, R_onc], writes=[PBR[4 + tt]])
                        P.op("scalar", lambda e, rs_ap=rs_ap, tt=tt:
                             e.activation(out=rs_ap[:, tt * 512:tt * 512 + 512],
                                          in_=PB[4 + tt][0:1, :], func=AF.Ln,
                                          scale=1.0 / DH, bias=eps_rms_t),
                             reads=[PBR[4 + tt], R_epr], writes=[rs_reg])
                    P.op("scalar", lambda e, rs_ap=rs_ap: e.activation(
                        out=rs_ap, in_=rs_ap, func=AF.Exp, scale=-0.5),
                        reads=[rs_reg], writes=[rs_reg])
                    for tt in range(2):
                        P.op("tensor", lambda e, rs_ap=rs_ap, tt=tt: e.matmul(
                            PB[4 + tt], nw_sb,
                            rs_ap[:, tt * 512:tt * 512 + 512],
                            start=True, stop=True),
                            reads=[rs_reg, R_nw], writes=[PBR[4 + tt]])
                    bct, R_bct = bctr0.nxt()
                    P.op("scalar", lambda e, bct=bct: e.copy(bct, PBP[2]),
                         reads=[PBR[4], PBR[5]], writes=[R_bct])
                    qn, R_qn = qnr.nxt()
                    P.op("vector", lambda e, qn=qn, pr=pr, bct=bct:
                         e.tensor_mul(qn, pr, bct),
                         reads=[R_pr, R_bct], writes=[R_qn])
                    t0, R_t0 = t0r.nxt()
                    P.op("vector", lambda e, t0=t0, qn=qn, o=o:
                         e.tensor_scalar(t0, qn, ropeC_sb[:, o:o + 1], None,
                                         ALU.mult),
                         reads=[R_qn, R_rc], writes=[R_t0])
                    P.op("vector", lambda e, qn=qn, o=o: e.tensor_scalar(
                        PBP[2], qn, ropeS_sb[:, o:o + 1], None, ALU.mult),
                        reads=[R_qn, R_rs], writes=[PBR[4], PBR[5]])
                    d_sl, R_dsl = sink(o)
                    P.op("vector", lambda e, d_sl=d_sl, t0=t0: e.tensor_sub(
                        d_sl[0:64, :], t0[0:64, :], PBP[2][64:128, :]),
                        reads=[R_t0, PBR[4], PBR[5]], writes=[R_dsl])
                    P.op("vector", lambda e, d_sl=d_sl, t0=t0: e.tensor_add(
                        d_sl[64:128, :], t0[64:128, :], PBP[2][0:64, :]),
                        reads=[R_t0, PBR[4], PBR[5]], writes=[R_dsl])
                return cb

            kstate = {}

            def k_sink(o):
                kstate["t"] = khead.nxt()
                return kstate["t"]

            def k_post_wrap(cb):
                def cb2(o, pp, pregs):
                    cb(o, pp, pregs)
                    kh, R_kh = kstate["t"]
                    stage_k(o, kh, R_kh)
                return cb2

            def bias_sink_cb(sink, b_sb, R_b, stage_fn=None):
                def cb(o, pp, pregs):
                    d_sl, R_dsl = sink(o)
                    if b_sb is None:
                        P.op("scalar", lambda e, d_sl=d_sl, pp=pp: e.copy(
                            d_sl, pp), reads=pregs, writes=[R_dsl])
                    else:
                        P.op("vector", lambda e, d_sl=d_sl, pp=pp, o=o:
                             e.tensor_scalar(d_sl, pp, b_sb[:, o:o + 1],
                                             None, ALU.add),
                             reads=pregs + [R_b], writes=[R_dsl])
                    if stage_fn is not None:
                        stage_fn(o)
                return cb

            if is0:
                emit_proj(dram["wkT0"], "wkT0", kvh_n, xn_bf, R_xn,
                          k_post_wrap(qk_post_l0(k_sink, knw_sb, R_knw)),
                          layer + ".k")
            else:
                emit_proj(dram["wk1"], "wk1", kvh_n, xn_bf, R_xn,
                          bias_sink_cb(k_sink, bk_sb, R_bk,
                                       lambda o: stage_k(*((o,) + kstate["t"]))),
                          layer + ".k")

            P.op("gpsimd", lambda e: e.collective_compute(
                "AllGather", ALU.bypass, replica_groups=GROUPS,
                ins=[cc[kiname]], outs=[cc[kgname]]),
                reads=[Region("dram:" + kiname)], writes=[dreg(kgname)],
                coll=True)

            def v_evac_cb(o, pp, pregs):
                vt, R_vtt = vtmp.nxt()
                if is0:
                    P.op("scalar", lambda e, vt=vt, pp=pp: e.copy(vt, pp),
                         reads=pregs, writes=[R_vtt])
                else:
                    P.op("vector", lambda e, vt=vt, pp=pp, o=o:
                         e.tensor_scalar(vt, pp, bv_sb[:, o:o + 1], None,
                                         ALU.add),
                         reads=pregs + [R_bv], writes=[R_vtt])
                vh, R_vh = vhead.nxt()
                for tc in range(8):
                    bank2 = 4 + tc % 2
                    P.op("tensor", lambda e, vt=vt, tc=tc, bank2=bank2:
                         e.transpose(PBbf[bank2][:, 0:128],
                                     vt[:, tc * 128:tc * 128 + 128], idb_sb),
                         reads=[R_vtt, R_idb], writes=[PBR[bank2]])
                    P.op("vector", lambda e, vh=vh, tc=tc, bank2=bank2:
                         e.tensor_copy(vh[:, tc * 128:tc * 128 + 128],
                                       PBbf[bank2][:, 0:128]),
                         reads=[PBR[bank2]], writes=[R_vh])
                stage_v(o, vh, R_vh)

            if is0:
                emit_proj(dram["wvT0"], "wvT0", kvh_n, xn_bf, R_xn, v_evac_cb,
                          layer + ".v")
            else:
                emit_proj(dram["wv1"], "wv1", kvh_n, xn_bf, R_xn, v_evac_cb,
                          layer + ".v")

            P.op("gpsimd", lambda e: e.collective_compute(
                "AllGather", ALU.bypass, replica_groups=GROUPS,
                ins=[cc[viname]], outs=[cc[vgname]]),
                reads=[Region("dram:" + viname)], writes=[dreg(vgname)],
                coll=True)

            def q_sink(o):
                sl = q_bf[:, o * T:(o + 1) * T]
                return sl, R_q.sub(2 * o * T, 2 * T)

            if is0:
                emit_proj(dram["wqT0"], "wqT0", 16, xn_bf, R_xn,
                          qk_post_l0(q_sink, qnw_sb, R_qnw), layer + ".q")
            else:
                emit_proj(dram["wq1"], "wq1", 16, xn_bf, R_xn,
                          bias_sink_cb(q_sink, bq_sb, R_bq), layer + ".q")
            A.pop()

            A.push()
            mask_sb, R_mask = A.tile(16 * T, BF16)
            P.op("sync", lambda e: e.dma_start(out=mask_sb, in_=dram["pmask"]),
                reads=[dreg("pmask")], writes=[R_mask], dma=True)
            GH = 4
            kbuf = Rot(A, 2, GH * SEQ, BF16)
            vbuf = Rot(A, 2, GH * SEQ, BF16)
            ptr = Rot(A, 6, T, BF16)
            utr = Rot(A, 3, T, F32)
            bctr = Rot(A, 2, T, F32)

            def load_group(g):
                kt, R_kt = kbuf.nxt()
                vt, R_vt2 = vbuf.nxt()
                for hl in range(GH):
                    h = g * GH + hl
                    kvh = h * kvh_n // 16
                    for r in range(2):
                        P.op("sync", lambda e, hl=hl, r=r, kvh=kvh, kt=kt:
                             e.dma_start(
                                 out=kt[:, hl * SEQ + r * T: hl * SEQ + r * T + T],
                                 in_=cc[kgname][r * kv_dim + 128 * kvh:
                                                r * kv_dim + 128 * kvh + 128, :]),
                             reads=[dreg(kgname)], writes=[R_kt], dma=True)
                        P.op("sync", lambda e, hl=hl, r=r, kvh=kvh, vt=vt:
                             e.dma_start(
                                 out=vt[:, hl * SEQ + r * T: hl * SEQ + r * T + T]
                                 .rearrange("p (tc d) -> p tc d", d=128),
                                 in_=cc[vgname][r * T:(r + 1) * T,
                                                128 * kvh:128 * kvh + 128]
                                 .rearrange("(tc p) d -> p tc d", p=128)),
                             reads=[dreg(vgname)], writes=[R_vt2], dma=True)
                return kt, R_kt, vt, R_vt2

            cur = load_group(0)
            for g in range(4):
                kt, R_kt, vt, R_vt2 = cur
                nxt = load_group(g + 1) if g < 3 else None
                for hl in range(GH):
                    h = g * GH + hl
                    ptiles = {}

                    def emit_sums_O(j):
                        pt, R_pt = ptiles.pop(j)
                        for tt in range(2):
                            P.op("tensor", lambda e, pt=pt, j=j, tt=tt:
                                 e.matmul(PB[4 + tt][0:1, :], ones_col,
                                          pt[:, tt * 512:tt * 512 + 512],
                                          start=(j == 0), stop=(j == 15)),
                                 reads=[R_pt, R_onc], writes=[PBR[4 + tt]])
                            vsl = vt[:, hl * SEQ + j * 128:
                                     hl * SEQ + j * 128 + 128]
                            P.op("tensor", lambda e, vsl=vsl, pt=pt, j=j, tt=tt:
                                 e.matmul(PB[6 + tt], vsl,
                                          pt[:, tt * 512:tt * 512 + 512],
                                          start=(j == 0), stop=(j == 15)),
                                 reads=[R_vt2, R_pt], writes=[PBR[6 + tt]])

                    R_qsl = R_q.sub(2 * h * T, 2 * T)
                    for j in range(16):
                        sp = j % 2
                        ksl = kt[:, hl * SEQ + j * 128: hl * SEQ + j * 128 + 128]
                        for tt in range(2):
                            qsl = q_bf[:, h * T + tt * 512: h * T + tt * 512 + 512]
                            P.op("tensor", lambda e, ksl=ksl, qsl=qsl, sp=sp,
                                 tt=tt: e.matmul(PB[sp * 2 + tt], ksl, qsl,
                                                 start=True, stop=True),
                                 reads=[R_kt, R_qsl], writes=[PBR[sp * 2 + tt]])
                        pt, R_pt = ptr.nxt()
                        ptiles[j] = (pt, R_pt)
                        if is0:
                            ut, R_ut = utr.nxt()
                            P.op("scalar", lambda e, sp=sp, ut=ut: e.activation(
                                out=ut, in_=PBP[sp], func=AF.Tanh,
                                scale=SC_ATT / SOFTCAP),
                                reads=[PBR[sp * 2], PBR[sp * 2 + 1]],
                                writes=[R_ut])
                            P.op("scalar", lambda e, ut=ut, pt=pt: e.activation(
                                out=pt, in_=ut, func=AF.Exp, scale=SOFTCAP),
                                reads=[R_ut], writes=[R_pt])
                        else:
                            P.op("scalar", lambda e, sp=sp, pt=pt: e.activation(
                                out=pt, in_=PBP[sp], func=AF.Exp, scale=SC_ATT),
                                reads=[PBR[sp * 2], PBR[sp * 2 + 1]],
                                writes=[R_pt])
                        msl = mask_sb[:, j * T:(j + 1) * T]
                        P.op("vector", lambda e, pt=pt, msl=msl: e.tensor_mul(
                            pt, pt, msl), reads=[R_mask, R_pt], writes=[R_pt])
                        if j >= 3:
                            emit_sums_O(j - 3)
                    emit_sums_O(13)
                    emit_sums_O(14)
                    emit_sums_O(15)
                    roff = T + 0
                    inv_ap, inv_reg = rows(roff, 1024)
                    for tt in range(2):
                        P.op("vector", lambda e, tt=tt, inv_ap=inv_ap:
                             e.reciprocal(inv_ap[:, tt * 512:tt * 512 + 512],
                                          PB[4 + tt][0:1, :]),
                             reads=[PBR[4 + tt]], writes=[inv_reg])
                        P.op("tensor", lambda e, tt=tt, inv_ap=inv_ap: e.matmul(
                            PB[4 + tt], onesf_row,
                            inv_ap[:, tt * 512:tt * 512 + 512],
                            start=True, stop=True),
                            reads=[inv_reg, R_onfr], writes=[PBR[4 + tt]])
                    bct, R_bct = bctr.nxt()
                    P.op("scalar", lambda e, bct=bct: e.copy(bct, PBP[2]),
                         reads=[PBR[4], PBR[5]], writes=[R_bct])
                    osl = ao_bf[:, h * T:(h + 1) * T]
                    R_osl = R_ao.sub(2 * h * T, 2 * T)
                    P.op("vector", lambda e, osl=osl, bct=bct: e.tensor_mul(
                        osl, PBP[3], bct),
                        reads=[PBR[6], PBR[7], R_bct], writes=[R_osl])
                cur = nxt
            A.pop()

            aoo_bf, R_aoo = A.tile(NC * T, BF16)

            def o_sink(o):
                sl = aoo_bf[:, o * T:(o + 1) * T]
                return sl, R_aoo.sub(2 * o * T, 2 * T)

            if is0:
                emit_proj(dram["woT0"], "woT0", NC, ao_bf, R_ao,
                          bias_sink_cb(o_sink, None, None), layer + ".o",
                          in_ch=16)
            else:
                emit_proj(dram["wo1"], "wo1", NC, ao_bf, R_ao,
                          bias_sink_cb(o_sink, bo_sb, R_bo), layer + ".o",
                          in_ch=16)
            emit_postres(src_ap, src_name, dst_ap, dst_name, aoo_bf, R_aoo,
                         layer + ".post")
            A.pop()

        def phase_f0(first, last):
            src_name = "x" if first else "x_spill"
            src_ap = x_in if first else x_spill
            dst_name = "xout" if last else "x_spill"
            dst_ap = x_out if last else x_spill

            A.push()
            y_sb, R_y = A.tile(NC * T, F32)
            A.push()
            hp_bf, R_hp = A.tile(ICH * T, BF16)
            xn_bf, R_xn = A.tile(NC * T, BF16)
            wg_sb, R_wg = A.tile(16 * 16, F32)
            idf_t, R_idft = A.tile(128, F32)
            P.op("sync", lambda e: e.dma_start(out=wg_sb, in_=dram["WgT"]),
                reads=[dreg("WgT")], writes=[R_wg], dma=True)
            P.op("sync", lambda e: e.dma_start(out=idf_t, in_=dram["identf"]),
                 reads=[dreg("identf")], writes=[R_idft], dma=True)

            def router_cb(c, tf, R_tf):
                for tt in range(2):
                    P.op("tensor", lambda e, c=c, tf=tf, tt=tt: e.matmul(
                        PB[2 + tt][0:16, :], wg_sb[:, c * 16:(c + 1) * 16],
                        tf[:, tt * 512:tt * 512 + 512],
                        start=(c == 0), stop=(c == NC - 1)),
                        reads=[R_tf, R_wg], writes=[PBR[2 + tt]])

            emit_preln(src_ap, src_name, xn_bf, R_xn, "f0.ln",
                       f32_cb=router_cb)

            lg_sb, R_lg = A.tile(T, F32, parts=16)
            for tt in range(2):
                P.op("scalar", lambda e, tt=tt: e.copy(
                    lg_sb[:, tt * 512:tt * 512 + 512], PB[2 + tt][0:16, :]),
                    reads=[PBR[2 + tt]], writes=[R_lg])
            ltm, R_ltm = A.tile(8 * 16, F32)
            for tc in range(8):
                bank = tc % 2
                P.op("tensor", lambda e, tc=tc, bank=bank: e.transpose(
                    PB[bank][:, 0:16], lg_sb[:, tc * 128:(tc + 1) * 128],
                    idf_t[0:16, 0:16]),
                    reads=[R_lg, R_idft], writes=[PBR[bank]])
                P.op("vector", lambda e, tc=tc, bank=bank: e.tensor_copy(
                    ltm[:, tc * 16:(tc + 1) * 16], PB[bank][:, 0:16]),
                    reads=[PBR[bank]], writes=[R_ltm])

            cmb_tm, R_cmb = A.tile(8 * 16, F32)
            sc_t, R_sct = A.tile(16, F32)
            gs_t, R_gst = A.tile(4, F32)
            gt_t, R_gt = A.tile(16, F32)
            r1_t, R_r1 = A.tile(1, F32)
            r2_t, R_r2 = A.tile(1, F32)
            ex_t, R_ex = A.tile(16, F32)
            for tc in range(8):
                lt = ltm[:, tc * 16:(tc + 1) * 16]
                R_lt = R_ltm.sub(4 * tc * 16, 4 * 16)
                P.op("scalar", lambda e, lt=lt: e.activation(
                    out=ex_t, in_=lt, func=AF.Exp),
                    reads=[R_lt], writes=[R_ex])
                P.op("vector", lambda e: e.reduce_sum(r1_t, ex_t, axis=AX.X),
                     reads=[R_ex], writes=[R_r1])
                P.op("vector", lambda e: e.reciprocal(r2_t, r1_t),
                     reads=[R_r1], writes=[R_r2])
                P.op("vector", lambda e: e.tensor_scalar(
                    sc_t, ex_t, r2_t, None, ALU.mult),
                    reads=[R_ex, R_r2], writes=[R_sct])
                for gg in range(4):
                    P.op("vector", lambda e, gg=gg: e.reduce_max(
                        gs_t[:, gg:gg + 1], sc_t[:, gg * 4:(gg + 1) * 4],
                        axis=AX.X), reads=[R_sct], writes=[R_gst])
                for gg in range(4):
                    P.op("vector", lambda e, gg=gg: e.tensor_scalar(
                        gt_t[:, 0:4], gs_t, gs_t[:, gg:gg + 1], None,
                        ALU.is_gt), reads=[R_gst], writes=[R_gt])
                    P.op("vector", lambda e: e.reduce_sum(
                        r1_t, gt_t[:, 0:4], axis=AX.X),
                        reads=[R_gt], writes=[R_r1])
                    P.op("vector", lambda e: e.tensor_scalar(
                        r1_t, r1_t, 1.5, None, ALU.is_le),
                        reads=[R_r1], writes=[R_r1])
                    P.op("vector", lambda e: e.tensor_scalar(
                        r2_t, r1_t, 1e9, -1e9, ALU.mult, ALU.add),
                        reads=[R_r1], writes=[R_r2])
                    P.op("vector", lambda e, gg=gg: e.tensor_scalar(
                        ex_t[:, gg * 4:(gg + 1) * 4],
                        sc_t[:, gg * 4:(gg + 1) * 4], r1_t, r2_t,
                        ALU.mult, ALU.add),
                        reads=[R_sct, R_r1, R_r2], writes=[R_ex])
                for ee in range(16):
                    P.op("vector", lambda e, ee=ee: e.tensor_scalar(
                        gt_t, ex_t, ex_t[:, ee:ee + 1], None, ALU.is_gt),
                        reads=[R_ex], writes=[R_gt])
                    P.op("vector", lambda e: e.reduce_sum(
                        r1_t, gt_t, axis=AX.X), reads=[R_gt], writes=[R_r1])
                    P.op("vector", lambda e: e.tensor_scalar(
                        r1_t, r1_t, 3.5, None, ALU.is_le),
                        reads=[R_r1], writes=[R_r1])
                    dsl = cmb_tm[:, tc * 16 + ee: tc * 16 + ee + 1]
                    R_dsl = R_cmb.sub(4 * (tc * 16 + ee), 4)
                    P.op("vector", lambda e, dsl=dsl, ee=ee: e.tensor_mul(
                        dsl, sc_t[:, ee:ee + 1], r1_t),
                        reads=[R_sct, R_r1], writes=[R_dsl])
            cmb_fm, R_cfm = A.tile(T, F32, parts=16)
            for tc in range(8):
                bank = tc % 2
                P.op("tensor", lambda e, tc=tc, bank=bank: e.transpose(
                    PB[bank][0:16, 0:128], cmb_tm[:, tc * 16:(tc + 1) * 16],
                    idf_t),
                    reads=[R_cmb, R_idft], writes=[PBR[bank]])
                P.op("vector", lambda e, tc=tc, bank=bank: e.tensor_copy(
                    cmb_fm[:, tc * 128:(tc + 1) * 128], PB[bank][0:16, 0:128]),
                    reads=[PBR[bank]], writes=[R_cfm])
            esel_sb, R_esel = A.tile(16 * 128, F32, parts=16)
            P.op("sync", lambda e: e.dma_start(out=esel_sb, in_=dram["esel"]),
                 reads=[dreg("esel")], writes=[R_esel], dma=True)

            w13 = Rot(A, 4, NC * 128, BF16)
            w2r = Rot(A, 2, ICH * 128, BF16)
            bce, R_bce = A.tile(T, F32)
            sar = Rot(A, 2, 512, F32)
            tbr = Rot(A, 2, 512, F32)

            for ei, ex in enumerate([16] + list(range(16)) + [17]):
                first_ex = (ei == 0)
                routed = ex < 16
                if routed:
                    for tt in range(2):
                        P.op("tensor", lambda e, ex=ex, tt=tt: e.matmul(
                            PB[6 + tt], esel_sb[:, ex * 128:(ex + 1) * 128],
                            cmb_fm[:, tt * 512:tt * 512 + 512],
                            start=True, stop=True),
                            reads=[R_cfm, R_esel], writes=[PBR[6 + tt]])
                        P.op("scalar", lambda e, tt=tt: e.copy(
                            bce[:, tt * 512:tt * 512 + 512], PB[6 + tt]),
                            reads=[PBR[6 + tt]], writes=[R_bce])
                for ic in range(ICH):
                    bs = 0 if ic % 2 == 0 else 4
                    w1, R_w1 = w13.nxt()
                    w3, R_w3 = w13.nxt()
                    P.op("gpsimd", lambda e, ex=ex, ic=ic, w1=w1: e.dma_start(
                        out=w1, in_=dram["W1T"][ex, ic]),
                        reads=[dreg("W1T")], writes=[R_w1], dma=True)
                    P.op("gpsimd", lambda e, ex=ex, ic=ic, w3=w3: e.dma_start(
                        out=w3, in_=dram["W3T"][ex, ic]),
                        reads=[dreg("W3T")], writes=[R_w3], dma=True)
                    for tt in range(2):
                        for mi, (w, R_w) in enumerate(((w1, R_w1), (w3, R_w3))):
                            bank = bs + mi * 2 + tt

                            def mm(e, w=w, tt=tt, bank=bank):
                                ins = None
                                for c in range(NC):
                                    ins = e.matmul(
                                        PB[bank], w[:, c * 128:(c + 1) * 128],
                                        xn_bf[:, c * T + tt * 512:
                                              c * T + tt * 512 + 512],
                                        start=(c == 0), stop=(c == NC - 1))
                                return ins
                            P.op("tensor", mm, reads=[R_w, R_xn],
                                 writes=[PBR[bank]],
                                 tag=f"f0.e{ex}i{ic}t{tt}m{mi}")
                        sa, R_sa = sar.nxt()
                        P.op("scalar", lambda e, sa=sa, bs=bs, tt=tt:
                             e.activation(out=sa, in_=PB[bs + tt],
                                          func=AF.Silu),
                             reads=[PBR[bs + tt]], writes=[R_sa])
                        hsl = hp_bf[:, ic * T + tt * 512: ic * T + tt * 512 + 512]
                        R_hsl = R_hp.sub(2 * (ic * T + tt * 512), 2 * 512)
                        if routed:
                            tb, R_tb = tbr.nxt()
                            P.op("vector", lambda e, tb=tb, bs=bs, tt=tt:
                                 e.tensor_mul(tb, PB[bs + 2 + tt],
                                              bce[:, tt * 512:tt * 512 + 512]),
                                 reads=[PBR[bs + 2 + tt], R_bce],
                                 writes=[R_tb])
                            P.op("vector", lambda e, hsl=hsl, tb=tb, sa=sa:
                                 e.tensor_mul(hsl, tb, sa),
                                 reads=[R_tb, R_sa], writes=[R_hsl])
                        else:
                            P.op("vector", lambda e, hsl=hsl, bs=bs, tt=tt,
                                 sa=sa: e.tensor_mul(hsl, PB[bs + 2 + tt], sa),
                                 reads=[PBR[bs + 2 + tt], R_sa],
                                 writes=[R_hsl])
                for o in range(NC):
                    w2, R_w2 = w2r.nxt()
                    P.op("gpsimd", lambda e, ex=ex, o=o, w2=w2: e.dma_start(
                        out=w2, in_=dram["W2T"][ex, o]),
                        reads=[dreg("W2T")], writes=[R_w2], dma=True)
                    for tt in range(2):
                        bank = (o * 2 + tt) % 4

                        def mm2(e, w2=w2, tt=tt, bank=bank):
                            ins = None
                            for ci in range(ICH):
                                ins = e.matmul(
                                    PB[bank], w2[:, ci * 128:(ci + 1) * 128],
                                    hp_bf[:, ci * T + tt * 512:
                                          ci * T + tt * 512 + 512],
                                    start=(ci == 0), stop=(ci == ICH - 1))
                            return ins
                        P.op("tensor", mm2, reads=[R_w2, R_hp],
                             writes=[PBR[bank]], tag=f"f0.w2e{ex}o{o}t{tt}")
                        ysl = y_sb[:, o * T + tt * 512: o * T + tt * 512 + 512]
                        R_ysl = R_y.sub(4 * (o * T + tt * 512), 4 * 512)
                        if first_ex:
                            P.op("vector", lambda e, ysl=ysl, bank=bank:
                                 e.tensor_copy(ysl, PB[bank]),
                                 reads=[PBR[bank]], writes=[R_ysl])
                        else:
                            P.op("vector", lambda e, ysl=ysl, bank=bank:
                                 e.tensor_add(ysl, ysl, PB[bank]),
                                 reads=[PBR[bank], R_ysl], writes=[R_ysl])
            A.pop()

            ybr = Rot(A, 2, T, BF16)
            ycast = {}

            def ychunk(c):
                yb, R_yb = ybr.nxt()
                P.op("vector", lambda e, c=c, yb=yb: e.tensor_copy(
                    yb, y_sb[:, c * T:(c + 1) * T]),
                    reads=[R_y.sub(4 * c * T, 4 * T)], writes=[R_yb])
                return yb, [R_yb]

            bc_r, R_bcr, bc_m, R_bcm = ln_stats_bc(ychunk, NC, D, EPS_LN,
                                                   "f0.post")
            xcr = Rot(A, 3, T, F32)
            xnr = Rot(A, 3, T, F32)
            tmp = Rot(A, 2, T, F32)
            for c in range(NC):
                xc, R_xc = xcr.nxt()
                P.op("sync", lambda e, c=c, xc=xc: e.dma_start(
                    out=xc, in_=src_ap[c * 128:(c + 1) * 128, :]),
                    reads=[Region("dram:" + src_name, c * 128 * T * 4,
                                  (c + 1) * 128 * T * 4)],
                    writes=[R_xc], dma=True)
                tf, R_tf = tmp.nxt()
                P.op("vector", lambda e, c=c, tf=tf: e.tensor_mul(
                    tf, y_sb[:, c * T:(c + 1) * T], bc_r),
                    reads=[R_y.sub(4 * c * T, 4 * T), R_bcr], writes=[R_tf])
                P.op("vector", lambda e, tf=tf: e.tensor_sub(tf, tf, bc_m),
                     reads=[R_tf, R_bcm], writes=[R_tf])
                xo, R_xo = xnr.nxt()
                P.op("vector", lambda e, xo=xo, xc=xc, tf=tf: e.tensor_add(
                    xo, xc, tf), reads=[R_xc, R_tf], writes=[R_xo])
                P.op("sync", lambda e, c=c, xo=xo: e.dma_start(
                    out=dst_ap[c * 128:(c + 1) * 128, :], in_=xo),
                    reads=[R_xo],
                    writes=[Region("dram:" + dst_name, c * 128 * T * 4,
                                   (c + 1) * 128 * T * 4)], dma=True)
            A.pop()

        def phase_f1(first, last):
            src_name = "x" if first else "x_spill"
            src_ap = x_in if first else x_spill
            dst_name = "xout" if last else "x_spill"
            dst_ap = x_out if last else x_spill

            A.push()
            d_bf, R_d = A.tile(NC * T, BF16)
            sg_sb, R_sg = A.tile(L1HCH, F32)
            su_sb, R_su = A.tile(L1HCH, F32)
            sd_sb, R_sd = A.tile(16, F32)
            for nm, ap_, rg in (("sg1", sg_sb, R_sg), ("su1", su_sb, R_su),
                                ("sd1", sd_sb, R_sd)):
                P.op("sync", lambda e, nm=nm, ap_=ap_: e.dma_start(
                    out=ap_, in_=dram[nm]),
                    reads=[dreg(nm)], writes=[rg], dma=True)
            A.push()
            hp_bf, R_hp = A.tile(L1HCH * T, BF16)
            A.push()
            xn_bf, R_xn = A.tile(NC * T, BF16)
            emit_preln(src_ap, src_name, xn_bf, R_xn, "f1.ln")
            debug_tap("f1.xn", xn_bf, R_xn, NC, BF16)

            bcx, R_bcx = A.tile(T, F32)
            for tt in range(2):
                def fcs(e, tt=tt):
                    ins = None
                    for c in range(NC):
                        ins = e.matmul(
                            PB[6 + tt][0:1, :], ones_col,
                            xn_bf[:, c * T + tt * 512: c * T + tt * 512 + 512],
                            start=(c == 0), stop=(c == NC - 1))
                    return ins
                P.op("tensor", fcs, reads=[R_xn, R_onc], writes=[PBR[6 + tt]])
                csr, R_csr = rows(tt * 512, 512)
                P.op("scalar", lambda e, csr=csr, tt=tt: e.copy(
                    csr, PB[6 + tt][0:1, :]),
                    reads=[PBR[6 + tt]], writes=[R_csr])
                P.op("tensor", lambda e, csr=csr, tt=tt: e.matmul(
                    PB[6 + tt], c128f_row, csr, start=True, stop=True),
                    reads=[R_csr, R_c128], writes=[PBR[6 + tt]])
                P.op("scalar", lambda e, tt=tt: e.copy(
                    bcx[:, tt * 512:tt * 512 + 512], PB[6 + tt]),
                    reads=[PBR[6 + tt]], writes=[R_bcx])

            wgr = Rot(A, 2, NC * 128, BF16)
            wur = Rot(A, 2, NC * 128, BF16)
            tgr = Rot(A, 2, 512, F32)
            tur = Rot(A, 2, 512, F32)
            for o in range(L1HCH):
                wg_, R_wg_ = wgr.nxt()
                wu_, R_wu_ = wur.nxt()
                P.op("gpsimd", lambda e, o=o, wg_=wg_: e.dma_start(
                    out=wg_, in_=dram["wgT1"][o]),
                    reads=[dreg("wgT1")], writes=[R_wg_], dma=True)
                P.op("gpsimd", lambda e, o=o, wu_=wu_: e.dma_start(
                    out=wu_, in_=dram["wuT1"][o]),
                    reads=[dreg("wuT1")], writes=[R_wu_], dma=True)
                for tt in range(2):
                    bs = (o % 2) * 4
                    for mi, (w, R_w) in enumerate(((wg_, R_wg_), (wu_, R_wu_))):
                        bank = bs + mi * 2 + tt

                        def mm(e, w=w, tt=tt, bank=bank):
                            ins = None
                            for c in range(NC):
                                ins = e.matmul(
                                    PB[bank], w[:, c * 128:(c + 1) * 128],
                                    xn_bf[:, c * T + tt * 512:
                                          c * T + tt * 512 + 512],
                                    start=(c == 0), stop=(c == NC - 1))
                            return ins
                        P.op("tensor", mm, reads=[R_w, R_xn],
                             writes=[PBR[bank]], tag=f"f1.gu{o}t{tt}m{mi}")
                    tg, R_tg = tgr.nxt()
                    tu, R_tu = tur.nxt()
                    P.op("vector", lambda e, tg=tg, bs=bs, tt=tt: e.tensor_sub(
                        tg, PB[bs + tt], bcx[:, tt * 512:tt * 512 + 512]),
                        reads=[PBR[bs + tt], R_bcx], writes=[R_tg])
                    P.op("scalar", lambda e, tg=tg, o=o: e.activation(
                        out=tg, in_=tg, func=AF.Silu, scale=sg_sb[:, o:o + 1]),
                        reads=[R_tg, R_sg], writes=[R_tg])
                    P.op("vector", lambda e, tu=tu, bs=bs, tt=tt: e.tensor_sub(
                        tu, PB[bs + 2 + tt], bcx[:, tt * 512:tt * 512 + 512]),
                        reads=[PBR[bs + 2 + tt], R_bcx], writes=[R_tu])
                    P.op("vector", lambda e, tu=tu, o=o: e.tensor_scalar(
                        tu, tu, su_sb[:, o:o + 1], None, ALU.mult),
                        reads=[R_tu, R_su], writes=[R_tu])
                    hsl = hp_bf[:, o * T + tt * 512: o * T + tt * 512 + 512]
                    R_hsl = R_hp.sub(2 * (o * T + tt * 512), 2 * 512)
                    P.op("vector", lambda e, hsl=hsl, tg=tg, tu=tu:
                         e.tensor_mul(hsl, tg, tu),
                         reads=[R_tg, R_tu], writes=[R_hsl])
                    if DEBUG_TAP in (f"f1.tg{o}_{tt}", f"f1.tu{o}_{tt}"):
                        dsrc, drg = (tg, R_tg) if "tg" in DEBUG_TAP else (tu, R_tu)
                        dbg2 = nc.dram_tensor("dbg", [128, 512], F32,
                                              kind="ExternalOutput").ap()
                        tok = P.op("sync", lambda e, dsrc=dsrc: e.dma_start(
                            out=dbg2, in_=dsrc), reads=[drg],
                            writes=[Region("dram:dbg")], dma=True)
                        P.final_tokens.append(tok)
                    if DEBUG_TAP == f"f1.hp{o}_{tt}":
                        dbg3 = nc.dram_tensor("dbg", [128, 512], BF16,
                                              kind="ExternalOutput").ap()
                        tok = P.op("sync", lambda e, hsl=hsl: e.dma_start(
                            out=dbg3, in_=hsl), reads=[R_hsl],
                            writes=[Region("dram:dbg")], dma=True)
                        P.final_tokens.append(tok)
            debug_tap("f1.bcx", bcx, R_bcx, 1, F32)
            A.pop()
            debug_tap("f1.hp", hp_bf, R_hp, L1HCH, BF16)
            if DEBUG_STOP == "gate":
                return
            if DEBUG_TAP == "f1.hptail":
                dbg4 = nc.dram_tensor("dbg", [4 * 128, T], BF16,
                                      kind="ExternalOutput").ap()
                tok = P.op("sync", lambda e: e.dma_start(
                    out=dbg4.rearrange("(c p) t -> p c t", p=128),
                    in_=hp_bf[:, 40 * T:44 * T].rearrange(
                        "p (c t) -> p c t", c=4)),
                    reads=[R_hp], writes=[Region("dram:dbg")], dma=True)
                P.final_tokens.append(tok)

            bch, R_bch = A.tile(T, F32)
            for tt in ([] if DEBUG_STOP in ("down_nobch",) else range(2)):
                def fcs2(e, tt=tt):
                    ins = None
                    for c in range(L1HCH):
                        ins = e.matmul(
                            PB[6 + tt][0:1, :], ones_col,
                            hp_bf[:, c * T + tt * 512: c * T + tt * 512 + 512],
                            start=(c == 0), stop=(c == L1HCH - 1))
                    return ins
                P.op("tensor", fcs2, reads=[R_hp, R_onc], writes=[PBR[6 + tt]])
                csr, R_csr = rows(tt * 512, 512)
                P.op("scalar", lambda e, csr=csr, tt=tt: e.copy(
                    csr, PB[6 + tt][0:1, :]),
                    reads=[PBR[6 + tt]], writes=[R_csr])
                P.op("tensor", lambda e, csr=csr, tt=tt: e.matmul(
                    PB[6 + tt], c128f_row, csr, start=True, stop=True),
                    reads=[R_csr, R_c128], writes=[PBR[6 + tt]])
                P.op("scalar", lambda e, tt=tt: e.copy(
                    bch[:, tt * 512:tt * 512 + 512], PB[6 + tt]),
                    reads=[PBR[6 + tt]], writes=[R_bch])

            if DEBUG_STOP == "bch":
                return
            debug_tap("f1.bch", bch, R_bch, 1, F32)
            if DEBUG_STOP == "down_nobch":
                wdr0 = Rot(A, 2, L1HCH * 128, BF16)
                for o in range(16):
                    wd_, R_wd_ = wdr0.nxt()
                    P.op("gpsimd", lambda e, o=o, wd_=wd_: e.dma_start(
                        out=wd_, in_=dram["wdT1"][o]),
                        reads=[dreg("wdT1")], writes=[R_wd_], dma=True)
                return
            if DEBUG_STOP == "down_fresh":
                A.off = 183 * 1024
                wdrf = Rot(A, 2, L1HCH * 128, BF16)
                for o in range(1):
                    wd_, R_wd_ = wdrf.nxt()
                    P.op("gpsimd", lambda e, o=o, wd_=wd_: e.dma_start(
                        out=wd_, in_=dram["wdT1"][o]),
                        reads=[dreg("wdT1")], writes=[R_wd_], dma=True)
                return
            wdr = Rot(A, 2, L1HCH * 128, BF16)
            if DEBUG_STOP in ("down_dma1", "down_dmaonly2"):
                nloads = 1 if DEBUG_STOP == "down_dma1" else 16
                for o in range(nloads):
                    wd_, R_wd_ = wdr.nxt()
                    P.op("gpsimd", lambda e, o=o, wd_=wd_: e.dma_start(
                        out=wd_, in_=dram["wdT1"][o]),
                        reads=[dreg("wdT1")], writes=[R_wd_], dma=True)
                return
            tdr = Rot(A, 2, 512, F32)
            for o in range(NC):
                wd_, R_wd_ = wdr.nxt()
                for q4 in range(4):
                    qs = L1HID // 4
                    P.op("gpsimd", lambda e, o=o, wd_=wd_, q4=q4, qs=qs:
                         e.dma_start(out=wd_[:, q4 * qs:(q4 + 1) * qs],
                                     in_=dram["wdT1"][o][:, q4 * qs:(q4 + 1) * qs]),
                         reads=[dreg("wdT1")], writes=[R_wd_], dma=True)
                for tt in range(2):
                    if DEBUG_STOP == "down_dmaonly":
                        continue
                    bank = (o * 2 + tt) % 4

                    def mmd(e, wd_=wd_, tt=tt, bank=bank):
                        ins = None
                        for c in range(L1HCH):
                            ins = e.matmul(
                                PB[bank], wd_[:, c * 128:(c + 1) * 128],
                                hp_bf[:, c * T + tt * 512:
                                      c * T + tt * 512 + 512],
                                start=(c == 0), stop=(c == L1HCH - 1))
                        return ins
                    P.op("tensor", mmd, reads=[R_wd_, R_hp],
                         writes=[PBR[bank]], tag=f"f1.d{o}t{tt}")
                    td, R_td = tdr.nxt()
                    P.op("vector", lambda e, td=td, bank=bank, tt=tt:
                         e.tensor_sub(td, PB[bank],
                                      bch[:, tt * 512:tt * 512 + 512]),
                         reads=[PBR[bank], R_bch], writes=[R_td])
                    dsl = d_bf[:, o * T + tt * 512: o * T + tt * 512 + 512]
                    R_dsl = R_d.sub(2 * (o * T + tt * 512), 2 * 512)
                    P.op("vector", lambda e, dsl=dsl, td=td, o=o:
                         e.tensor_scalar(dsl, td, sd_sb[:, o:o + 1], None,
                                         ALU.mult),
                         reads=[R_td, R_sd], writes=[R_dsl])
            A.pop()
            debug_tap("f1.d", d_bf, R_d, NC, BF16)
            if DEBUG_STOP == "down":
                return

            emit_postres(src_ap, src_name, dst_ap, dst_name, d_bf, R_d,
                         "f1.post")
            A.pop()

        for pi, ph in enumerate(phases):
            first = (pi == 0)
            last = (pi == len(phases) - 1)
            if ph in ("a0", "a1"):
                phase_attn(ph, first, last)
            elif ph == "f0":
                phase_f0(first, last)
            elif ph == "f1":
                phase_f1(first, last)

        for reg, tok in P.writers:
            if reg.space == "dram:xout":
                P.final_tokens.append(tok)

        P.emit(nc)
    return nc, sorted(dram.keys()), A.peak, P.n_ops, P


def _to_native(WT, in_ch, n_out):
    W = np.asarray(WT)
    return np.ascontiguousarray(
        W.reshape(in_ch, 128, n_out, 128).transpose(2, 1, 0, 3)
        .reshape(n_out, 128, in_ch * 128))


def _prep_host(inp):
    H = {}
    f32 = np.float32

    def ct(a):
        return np.ascontiguousarray(np.asarray(a, dtype=f32))

    H["wqT0"] = ct(np.asarray(inp["L0_attn_wq"], f32).T)
    H["wkT0"] = ct(np.asarray(inp["L0_attn_wk"], f32).T)
    H["wvT0"] = ct(np.asarray(inp["L0_attn_wv"], f32).T)
    H["woT0"] = ct(np.asarray(inp["L0_attn_wo"], f32).T)
    H["qnw"] = ct(inp["L0_attn_q_norm_w"]).reshape(1, 128)
    H["knw"] = ct(inp["L0_attn_k_norm_w"]).reshape(1, 128)
    cos16 = ct(inp["L0_attn_rope_cos"])[0:16, :].T
    sin16 = ct(inp["L0_attn_rope_sin"])[0:16, :].T
    H["ropeHC"] = np.ascontiguousarray(np.concatenate([cos16, cos16], 0))
    H["ropeHS"] = np.ascontiguousarray(np.concatenate([sin16, sin16], 0))
    perm = np.concatenate([np.arange(0, 128, 2), np.arange(1, 128, 2)])
    full_perm = np.concatenate([h * 128 + perm for h in range(16)])
    H["wqT0"] = _to_native(H["wqT0"][:, full_perm], NC, 16)
    H["wkT0"] = _to_native(H["wkT0"][:, full_perm], NC, 16)
    H["wvT0"] = _to_native(H["wvT0"], NC, 16)
    H["woT0"] = _to_native(H["woT0"], 16, NC)
    H["qnw"] = np.ascontiguousarray(H["qnw"][:, perm])
    H["knw"] = np.ascontiguousarray(H["knw"][:, perm])

    wgt = ct(np.asarray(inp["L0_ffn_Wg"], f32).T)
    H["WgT"] = np.ascontiguousarray(
        wgt.reshape(NC, 128, 16).transpose(1, 0, 2).reshape(128, NC * 16))
    W1 = np.asarray(inp["L0_ffn_W1"], f32)
    W3 = np.asarray(inp["L0_ffn_W3"], f32)
    W2 = np.asarray(inp["L0_ffn_W2"], f32)
    W1T = np.empty((NEX, D, I_MOE), f32)
    W3T = np.empty((NEX, D, I_MOE), f32)
    W2T = np.empty((NEX, I_MOE, D), f32)
    for e in range(16):
        W1T[e] = W1[e].T
        W3T[e] = W3[e].T
        W2T[e] = W2[e].T
    Ws1T = np.asarray(inp["L0_ffn_Ws1"], f32).T
    Ws3T = np.asarray(inp["L0_ffn_Ws3"], f32).T
    Ws2T = np.asarray(inp["L0_ffn_Ws2"], f32).T
    W1T[16] = Ws1T[:, 0:I_MOE]
    W1T[17] = Ws1T[:, I_MOE:]
    W3T[16] = Ws3T[:, 0:I_MOE]
    W3T[17] = Ws3T[:, I_MOE:]
    W2T[16] = Ws2T[0:I_MOE, :]
    W2T[17] = Ws2T[I_MOE:, :]
    W1Tn = np.empty((NEX, ICH, 128, D), f32)
    W3Tn = np.empty((NEX, ICH, 128, D), f32)
    W2Tn = np.empty((NEX, 16, 128, ICH * 128), f32)
    for e in range(NEX):
        W1Tn[e] = _to_native(W1T[e], NC, ICH)
        W3Tn[e] = _to_native(W3T[e], NC, ICH)
        W2Tn[e] = _to_native(W2T[e], ICH, NC)
    H["W1T"], H["W3T"], H["W2T"] = W1Tn, W3Tn, W2Tn
    H["identf"] = np.eye(128, dtype=f32)
    H["identb"] = np.eye(128, dtype=BF)
    esel = np.zeros((16, 16 * 128), f32)
    for e in range(16):
        esel[e, e * 128:(e + 1) * 128] = 1.0
    H["esel"] = esel

    H["wq1"] = _to_native(ct(inp["L1_attn_wq"]), NC, 16)
    H["wk1"] = _to_native(ct(inp["L1_attn_wk"]), NC, 4)
    H["wv1"] = _to_native(ct(inp["L1_attn_wv"]), NC, 4)
    H["wo1"] = _to_native(ct(inp["L1_attn_wo"]), 16, NC)
    H["bq1"] = ct(inp["L1_attn_bq"]).reshape(16, 128).T.copy()
    H["bk1"] = ct(inp["L1_attn_bk"]).reshape(4, 128).T.copy()
    H["bv1"] = ct(inp["L1_attn_bv"]).reshape(4, 128).T.copy()
    H["bo1"] = ct(inp["L1_attn_bo"]).reshape(16, 128).T.copy()

    H["wgT1"] = _to_native(np.asarray(inp["L1_ffn_wq_gate"], np.int32).T,
                           NC, L1HCH)
    H["wuT1"] = _to_native(np.asarray(inp["L1_ffn_wq_up"], np.int32).T,
                           NC, L1HCH)
    H["wdT1"] = _to_native(np.asarray(inp["L1_ffn_wq_down"], np.int32).T,
                           L1HCH, NC)
    H["sg1"] = ct(inp["L1_ffn_s_gate"]).reshape(L1HCH, 128).T.copy()
    H["su1"] = ct(inp["L1_ffn_s_up"]).reshape(L1HCH, 128).T.copy()
    H["sd1"] = ct(inp["L1_ffn_s_down"]).reshape(16, 128).T.copy()
    return H


def _prep_core(inp, H, c):
    b, hf = c // 2, c % 2
    lo = hf * T
    per = {}
    x = np.asarray(inp["L0_attn_x"], np.float32)
    per["x"] = np.ascontiguousarray(x[b].T[:, lo:lo + T])
    tk = np.arange(SEQ).reshape(16, 128)
    tq = lo + np.arange(T)
    m = (tk[:, :, None] <= tq[None, None, :]).astype(BF)
    per["pmask"] = np.ascontiguousarray(m.transpose(1, 0, 2).reshape(128, 16 * T))
    return per


_CACHE = {}


def kernel(**inputs):
    H = _prep_host(inputs)
    per_core = [_prep_core(inputs, H, c) for c in range(8)]
    x_cur = [per_core[c]["x"] for c in range(8)]

    for phases in LAUNCH_PLAN:
        if phases not in _CACHE:
            _CACHE[phases] = build_program(phases)
        nc, names, peak, nops = _CACHE[phases][:4]
        in_maps = []
        for c in range(8):
            m = {}
            for n in names:
                if n == "x":
                    m[n] = x_cur[c]
                elif n in per_core[c]:
                    m[n] = per_core[c][n]
                else:
                    m[n] = H[n]
            in_maps.append(m)
        res = run_bass_kernel_spmd(nc, in_maps, core_ids=list(range(8)))
        x_cur = [np.asarray(res.results[c]["xout"]) for c in range(8)]

    out = np.empty((NB, SEQ, D), np.float32)
    for c in range(8):
        b, hf = c // 2, c % 2
        out[b, hf * T:(hf + 1) * T, :] = x_cur[c].T
    return out


# version 06 (final)
# speedup vs baseline: 1.0328x; 1.0190x over previous
import numpy as np
import ml_dtypes
from contextlib import ExitStack

import concourse.bass as bass
import concourse.mybir as mybir
from concourse.bass_utils import run_bass_kernel_spmd

F32 = mybir.dt.float32
BF16 = mybir.dt.bfloat16
I32 = mybir.dt.int32
AF = mybir.ActivationFunctionType
ALU = mybir.AluOpType
AX = mybir.AxisListType
BF = ml_dtypes.bfloat16

D = 2048
NC = 16
T = 1024
SEQ = 2048
DH = 128
NEX = 18
ICH = 11
I_MOE = 1408
L1HCH = 44
L1HID = 5632
NB = 4
EPS_LN = 1e-5
EPS_RMS = 1e-6
SOFTCAP = 50.0
SC_ATT = DH ** -0.5

ENGINES = ("sync", "tensor", "vector", "scalar", "gpsimd")
N_DMA_SEM = 12
GROUPS = [[0, 1], [2, 3], [4, 5], [6, 7]]
LAUNCH_PLAN = [("a0", "f0", "a1", "f1")]
DEBUG_TAP = None
DEBUG_STOP = None


class Region:
    __slots__ = ("space", "lo", "hi")

    def __init__(self, space, lo=0, hi=1 << 62):
        self.space, self.lo, self.hi = space, lo, hi

    def sub(self, lo_b, nb):
        return Region(self.space, self.lo + lo_b, self.lo + lo_b + nb)

    def overlaps(self, o):
        return self.space == o.space and self.lo < o.hi and o.lo < self.hi


class Prog:
    def __init__(self):
        self.ops = {e: [] for e in ENGINES}
        self.counts = {}
        self.writers = []
        self.readers = []
        self.dma_rr = {}
        self.sem_names = set()
        self.final_tokens = []
        self.oplog = []
        self.n_ops = 0

    def _sem(self, name):
        self.sem_names.add(name)
        return name

    def op(self, eng, fn, reads=(), writes=(), dma=False, coll=False, tag=""):
        waits = {}

        def need(tok):
            s, v = tok
            if waits.get(s, 0) < v:
                waits[s] = v

        for r in reads:
            for reg, tok in self.writers:
                if r.overlaps(reg):
                    need(tok)
        for w in writes:
            for reg, tok in self.writers:
                if w.overlaps(reg):
                    need(tok)
            for reg, tok in self.readers:
                if w.overlaps(reg):
                    need(tok)
        if dma:
            i = self.dma_rr.get(eng, 0)
            self.dma_rr[eng] = i + 1
            sem = self._sem(f"dq_{eng}_{i % N_DMA_SEM}")
            inc = 16
        elif coll:
            sem = self._sem("cc")
            inc = 1
        else:
            sem = self._sem(f"tk_{eng}")
            inc = 1
        val = self.counts.get(sem, 0) + inc
        self.counts[sem] = val
        tok = (sem, val)
        self.ops[eng].append((fn, waits, sem, inc, tag))
        self.oplog.append((eng, tag, [(w.space, w.lo, w.hi) for w in writes],
                           [(r.space, r.lo, r.hi) for r in reads]))
        self.n_ops += 1
        def contained(g, w):
            return g.space == w.space and g.lo >= w.lo and g.hi <= w.hi

        for w in writes:
            self.writers = [(g, t) for g, t in self.writers
                            if not contained(g, w)]
            self.readers = [(g, t) for g, t in self.readers
                            if not contained(g, w)]
            self.writers.append((w, tok))
        for r in reads:
            kept = []
            for g, t in self.readers:
                if (g.space == r.space and g.lo == r.lo and g.hi == r.hi
                        and t[0] == tok[0]):
                    continue
                kept.append((g, t))
            kept.append((r, tok))
            self.readers = kept
        return tok

    def emit(self, nc):
        with ExitStack() as st:
            sems = {n: st.enter_context(nc.semaphore(n))
                    for n in sorted(self.sem_names)}
            with nc.Block() as block:
                for ename in ENGINES:
                    ops = self.ops[ename]
                    fin = self.final_tokens if ename == "sync" else []
                    if not ops and not fin:
                        continue

                    def mkbody(ops=ops, fin=fin):
                        def body(eng):
                            water = {}
                            for fn, waits, sem, inc, tag in ops:
                                for s in sorted(waits):
                                    v = waits[s]
                                    if water.get(s, 0) < v:
                                        eng.wait_ge(sems[s], v)
                                        water[s] = v
                                ins = fn(eng)
                                ins.then_inc(sems[sem], inc)
                            for s, v in fin:
                                if water.get(s, 0) < v:
                                    eng.wait_ge(sems[s], v)
                                    water[s] = v
                        return body

                    getattr(block, ename)(mkbody())


class Arena:
    def __init__(self, ap_u8, nbytes):
        self.ap = ap_u8
        self.nbytes = nbytes
        self.off = 0
        self.marks = []
        self.peak = 0

    def tile(self, cols, dtype, parts=128):
        esz = mybir.dt.size(dtype)
        nb = cols * esz
        nb_al = (nb + 31) & ~31
        assert self.off + nb_al <= self.nbytes, \
            f"SBUF arena overflow: {self.off}+{nb_al} > {self.nbytes}"
        lo = self.off
        self.off += nb_al
        self.peak = max(self.peak, self.off)
        ap = self.ap[:, lo:lo + nb].bitcast(dtype)
        if parts != 128:
            ap = ap[0:parts, :]
        return ap, Region("sbuf", lo, lo + nb)

    def push(self):
        self.marks.append(self.off)

    def pop(self):
        self.off = self.marks.pop()


class Rot:
    def __init__(self, A, n, cols, dtype, parts=128):
        self.items = [A.tile(cols, dtype, parts) for _ in range(n)]
        self.i = 0

    def nxt(self):
        it = self.items[self.i % len(self.items)]
        self.i += 1
        return it


def build_program(phases):
    nc = bass.Bass("TRN2", target_bir_lowering=False)
    P = Prog()
    dram = {}

    def din(name, shape, dtype):
        dram[name] = nc.dram_tensor(name, list(shape), dtype,
                                    kind="ExternalInput").ap()
        return dram[name]

    def dreg(name):
        return Region("dram:" + name)

    x_in = din("x", (D, T), F32)
    x_out = nc.dram_tensor("xout", [D, T], F32, kind="ExternalOutput").ap()
    x_spill = nc.dram_tensor("x_spill", [D, T], F32).ap()
    dbg_state = {}

    def debug_tap(tag, ap, R, n_ch, dtype):
        if DEBUG_TAP != tag or "done" in dbg_state:
            return
        dbg_state["done"] = True
        dbg = nc.dram_tensor("dbg", [n_ch * 128, T], dtype,
                             kind="ExternalOutput").ap()
        tok = P.op("sync", lambda e: e.dma_start(
            out=dbg.rearrange("(c p) t -> p c t", p=128),
            in_=ap.rearrange("p (c t) -> p c t", c=n_ch)),
            reads=[R], writes=[Region("dram:dbg")], dma=True)
        P.final_tokens.append(tok)

    need = set(phases)
    if need & {"a0", "a1"}:
        din("pmask", (128, 16 * T), BF16)
        din("identb", (128, 128), BF16)
    if "a0" in need:
        din("wqT0", (16, 128, D), F32)
        din("wkT0", (16, 128, D), F32)
        din("wvT0", (16, 128, D), F32)
        din("woT0", (16, 128, D), F32)
        din("qnw", (1, 128), F32)
        din("knw", (1, 128), F32)
        din("ropeHC", (128, 16), F32)
        din("ropeHS", (128, 16), F32)
    if "f0" in need:
        din("WgT", (128, 256), F32)
        din("identf", (128, 128), F32)
        din("esel", (16, 16 * 128), F32)
        din("W1T", (NEX, ICH, 128, D), F32)
        din("W3T", (NEX, ICH, 128, D), F32)
        din("W2T", (NEX, 16, 128, ICH * 128), F32)
    if "a1" in need:
        din("wq1", (16, 128, D), F32)
        din("wk1", (4, 128, D), F32)
        din("wv1", (4, 128, D), F32)
        din("wo1", (16, 128, D), F32)
        din("bq1", (128, 16), F32)
        din("bk1", (128, 4), F32)
        din("bv1", (128, 4), F32)
        din("bo1", (128, 16), F32)
    if "f1" in need:
        din("wgT1", (L1HCH, 128, D), I32)
        din("wuT1", (L1HCH, 128, D), I32)
        din("wdT1", (16, 128, L1HID), I32)
        din("sg1", (128, L1HCH), F32)
        din("su1", (128, L1HCH), F32)
        din("sd1", (128, 16), F32)

    cc = {}
    if "a0" in need:
        cc["k0i"] = nc.dram_tensor("k0i", [D, T], BF16).ap()
        cc["k0g"] = nc.dram_tensor("k0g", [2 * D, T], BF16).ap()
        cc["v0i"] = nc.dram_tensor("v0i", [T, D], BF16).ap()
        cc["v0g"] = nc.dram_tensor("v0g", [2 * T, D], BF16).ap()
    if "a1" in need:
        cc["k1i"] = nc.dram_tensor("k1i", [512, T], BF16).ap()
        cc["k1g"] = nc.dram_tensor("k1g", [1024, T], BF16).ap()
        cc["v1i"] = nc.dram_tensor("v1i", [T, 512], BF16).ap()
        cc["v1g"] = nc.dram_tensor("v1g", [2 * T, 512], BF16).ap()

    with ExitStack() as st:
        ARENA = 212608
        arena_t = st.enter_context(
            nc.sbuf_tensor("arena", [128, ARENA], mybir.dt.uint8))
        A = Arena(arena_t[:, :], ARENA)
        PBT = st.enter_context(nc.psum_tensor("pball", [128, 4096], F32))
        PBALL = PBT[:, :]
        PB = [PBALL[:, 512 * k:512 * (k + 1)] for k in range(8)]
        PBR = [Region(f"ps{k}") for k in range(8)]
        PBP = [PBALL[:, 1024 * p:1024 * (p + 1)] for p in range(4)]
        PBbf = [PB[k].bitcast(BF16) for k in range(8)]

        ones_col, R_onc = A.tile(1, BF16)
        onesf_row, R_onfr = A.tile(128, F32, parts=1)
        c128f_row, R_c128 = A.tile(128, F32, parts=1)
        rowbuf, R_rows = A.tile(2048, F32, parts=1)

        def rows(off, n):
            return rowbuf[0:1, off:off + n], R_rows.sub(4 * off, 4 * n)

        eps_ln_t, R_epl = A.tile(1, F32, parts=1)
        eps_rms_t, R_epr = A.tile(1, F32, parts=1)
        P.op("gpsimd", lambda e: e.memset(ones_col, 1.0), writes=[R_onc])
        P.op("gpsimd", lambda e: e.memset(onesf_row, 1.0), writes=[R_onfr])
        P.op("gpsimd", lambda e: e.memset(c128f_row, 128.0), writes=[R_c128])
        P.op("gpsimd", lambda e: e.memset(eps_ln_t, EPS_LN), writes=[R_epl])
        P.op("gpsimd", lambda e: e.memset(eps_rms_t, EPS_RMS), writes=[R_epr])

        def ln_stats_bc(chunk_src, n_ch, n_feat, eps, tag):
            bc_r, R_bcr = A.tile(T, F32)
            bc_m, R_bcm = A.tile(T, F32)
            A.push()
            sqr = Rot(A, 3, T, BF16)
            for c in range(n_ch):
                src_ap, src_regs = chunk_src(c)
                sq, R_sq = sqr.nxt()
                P.op("scalar", lambda e, src_ap=src_ap, sq=sq: e.activation(
                    out=sq, in_=src_ap, func=AF.Square),
                    reads=list(src_regs), writes=[R_sq], tag=tag + f".sq{c}")
                for tt in range(2):
                    P.op("tensor", lambda e, src_ap=src_ap, tt=tt, c=c: e.matmul(
                        PB[4 + tt][0:1, :], ones_col,
                        src_ap[:, tt * 512:tt * 512 + 512],
                        start=(c == 0), stop=(c == n_ch - 1)),
                        reads=list(src_regs) + [R_onc], writes=[PBR[4 + tt]],
                        tag=tag + f".s1_{c}{tt}")
                    P.op("tensor", lambda e, sq=sq, tt=tt, c=c: e.matmul(
                        PB[6 + tt][0:1, :], ones_col,
                        sq[:, tt * 512:tt * 512 + 512],
                        start=(c == 0), stop=(c == n_ch - 1)),
                        reads=[R_sq, R_onc], writes=[PBR[6 + tt]],
                        tag=tag + f".s2_{c}{tt}")
            A.pop()
            m_ap, m_reg = rows(0, T)
            v_ap, v_reg = rows(T, T)
            inv = 1.0 / n_feat
            for tt in range(2):
                P.op("vector", lambda e, tt=tt: e.tensor_scalar(
                    m_ap[:, tt * 512:tt * 512 + 512], PB[4 + tt][0:1, :], inv,
                    None, ALU.mult), reads=[PBR[4 + tt]], writes=[m_reg])
                P.op("vector", lambda e, tt=tt: e.tensor_scalar(
                    v_ap[:, tt * 512:tt * 512 + 512], PB[6 + tt][0:1, :], inv,
                    None, ALU.mult), reads=[PBR[6 + tt]], writes=[v_reg])
            P.op("scalar", lambda e: e.activation(
                out=PB[4][0:1, 0:512], in_=m_ap[:, 0:512], func=AF.Square),
                reads=[m_reg], writes=[PBR[4]])
            P.op("scalar", lambda e: e.activation(
                out=PB[5][0:1, 0:512], in_=m_ap[:, 512:1024], func=AF.Square),
                reads=[m_reg], writes=[PBR[5]])
            P.op("vector", lambda e: e.tensor_sub(
                v_ap[:, 0:512], v_ap[:, 0:512], PB[4][0:1, 0:512]),
                reads=[v_reg, PBR[4]], writes=[v_reg])
            P.op("vector", lambda e: e.tensor_sub(
                v_ap[:, 512:1024], v_ap[:, 512:1024], PB[5][0:1, 0:512]),
                reads=[v_reg, PBR[5]], writes=[v_reg])
            P.op("scalar", lambda e: e.activation(out=v_ap, in_=v_ap, func=AF.Ln,
                                                  bias=eps_ln_t),
                 reads=[v_reg, R_epl], writes=[v_reg])
            P.op("scalar", lambda e: e.activation(out=v_ap, in_=v_ap,
                                                  func=AF.Exp, scale=-0.5),
                 reads=[v_reg], writes=[v_reg])
            P.op("vector", lambda e: e.tensor_mul(m_ap, m_ap, v_ap),
                 reads=[m_reg, v_reg], writes=[m_reg])
            for tt in range(2):
                P.op("tensor", lambda e, tt=tt: e.matmul(
                    PB[4 + tt], onesf_row, v_ap[:, tt * 512:tt * 512 + 512],
                    start=True, stop=True),
                    reads=[v_reg, R_onfr], writes=[PBR[4 + tt]])
                P.op("tensor", lambda e, tt=tt: e.matmul(
                    PB[6 + tt], onesf_row, m_ap[:, tt * 512:tt * 512 + 512],
                    start=True, stop=True),
                    reads=[m_reg, R_onfr], writes=[PBR[6 + tt]])
                P.op("scalar", lambda e, tt=tt: e.copy(
                    bc_r[:, tt * 512:tt * 512 + 512], PB[4 + tt]),
                    reads=[PBR[4 + tt]], writes=[R_bcr])
                P.op("scalar", lambda e, tt=tt: e.copy(
                    bc_m[:, tt * 512:tt * 512 + 512], PB[6 + tt]),
                    reads=[PBR[6 + tt]], writes=[R_bcm])
            return bc_r, R_bcr, bc_m, R_bcm

        def emit_preln(src_ap, src_name, xn_bf, R_xn, tag, f32_cb=None):
            A.push()
            xcr = Rot(A, 5, T, F32)
            xbr = Rot(A, 3, T, BF16)

            def load_chunk(c, xc, R_xc):
                P.op("sync", lambda e, c=c, xc=xc: e.dma_start(
                    out=xc, in_=src_ap[c * 128:(c + 1) * 128, :]),
                    reads=[Region("dram:" + src_name, c * 128 * T * 4,
                                  (c + 1) * 128 * T * 4)],
                    writes=[R_xc], dma=True, tag=tag + f".ldx{c}")

            chunk_cache = {}

            def chunk_src(c):
                xc, R_xc = xcr.nxt()
                load_chunk(c, xc, R_xc)
                xb, R_xb = xbr.nxt()
                P.op("vector", lambda e, xb=xb, xc=xc: e.tensor_copy(xb, xc),
                     reads=[R_xc], writes=[R_xb])
                return xb, [R_xb]

            bc_r, R_bcr, bc_m, R_bcm = ln_stats_bc(chunk_src, NC, D, EPS_LN, tag)
            tmp = Rot(A, 2, T, F32)
            for c in range(NC):
                xc, R_xc = xcr.nxt()
                load_chunk(c, xc, R_xc)
                tf, R_tf = tmp.nxt()
                R_nc = R_xn.sub(2 * c * T, 2 * T)
                P.op("vector", lambda e, xc=xc, tf=tf: e.tensor_mul(
                    tf, xc, bc_r), reads=[R_xc, R_bcr], writes=[R_tf])
                if f32_cb is None:
                    P.op("vector", lambda e, c=c, tf=tf: e.tensor_sub(
                        xn_bf[:, c * T:(c + 1) * T], tf, bc_m),
                        reads=[R_tf, R_bcm], writes=[R_nc])
                else:
                    P.op("vector", lambda e, tf=tf: e.tensor_sub(tf, tf, bc_m),
                         reads=[R_tf, R_bcm], writes=[R_tf])
                    P.op("gpsimd", lambda e, c=c, tf=tf: e.tensor_copy(
                        xn_bf[:, c * T:(c + 1) * T], tf),
                        reads=[R_tf], writes=[R_nc])
                    f32_cb(c, tf, R_tf)
            A.pop()

        def emit_postres(src_ap, src_name, dst_ap, dst_name, out_bf, R_out,
                         tag):
            A.push()

            def chunk_src(c):
                return (out_bf[:, c * T:(c + 1) * T],
                        [R_out.sub(2 * c * T, 2 * T)])

            bc_r, R_bcr, bc_m, R_bcm = ln_stats_bc(chunk_src, NC, D, EPS_LN,
                                                   tag)
            xcr = Rot(A, 3, T, F32)
            xnr = Rot(A, 3, T, F32)
            tmp = Rot(A, 2, T, F32)
            store_toks = []
            for c in range(NC):
                xc, R_xc = xcr.nxt()
                P.op("sync", lambda e, c=c, xc=xc: e.dma_start(
                    out=xc, in_=src_ap[c * 128:(c + 1) * 128, :]),
                    reads=[Region("dram:" + src_name, c * 128 * T * 4,
                                  (c + 1) * 128 * T * 4)],
                    writes=[R_xc], dma=True)
                tf, R_tf = tmp.nxt()
                P.op("vector", lambda e, c=c, tf=tf: e.tensor_mul(
                    tf, out_bf[:, c * T:(c + 1) * T], bc_r),
                    reads=[R_out.sub(2 * c * T, 2 * T), R_bcr], writes=[R_tf])
                P.op("vector", lambda e, tf=tf: e.tensor_sub(tf, tf, bc_m),
                     reads=[R_tf, R_bcm], writes=[R_tf])
                xo, R_xo = xnr.nxt()
                P.op("vector", lambda e, xo=xo, xc=xc, tf=tf: e.tensor_add(
                    xo, xc, tf), reads=[R_xc, R_tf], writes=[R_xo])
                tok = P.op("sync", lambda e, c=c, xo=xo: e.dma_start(
                    out=dst_ap[c * 128:(c + 1) * 128, :], in_=xo),
                    reads=[R_xo], writes=[dreg(dst_name) if False else
                                          Region("dram:" + dst_name,
                                                 c * 128 * T * 4,
                                                 (c + 1) * 128 * T * 4)],
                    dma=True)
                store_toks.append(tok)
            A.pop()
            return store_toks

        def emit_proj(w_hbm, wname, n_out_ch, src_bf, R_src, out_cb, tag,
                      in_ch=NC, nbuf=3):
            A.push()
            wt = Rot(A, nbuf, in_ch * 128, BF16)
            for o in range(n_out_ch):
                w, R_w = wt.nxt()

                def ldw(e, o=o, w=w):
                    return e.dma_start(out=w, in_=w_hbm[o])
                P.op("gpsimd", ldw, reads=[dreg(wname)], writes=[R_w], dma=True,
                     tag=tag + f".w{o}")
                p = o % 2
                for tt in range(2):
                    bank = p * 2 + tt

                    def mm(e, tt=tt, w=w, bank=bank):
                        ins = None
                        for c in range(in_ch):
                            ins = e.matmul(
                                PB[bank], w[:, c * 128:(c + 1) * 128],
                                src_bf[:, c * T + tt * 512: c * T + tt * 512 + 512],
                                start=(c == 0), stop=(c == in_ch - 1))
                        return ins
                    P.op("tensor", mm, reads=[R_w, R_src], writes=[PBR[bank]],
                         tag=tag + f".mm{o}_{tt}")
                out_cb(o, PBP[p], [PBR[p * 2], PBR[p * 2 + 1]])
            A.pop()

        def phase_attn(layer, first, last):
            is0 = (layer == "a0")
            kvh_n = 16 if is0 else 4
            kv_dim = kvh_n * 128
            kgname, vgname = ("k0g", "v0g") if is0 else ("k1g", "v1g")
            kiname, viname = ("k0i", "v0i") if is0 else ("k1i", "v1i")
            src_name = "x" if first else "x_spill"
            src_ap = x_in if first else x_spill
            dst_name = "xout" if last else "x_spill"
            dst_ap = x_out if last else x_spill

            A.push()
            q_bf, R_q = A.tile(16 * T, BF16)
            ao_bf, R_ao = A.tile(16 * T, BF16)
            idb_sb, R_idb = A.tile(128, BF16)
            P.op("sync", lambda e: e.dma_start(out=idb_sb, in_=dram["identb"]),
                 reads=[dreg("identb")], writes=[R_idb], dma=True)
            if is0:
                ropeC_sb, R_rc = A.tile(16, F32)
                ropeS_sb, R_rs = A.tile(16, F32)
                qnw_sb, R_qnw = A.tile(128, F32, parts=1)
                knw_sb, R_knw = A.tile(128, F32, parts=1)
                for nm, ap_, rg in (("ropeHC", ropeC_sb, R_rc),
                                    ("ropeHS", ropeS_sb, R_rs),
                                    ("qnw", qnw_sb, R_qnw),
                                    ("knw", knw_sb, R_knw)):
                    P.op("sync", lambda e, nm=nm, ap_=ap_: e.dma_start(
                        out=ap_, in_=dram[nm]),
                        reads=[dreg(nm)], writes=[rg], dma=True)
            else:
                bq_sb, R_bq = A.tile(16, F32)
                bk_sb, R_bk = A.tile(4, F32)
                bv_sb, R_bv = A.tile(4, F32)
                bo_sb, R_bo = A.tile(16, F32)
                for nm, ap_, rg in (("bq1", bq_sb, R_bq), ("bk1", bk_sb, R_bk),
                                    ("bv1", bv_sb, R_bv), ("bo1", bo_sb, R_bo)):
                    P.op("sync", lambda e, nm=nm, ap_=ap_: e.dma_start(
                        out=ap_, in_=dram[nm]),
                        reads=[dreg(nm)], writes=[rg], dma=True)

            A.push()
            xn_bf, R_xn = A.tile(NC * T, BF16)
            emit_preln(src_ap, src_name, xn_bf, R_xn, layer + ".ln")

            praw = Rot(A, 2, T, F32)
            sqr = Rot(A, 2, T, BF16)
            bctr0 = Rot(A, 2, T, F32)
            qnr = Rot(A, 2, T, F32)
            t0r = Rot(A, 2, T, F32)
            khead = Rot(A, 2, T, BF16)
            vhead = Rot(A, 2, T, BF16)
            vtmp = Rot(A, 2, T, BF16)

            def stage_k(o, kh, R_kh):
                P.op("sync", lambda e, o=o, kh=kh: e.dma_start(
                    out=cc[kiname][o * 128:(o + 1) * 128, :], in_=kh),
                    reads=[R_kh],
                    writes=[Region("dram:" + kiname, o * 128 * T * 2,
                                   (o + 1) * 128 * T * 2)], dma=True)

            def stage_v(o, vh, R_vh):
                P.op("sync", lambda e, o=o, vh=vh: e.dma_start(
                    out=cc[viname][:, o * 128:(o + 1) * 128].rearrange(
                        "(tc p) d -> p tc d", p=128),
                    in_=vh.rearrange("p (tc d) -> p tc d", d=128)),
                    reads=[R_vh],
                    writes=[Region("dram:" + viname, o, o + 1)], dma=True)

            def qk_post_l0(sink, nw_sb, R_nw):
                def cb(o, pp, pregs):
                    pr, R_pr = praw.nxt()
                    P.op("scalar", lambda e, pr=pr, pp=pp: e.copy(pr, pp),
                         reads=pregs, writes=[R_pr])
                    sqb, R_sqb = sqr.nxt()
                    P.op("scalar", lambda e, sqb=sqb, pr=pr: e.activation(
                        out=sqb, in_=pr, func=AF.Square),
                        reads=[R_pr], writes=[R_sqb])
                    roff = 1024 * (o % 2)
                    rs_ap, rs_reg = rows(roff, 1024)
                    for tt in range(2):
                        P.op("tensor", lambda e, sqb=sqb, tt=tt: e.matmul(
                            PB[4 + tt][0:1, :], ones_col,
                            sqb[:, tt * 512:tt * 512 + 512],
                            start=True, stop=True),
                            reads=[R_sqb, R_onc], writes=[PBR[4 + tt]])
                        P.op("scalar", lambda e, rs_ap=rs_ap, tt=tt:
                             e.activation(out=rs_ap[:, tt * 512:tt * 512 + 512],
                                          in_=PB[4 + tt][0:1, :], func=AF.Ln,
                                          scale=1.0 / DH, bias=eps_rms_t),
                             reads=[PBR[4 + tt], R_epr], writes=[rs_reg])
                    P.op("scalar", lambda e, rs_ap=rs_ap: e.activation(
                        out=rs_ap, in_=rs_ap, func=AF.Exp, scale=-0.5),
                        reads=[rs_reg], writes=[rs_reg])
                    for tt in range(2):
                        P.op("tensor", lambda e, rs_ap=rs_ap, tt=tt: e.matmul(
                            PB[4 + tt], nw_sb,
                            rs_ap[:, tt * 512:tt * 512 + 512],
                            start=True, stop=True),
                            reads=[rs_reg, R_nw], writes=[PBR[4 + tt]])
                    bct, R_bct = bctr0.nxt()
                    P.op("scalar", lambda e, bct=bct: e.copy(bct, PBP[2]),
                         reads=[PBR[4], PBR[5]], writes=[R_bct])
                    qn, R_qn = qnr.nxt()
                    P.op("vector", lambda e, qn=qn, pr=pr, bct=bct:
                         e.tensor_mul(qn, pr, bct),
                         reads=[R_pr, R_bct], writes=[R_qn])
                    t0, R_t0 = t0r.nxt()
                    P.op("vector", lambda e, t0=t0, qn=qn, o=o:
                         e.tensor_scalar(t0, qn, ropeC_sb[:, o:o + 1], None,
                                         ALU.mult),
                         reads=[R_qn, R_rc], writes=[R_t0])
                    P.op("vector", lambda e, qn=qn, o=o: e.tensor_scalar(
                        PBP[2], qn, ropeS_sb[:, o:o + 1], None, ALU.mult),
                        reads=[R_qn, R_rs], writes=[PBR[4], PBR[5]])
                    d_sl, R_dsl = sink(o)
                    P.op("vector", lambda e, d_sl=d_sl, t0=t0: e.tensor_sub(
                        d_sl[0:64, :], t0[0:64, :], PBP[2][64:128, :]),
                        reads=[R_t0, PBR[4], PBR[5]], writes=[R_dsl])
                    P.op("vector", lambda e, d_sl=d_sl, t0=t0: e.tensor_add(
                        d_sl[64:128, :], t0[64:128, :], PBP[2][0:64, :]),
                        reads=[R_t0, PBR[4], PBR[5]], writes=[R_dsl])
                return cb

            kstate = {}

            def k_sink(o):
                kstate["t"] = khead.nxt()
                return kstate["t"]

            def k_post_wrap(cb):
                def cb2(o, pp, pregs):
                    cb(o, pp, pregs)
                    kh, R_kh = kstate["t"]
                    stage_k(o, kh, R_kh)
                return cb2

            def bias_sink_cb(sink, b_sb, R_b, stage_fn=None):
                def cb(o, pp, pregs):
                    d_sl, R_dsl = sink(o)
                    if b_sb is None:
                        P.op("scalar", lambda e, d_sl=d_sl, pp=pp: e.copy(
                            d_sl, pp), reads=pregs, writes=[R_dsl])
                    else:
                        P.op("vector", lambda e, d_sl=d_sl, pp=pp, o=o:
                             e.tensor_scalar(d_sl, pp, b_sb[:, o:o + 1],
                                             None, ALU.add),
                             reads=pregs + [R_b], writes=[R_dsl])
                    if stage_fn is not None:
                        stage_fn(o)
                return cb

            def v_evac_cb(o, pp, pregs):
                vt, R_vtt = vtmp.nxt()
                if is0:
                    P.op("scalar", lambda e, vt=vt, pp=pp: e.copy(vt, pp),
                         reads=pregs, writes=[R_vtt])
                else:
                    P.op("vector", lambda e, vt=vt, pp=pp, o=o:
                         e.tensor_scalar(vt, pp, bv_sb[:, o:o + 1], None,
                                         ALU.add),
                         reads=pregs + [R_bv], writes=[R_vtt])
                vh, R_vh = vhead.nxt()
                for tc in range(8):
                    bank2 = 4 + tc % 2
                    P.op("tensor", lambda e, vt=vt, tc=tc, bank2=bank2:
                         e.transpose(PBbf[bank2][:, 0:128],
                                     vt[:, tc * 128:tc * 128 + 128], idb_sb),
                         reads=[R_vtt, R_idb], writes=[PBR[bank2]])
                    P.op("vector", lambda e, vh=vh, tc=tc, bank2=bank2:
                         e.tensor_copy(vh[:, tc * 128:tc * 128 + 128],
                                       PBbf[bank2][:, 0:128]),
                         reads=[PBR[bank2]], writes=[R_vh])
                stage_v(o, vh, R_vh)

            if is0:
                emit_proj(dram["wvT0"], "wvT0", kvh_n, xn_bf, R_xn, v_evac_cb,
                          layer + ".v")
            else:
                emit_proj(dram["wv1"], "wv1", kvh_n, xn_bf, R_xn, v_evac_cb,
                          layer + ".v")

            P.op("gpsimd", lambda e: e.collective_compute(
                "AllGather", ALU.bypass, replica_groups=GROUPS,
                ins=[cc[viname]], outs=[cc[vgname]]),
                reads=[Region("dram:" + viname)], writes=[dreg(vgname)],
                coll=True)

            if is0:
                emit_proj(dram["wkT0"], "wkT0", kvh_n, xn_bf, R_xn,
                          k_post_wrap(qk_post_l0(k_sink, knw_sb, R_knw)),
                          layer + ".k")
            else:
                emit_proj(dram["wk1"], "wk1", kvh_n, xn_bf, R_xn,
                          bias_sink_cb(k_sink, bk_sb, R_bk,
                                       lambda o: stage_k(*((o,) + kstate["t"]))),
                          layer + ".k")

            P.op("gpsimd", lambda e: e.collective_compute(
                "AllGather", ALU.bypass, replica_groups=GROUPS,
                ins=[cc[kiname]], outs=[cc[kgname]]),
                reads=[Region("dram:" + kiname)], writes=[dreg(kgname)],
                coll=True)

            def q_sink(o):
                sl = q_bf[:, o * T:(o + 1) * T]
                return sl, R_q.sub(2 * o * T, 2 * T)

            if is0:
                emit_proj(dram["wqT0"], "wqT0", 16, xn_bf, R_xn,
                          qk_post_l0(q_sink, qnw_sb, R_qnw), layer + ".q")
            else:
                emit_proj(dram["wq1"], "wq1", 16, xn_bf, R_xn,
                          bias_sink_cb(q_sink, bq_sb, R_bq), layer + ".q")
            A.pop()

            A.push()
            mask_sb, R_mask = A.tile(16 * T, BF16)
            P.op("sync", lambda e: e.dma_start(out=mask_sb, in_=dram["pmask"]),
                reads=[dreg("pmask")], writes=[R_mask], dma=True)
            GH = 4
            kbuf = Rot(A, 2, GH * SEQ, BF16)
            vbuf = Rot(A, 2, GH * SEQ, BF16)
            ptr = Rot(A, 6, T, BF16)
            utr = Rot(A, 3, T, F32)
            bctr = Rot(A, 2, T, F32)

            def load_group(g):
                kt, R_kt = kbuf.nxt()
                vt, R_vt2 = vbuf.nxt()
                for hl in range(GH):
                    h = g * GH + hl
                    kvh = h * kvh_n // 16
                    for r in range(2):
                        P.op("sync", lambda e, hl=hl, r=r, kvh=kvh, kt=kt:
                             e.dma_start(
                                 out=kt[:, hl * SEQ + r * T: hl * SEQ + r * T + T],
                                 in_=cc[kgname][r * kv_dim + 128 * kvh:
                                                r * kv_dim + 128 * kvh + 128, :]),
                             reads=[dreg(kgname)], writes=[R_kt], dma=True)
                        P.op("sync", lambda e, hl=hl, r=r, kvh=kvh, vt=vt:
                             e.dma_start(
                                 out=vt[:, hl * SEQ + r * T: hl * SEQ + r * T + T]
                                 .rearrange("p (tc d) -> p tc d", d=128),
                                 in_=cc[vgname][r * T:(r + 1) * T,
                                                128 * kvh:128 * kvh + 128]
                                 .rearrange("(tc p) d -> p tc d", p=128)),
                             reads=[dreg(vgname)], writes=[R_vt2], dma=True)
                return kt, R_kt, vt, R_vt2

            cur = load_group(0)
            for g in range(4):
                kt, R_kt, vt, R_vt2 = cur
                nxt = load_group(g + 1) if g < 3 else None
                for hl in range(GH):
                    h = g * GH + hl
                    ptiles = {}

                    def emit_sums_O(j):
                        pt, R_pt = ptiles.pop(j)
                        for tt in range(2):
                            P.op("tensor", lambda e, pt=pt, j=j, tt=tt:
                                 e.matmul(PB[4 + tt][0:1, :], ones_col,
                                          pt[:, tt * 512:tt * 512 + 512],
                                          start=(j == 0), stop=(j == 15)),
                                 reads=[R_pt, R_onc], writes=[PBR[4 + tt]])
                            vsl = vt[:, hl * SEQ + j * 128:
                                     hl * SEQ + j * 128 + 128]
                            P.op("tensor", lambda e, vsl=vsl, pt=pt, j=j, tt=tt:
                                 e.matmul(PB[6 + tt], vsl,
                                          pt[:, tt * 512:tt * 512 + 512],
                                          start=(j == 0), stop=(j == 15)),
                                 reads=[R_vt2, R_pt], writes=[PBR[6 + tt]])

                    R_qsl = R_q.sub(2 * h * T, 2 * T)
                    for j in range(16):
                        sp = j % 2
                        ksl = kt[:, hl * SEQ + j * 128: hl * SEQ + j * 128 + 128]
                        for tt in range(2):
                            qsl = q_bf[:, h * T + tt * 512: h * T + tt * 512 + 512]
                            P.op("tensor", lambda e, ksl=ksl, qsl=qsl, sp=sp,
                                 tt=tt: e.matmul(PB[sp * 2 + tt], ksl, qsl,
                                                 start=True, stop=True),
                                 reads=[R_kt, R_qsl], writes=[PBR[sp * 2 + tt]])
                        pt, R_pt = ptr.nxt()
                        ptiles[j] = (pt, R_pt)
                        if is0:
                            ut, R_ut = utr.nxt()
                            P.op("scalar", lambda e, sp=sp, ut=ut: e.activation(
                                out=ut, in_=PBP[sp], func=AF.Tanh,
                                scale=SC_ATT / SOFTCAP),
                                reads=[PBR[sp * 2], PBR[sp * 2 + 1]],
                                writes=[R_ut])
                            P.op("scalar", lambda e, ut=ut, pt=pt: e.activation(
                                out=pt, in_=ut, func=AF.Exp, scale=SOFTCAP),
                                reads=[R_ut], writes=[R_pt])
                        else:
                            P.op("scalar", lambda e, sp=sp, pt=pt: e.activation(
                                out=pt, in_=PBP[sp], func=AF.Exp, scale=SC_ATT),
                                reads=[PBR[sp * 2], PBR[sp * 2 + 1]],
                                writes=[R_pt])
                        msl = mask_sb[:, j * T:(j + 1) * T]
                        P.op("vector", lambda e, pt=pt, msl=msl: e.tensor_mul(
                            pt, pt, msl), reads=[R_mask, R_pt], writes=[R_pt])
                        if j >= 3:
                            emit_sums_O(j - 3)
                    emit_sums_O(13)
                    emit_sums_O(14)
                    emit_sums_O(15)
                    roff = T + 0
                    inv_ap, inv_reg = rows(roff, 1024)
                    for tt in range(2):
                        P.op("vector", lambda e, tt=tt, inv_ap=inv_ap:
                             e.reciprocal(inv_ap[:, tt * 512:tt * 512 + 512],
                                          PB[4 + tt][0:1, :]),
                             reads=[PBR[4 + tt]], writes=[inv_reg])
                        P.op("tensor", lambda e, tt=tt, inv_ap=inv_ap: e.matmul(
                            PB[4 + tt], onesf_row,
                            inv_ap[:, tt * 512:tt * 512 + 512],
                            start=True, stop=True),
                            reads=[inv_reg, R_onfr], writes=[PBR[4 + tt]])
                    bct, R_bct = bctr.nxt()
                    P.op("scalar", lambda e, bct=bct: e.copy(bct, PBP[2]),
                         reads=[PBR[4], PBR[5]], writes=[R_bct])
                    osl = ao_bf[:, h * T:(h + 1) * T]
                    R_osl = R_ao.sub(2 * h * T, 2 * T)
                    P.op("vector", lambda e, osl=osl, bct=bct: e.tensor_mul(
                        osl, PBP[3], bct),
                        reads=[PBR[6], PBR[7], R_bct], writes=[R_osl])
                cur = nxt
            A.pop()

            aoo_bf, R_aoo = A.tile(NC * T, BF16)

            def o_sink(o):
                sl = aoo_bf[:, o * T:(o + 1) * T]
                return sl, R_aoo.sub(2 * o * T, 2 * T)

            if is0:
                emit_proj(dram["woT0"], "woT0", NC, ao_bf, R_ao,
                          bias_sink_cb(o_sink, None, None), layer + ".o",
                          in_ch=16)
            else:
                emit_proj(dram["wo1"], "wo1", NC, ao_bf, R_ao,
                          bias_sink_cb(o_sink, bo_sb, R_bo), layer + ".o",
                          in_ch=16)
            emit_postres(src_ap, src_name, dst_ap, dst_name, aoo_bf, R_aoo,
                         layer + ".post")
            A.pop()

        def phase_f0(first, last):
            src_name = "x" if first else "x_spill"
            src_ap = x_in if first else x_spill
            dst_name = "xout" if last else "x_spill"
            dst_ap = x_out if last else x_spill

            A.push()
            y_sb, R_y = A.tile(NC * T, F32)
            A.push()
            hp_bf, R_hp = A.tile(ICH * T, BF16)
            xn_bf, R_xn = A.tile(NC * T, BF16)
            wg_sb, R_wg = A.tile(16 * 16, F32)
            idf_t, R_idft = A.tile(128, F32)
            P.op("sync", lambda e: e.dma_start(out=wg_sb, in_=dram["WgT"]),
                reads=[dreg("WgT")], writes=[R_wg], dma=True)
            P.op("sync", lambda e: e.dma_start(out=idf_t, in_=dram["identf"]),
                 reads=[dreg("identf")], writes=[R_idft], dma=True)

            def router_cb(c, tf, R_tf):
                for tt in range(2):
                    P.op("tensor", lambda e, c=c, tf=tf, tt=tt: e.matmul(
                        PB[2 + tt][0:16, :], wg_sb[:, c * 16:(c + 1) * 16],
                        tf[:, tt * 512:tt * 512 + 512],
                        start=(c == 0), stop=(c == NC - 1)),
                        reads=[R_tf, R_wg], writes=[PBR[2 + tt]])

            emit_preln(src_ap, src_name, xn_bf, R_xn, "f0.ln",
                       f32_cb=router_cb)

            lg_sb, R_lg = A.tile(T, F32, parts=16)
            for tt in range(2):
                P.op("scalar", lambda e, tt=tt: e.copy(
                    lg_sb[:, tt * 512:tt * 512 + 512], PB[2 + tt][0:16, :]),
                    reads=[PBR[2 + tt]], writes=[R_lg])
            ltm, R_ltm = A.tile(8 * 16, F32)
            for tc in range(8):
                bank = tc % 2
                P.op("tensor", lambda e, tc=tc, bank=bank: e.transpose(
                    PB[bank][:, 0:16], lg_sb[:, tc * 128:(tc + 1) * 128],
                    idf_t[0:16, 0:16]),
                    reads=[R_lg, R_idft], writes=[PBR[bank]])
                P.op("vector", lambda e, tc=tc, bank=bank: e.tensor_copy(
                    ltm[:, tc * 16:(tc + 1) * 16], PB[bank][:, 0:16]),
                    reads=[PBR[bank]], writes=[R_ltm])

            cmb_tm, R_cmb = A.tile(8 * 16, F32)
            sc_t, R_sct = A.tile(16, F32)
            gs_t, R_gst = A.tile(4, F32)
            gt_t, R_gt = A.tile(16, F32)
            r1_t, R_r1 = A.tile(1, F32)
            r2_t, R_r2 = A.tile(1, F32)
            ex_t, R_ex = A.tile(16, F32)
            for tc in range(8):
                lt = ltm[:, tc * 16:(tc + 1) * 16]
                R_lt = R_ltm.sub(4 * tc * 16, 4 * 16)
                P.op("scalar", lambda e, lt=lt: e.activation(
                    out=ex_t, in_=lt, func=AF.Exp),
                    reads=[R_lt], writes=[R_ex])
                P.op("vector", lambda e: e.reduce_sum(r1_t, ex_t, axis=AX.X),
                     reads=[R_ex], writes=[R_r1])
                P.op("vector", lambda e: e.reciprocal(r2_t, r1_t),
                     reads=[R_r1], writes=[R_r2])
                P.op("vector", lambda e: e.tensor_scalar(
                    sc_t, ex_t, r2_t, None, ALU.mult),
                    reads=[R_ex, R_r2], writes=[R_sct])
                for gg in range(4):
                    P.op("vector", lambda e, gg=gg: e.reduce_max(
                        gs_t[:, gg:gg + 1], sc_t[:, gg * 4:(gg + 1) * 4],
                        axis=AX.X), reads=[R_sct], writes=[R_gst])
                for gg in range(4):
                    P.op("vector", lambda e, gg=gg: e.tensor_scalar(
                        gt_t[:, 0:4], gs_t, gs_t[:, gg:gg + 1], None,
                        ALU.is_gt), reads=[R_gst], writes=[R_gt])
                    P.op("vector", lambda e: e.reduce_sum(
                        r1_t, gt_t[:, 0:4], axis=AX.X),
                        reads=[R_gt], writes=[R_r1])
                    P.op("vector", lambda e: e.tensor_scalar(
                        r1_t, r1_t, 1.5, None, ALU.is_le),
                        reads=[R_r1], writes=[R_r1])
                    P.op("vector", lambda e: e.tensor_scalar(
                        r2_t, r1_t, 1e9, -1e9, ALU.mult, ALU.add),
                        reads=[R_r1], writes=[R_r2])
                    P.op("vector", lambda e, gg=gg: e.tensor_scalar(
                        ex_t[:, gg * 4:(gg + 1) * 4],
                        sc_t[:, gg * 4:(gg + 1) * 4], r1_t, r2_t,
                        ALU.mult, ALU.add),
                        reads=[R_sct, R_r1, R_r2], writes=[R_ex])
                for ee in range(16):
                    P.op("vector", lambda e, ee=ee: e.tensor_scalar(
                        gt_t, ex_t, ex_t[:, ee:ee + 1], None, ALU.is_gt),
                        reads=[R_ex], writes=[R_gt])
                    P.op("vector", lambda e: e.reduce_sum(
                        r1_t, gt_t, axis=AX.X), reads=[R_gt], writes=[R_r1])
                    P.op("vector", lambda e: e.tensor_scalar(
                        r1_t, r1_t, 3.5, None, ALU.is_le),
                        reads=[R_r1], writes=[R_r1])
                    dsl = cmb_tm[:, tc * 16 + ee: tc * 16 + ee + 1]
                    R_dsl = R_cmb.sub(4 * (tc * 16 + ee), 4)
                    P.op("vector", lambda e, dsl=dsl, ee=ee: e.tensor_mul(
                        dsl, sc_t[:, ee:ee + 1], r1_t),
                        reads=[R_sct, R_r1], writes=[R_dsl])
            cmb_fm, R_cfm = A.tile(T, F32, parts=16)
            esel_sb, R_esel = A.tile(16 * 128, F32, parts=16)

            def emit_cmb_fm():
                P.op("sync", lambda e: e.dma_start(out=esel_sb,
                                                   in_=dram["esel"]),
                     reads=[dreg("esel")], writes=[R_esel], dma=True)
                for tc in range(8):
                    bank = tc % 2
                    P.op("tensor", lambda e, tc=tc, bank=bank: e.transpose(
                        PB[bank][0:16, 0:128],
                        cmb_tm[:, tc * 16:(tc + 1) * 16], idf_t),
                        reads=[R_cmb, R_idft], writes=[PBR[bank]])
                    P.op("vector", lambda e, tc=tc, bank=bank: e.tensor_copy(
                        cmb_fm[:, tc * 128:(tc + 1) * 128],
                        PB[bank][0:16, 0:128]),
                        reads=[PBR[bank]], writes=[R_cfm])

            w13 = Rot(A, 4, NC * 128, BF16)
            w2r = Rot(A, 2, ICH * 128, BF16)
            bce, R_bce = A.tile(T, F32)
            sar = Rot(A, 2, 512, F32)
            tbr = Rot(A, 2, 512, F32)

            for ei, ex in enumerate([16] + list(range(16)) + [17]):
                first_ex = (ei == 0)
                if ei == 1:
                    emit_cmb_fm()
                routed = ex < 16
                if routed:
                    for tt in range(2):
                        P.op("tensor", lambda e, ex=ex, tt=tt: e.matmul(
                            PB[6 + tt], esel_sb[:, ex * 128:(ex + 1) * 128],
                            cmb_fm[:, tt * 512:tt * 512 + 512],
                            start=True, stop=True),
                            reads=[R_cfm, R_esel], writes=[PBR[6 + tt]])
                        P.op("scalar", lambda e, tt=tt: e.copy(
                            bce[:, tt * 512:tt * 512 + 512], PB[6 + tt]),
                            reads=[PBR[6 + tt]], writes=[R_bce])
                for ic in range(ICH):
                    bs = 0 if ic % 2 == 0 else 4
                    w1, R_w1 = w13.nxt()
                    w3, R_w3 = w13.nxt()
                    P.op("gpsimd", lambda e, ex=ex, ic=ic, w1=w1: e.dma_start(
                        out=w1, in_=dram["W1T"][ex, ic]),
                        reads=[dreg("W1T")], writes=[R_w1], dma=True)
                    P.op("gpsimd", lambda e, ex=ex, ic=ic, w3=w3: e.dma_start(
                        out=w3, in_=dram["W3T"][ex, ic]),
                        reads=[dreg("W3T")], writes=[R_w3], dma=True)
                    for tt in range(2):
                        for mi, (w, R_w) in enumerate(((w1, R_w1), (w3, R_w3))):
                            bank = bs + mi * 2 + tt

                            def mm(e, w=w, tt=tt, bank=bank):
                                ins = None
                                for c in range(NC):
                                    ins = e.matmul(
                                        PB[bank], w[:, c * 128:(c + 1) * 128],
                                        xn_bf[:, c * T + tt * 512:
                                              c * T + tt * 512 + 512],
                                        start=(c == 0), stop=(c == NC - 1))
                                return ins
                            P.op("tensor", mm, reads=[R_w, R_xn],
                                 writes=[PBR[bank]],
                                 tag=f"f0.e{ex}i{ic}t{tt}m{mi}")
                        sa, R_sa = sar.nxt()
                        P.op("scalar", lambda e, sa=sa, bs=bs, tt=tt:
                             e.activation(out=sa, in_=PB[bs + tt],
                                          func=AF.Silu),
                             reads=[PBR[bs + tt]], writes=[R_sa])
                        hsl = hp_bf[:, ic * T + tt * 512: ic * T + tt * 512 + 512]
                        R_hsl = R_hp.sub(2 * (ic * T + tt * 512), 2 * 512)
                        if routed:
                            tb, R_tb = tbr.nxt()
                            P.op("vector", lambda e, tb=tb, bs=bs, tt=tt:
                                 e.tensor_mul(tb, PB[bs + 2 + tt],
                                              bce[:, tt * 512:tt * 512 + 512]),
                                 reads=[PBR[bs + 2 + tt], R_bce],
                                 writes=[R_tb])
                            P.op("vector", lambda e, hsl=hsl, tb=tb, sa=sa:
                                 e.tensor_mul(hsl, tb, sa),
                                 reads=[R_tb, R_sa], writes=[R_hsl])
                        else:
                            P.op("vector", lambda e, hsl=hsl, bs=bs, tt=tt,
                                 sa=sa: e.tensor_mul(hsl, PB[bs + 2 + tt], sa),
                                 reads=[PBR[bs + 2 + tt], R_sa],
                                 writes=[R_hsl])
                for o in range(NC):
                    w2, R_w2 = w2r.nxt()
                    P.op("gpsimd", lambda e, ex=ex, o=o, w2=w2: e.dma_start(
                        out=w2, in_=dram["W2T"][ex, o]),
                        reads=[dreg("W2T")], writes=[R_w2], dma=True)
                    for tt in range(2):
                        bank = (o * 2 + tt) % 4

                        def mm2(e, w2=w2, tt=tt, bank=bank):
                            ins = None
                            for ci in range(ICH):
                                ins = e.matmul(
                                    PB[bank], w2[:, ci * 128:(ci + 1) * 128],
                                    hp_bf[:, ci * T + tt * 512:
                                          ci * T + tt * 512 + 512],
                                    start=(ci == 0), stop=(ci == ICH - 1))
                            return ins
                        P.op("tensor", mm2, reads=[R_w2, R_hp],
                             writes=[PBR[bank]], tag=f"f0.w2e{ex}o{o}t{tt}")
                        ysl = y_sb[:, o * T + tt * 512: o * T + tt * 512 + 512]
                        R_ysl = R_y.sub(4 * (o * T + tt * 512), 4 * 512)
                        if first_ex:
                            P.op("vector", lambda e, ysl=ysl, bank=bank:
                                 e.tensor_copy(ysl, PB[bank]),
                                 reads=[PBR[bank]], writes=[R_ysl])
                        else:
                            P.op("vector", lambda e, ysl=ysl, bank=bank:
                                 e.tensor_add(ysl, ysl, PB[bank]),
                                 reads=[PBR[bank], R_ysl], writes=[R_ysl])
            A.pop()

            ybr = Rot(A, 2, T, BF16)
            ycast = {}

            def ychunk(c):
                yb, R_yb = ybr.nxt()
                P.op("vector", lambda e, c=c, yb=yb: e.tensor_copy(
                    yb, y_sb[:, c * T:(c + 1) * T]),
                    reads=[R_y.sub(4 * c * T, 4 * T)], writes=[R_yb])
                return yb, [R_yb]

            bc_r, R_bcr, bc_m, R_bcm = ln_stats_bc(ychunk, NC, D, EPS_LN,
                                                   "f0.post")
            xcr = Rot(A, 3, T, F32)
            xnr = Rot(A, 3, T, F32)
            tmp = Rot(A, 2, T, F32)
            for c in range(NC):
                xc, R_xc = xcr.nxt()
                P.op("sync", lambda e, c=c, xc=xc: e.dma_start(
                    out=xc, in_=src_ap[c * 128:(c + 1) * 128, :]),
                    reads=[Region("dram:" + src_name, c * 128 * T * 4,
                                  (c + 1) * 128 * T * 4)],
                    writes=[R_xc], dma=True)
                tf, R_tf = tmp.nxt()
                P.op("vector", lambda e, c=c, tf=tf: e.tensor_mul(
                    tf, y_sb[:, c * T:(c + 1) * T], bc_r),
                    reads=[R_y.sub(4 * c * T, 4 * T), R_bcr], writes=[R_tf])
                P.op("vector", lambda e, tf=tf: e.tensor_sub(tf, tf, bc_m),
                     reads=[R_tf, R_bcm], writes=[R_tf])
                xo, R_xo = xnr.nxt()
                P.op("vector", lambda e, xo=xo, xc=xc, tf=tf: e.tensor_add(
                    xo, xc, tf), reads=[R_xc, R_tf], writes=[R_xo])
                P.op("sync", lambda e, c=c, xo=xo: e.dma_start(
                    out=dst_ap[c * 128:(c + 1) * 128, :], in_=xo),
                    reads=[R_xo],
                    writes=[Region("dram:" + dst_name, c * 128 * T * 4,
                                   (c + 1) * 128 * T * 4)], dma=True)
            A.pop()

        def phase_f1(first, last):
            src_name = "x" if first else "x_spill"
            src_ap = x_in if first else x_spill
            dst_name = "xout" if last else "x_spill"
            dst_ap = x_out if last else x_spill

            A.push()
            d_bf, R_d = A.tile(NC * T, BF16)
            sg_sb, R_sg = A.tile(L1HCH, F32)
            su_sb, R_su = A.tile(L1HCH, F32)
            sd_sb, R_sd = A.tile(16, F32)
            for nm, ap_, rg in (("sg1", sg_sb, R_sg), ("su1", su_sb, R_su),
                                ("sd1", sd_sb, R_sd)):
                P.op("sync", lambda e, nm=nm, ap_=ap_: e.dma_start(
                    out=ap_, in_=dram[nm]),
                    reads=[dreg(nm)], writes=[rg], dma=True)
            A.push()
            hp_bf, R_hp = A.tile(L1HCH * T, BF16)
            A.push()
            xn_bf, R_xn = A.tile(NC * T, BF16)
            emit_preln(src_ap, src_name, xn_bf, R_xn, "f1.ln")
            debug_tap("f1.xn", xn_bf, R_xn, NC, BF16)

            bcx, R_bcx = A.tile(T, F32)
            for tt in range(2):
                def fcs(e, tt=tt):
                    ins = None
                    for c in range(NC):
                        ins = e.matmul(
                            PB[6 + tt][0:1, :], ones_col,
                            xn_bf[:, c * T + tt * 512: c * T + tt * 512 + 512],
                            start=(c == 0), stop=(c == NC - 1))
                    return ins
                P.op("tensor", fcs, reads=[R_xn, R_onc], writes=[PBR[6 + tt]])
                csr, R_csr = rows(tt * 512, 512)
                P.op("scalar", lambda e, csr=csr, tt=tt: e.copy(
                    csr, PB[6 + tt][0:1, :]),
                    reads=[PBR[6 + tt]], writes=[R_csr])
                P.op("tensor", lambda e, csr=csr, tt=tt: e.matmul(
                    PB[6 + tt], c128f_row, csr, start=True, stop=True),
                    reads=[R_csr, R_c128], writes=[PBR[6 + tt]])
                P.op("scalar", lambda e, tt=tt: e.copy(
                    bcx[:, tt * 512:tt * 512 + 512], PB[6 + tt]),
                    reads=[PBR[6 + tt]], writes=[R_bcx])

            wgr = Rot(A, 2, NC * 128, BF16)
            wur = Rot(A, 2, NC * 128, BF16)
            tgr = Rot(A, 2, 512, F32)
            tur = Rot(A, 2, 512, F32)
            for o in range(L1HCH):
                wg_, R_wg_ = wgr.nxt()
                wu_, R_wu_ = wur.nxt()
                P.op("gpsimd", lambda e, o=o, wg_=wg_: e.dma_start(
                    out=wg_, in_=dram["wgT1"][o]),
                    reads=[dreg("wgT1")], writes=[R_wg_], dma=True)
                P.op("gpsimd", lambda e, o=o, wu_=wu_: e.dma_start(
                    out=wu_, in_=dram["wuT1"][o]),
                    reads=[dreg("wuT1")], writes=[R_wu_], dma=True)
                for tt in range(2):
                    bs = (o % 2) * 4
                    for mi, (w, R_w) in enumerate(((wg_, R_wg_), (wu_, R_wu_))):
                        bank = bs + mi * 2 + tt

                        def mm(e, w=w, tt=tt, bank=bank):
                            ins = None
                            for c in range(NC):
                                ins = e.matmul(
                                    PB[bank], w[:, c * 128:(c + 1) * 128],
                                    xn_bf[:, c * T + tt * 512:
                                          c * T + tt * 512 + 512],
                                    start=(c == 0), stop=(c == NC - 1))
                            return ins
                        P.op("tensor", mm, reads=[R_w, R_xn],
                             writes=[PBR[bank]], tag=f"f1.gu{o}t{tt}m{mi}")
                    tg, R_tg = tgr.nxt()
                    tu, R_tu = tur.nxt()
                    P.op("vector", lambda e, tg=tg, bs=bs, tt=tt: e.tensor_sub(
                        tg, PB[bs + tt], bcx[:, tt * 512:tt * 512 + 512]),
                        reads=[PBR[bs + tt], R_bcx], writes=[R_tg])
                    P.op("scalar", lambda e, tg=tg, o=o: e.activation(
                        out=tg, in_=tg, func=AF.Silu, scale=sg_sb[:, o:o + 1]),
                        reads=[R_tg, R_sg], writes=[R_tg])
                    P.op("vector", lambda e, tu=tu, bs=bs, tt=tt: e.tensor_sub(
                        tu, PB[bs + 2 + tt], bcx[:, tt * 512:tt * 512 + 512]),
                        reads=[PBR[bs + 2 + tt], R_bcx], writes=[R_tu])
                    P.op("vector", lambda e, tu=tu, o=o: e.tensor_scalar(
                        tu, tu, su_sb[:, o:o + 1], None, ALU.mult),
                        reads=[R_tu, R_su], writes=[R_tu])
                    hsl = hp_bf[:, o * T + tt * 512: o * T + tt * 512 + 512]
                    R_hsl = R_hp.sub(2 * (o * T + tt * 512), 2 * 512)
                    P.op("vector", lambda e, hsl=hsl, tg=tg, tu=tu:
                         e.tensor_mul(hsl, tg, tu),
                         reads=[R_tg, R_tu], writes=[R_hsl])
                    if DEBUG_TAP in (f"f1.tg{o}_{tt}", f"f1.tu{o}_{tt}"):
                        dsrc, drg = (tg, R_tg) if "tg" in DEBUG_TAP else (tu, R_tu)
                        dbg2 = nc.dram_tensor("dbg", [128, 512], F32,
                                              kind="ExternalOutput").ap()
                        tok = P.op("sync", lambda e, dsrc=dsrc: e.dma_start(
                            out=dbg2, in_=dsrc), reads=[drg],
                            writes=[Region("dram:dbg")], dma=True)
                        P.final_tokens.append(tok)
                    if DEBUG_TAP == f"f1.hp{o}_{tt}":
                        dbg3 = nc.dram_tensor("dbg", [128, 512], BF16,
                                              kind="ExternalOutput").ap()
                        tok = P.op("sync", lambda e, hsl=hsl: e.dma_start(
                            out=dbg3, in_=hsl), reads=[R_hsl],
                            writes=[Region("dram:dbg")], dma=True)
                        P.final_tokens.append(tok)
            debug_tap("f1.bcx", bcx, R_bcx, 1, F32)
            A.pop()
            debug_tap("f1.hp", hp_bf, R_hp, L1HCH, BF16)
            if DEBUG_STOP == "gate":
                return
            if DEBUG_TAP == "f1.hptail":
                dbg4 = nc.dram_tensor("dbg", [4 * 128, T], BF16,
                                      kind="ExternalOutput").ap()
                tok = P.op("sync", lambda e: e.dma_start(
                    out=dbg4.rearrange("(c p) t -> p c t", p=128),
                    in_=hp_bf[:, 40 * T:44 * T].rearrange(
                        "p (c t) -> p c t", c=4)),
                    reads=[R_hp], writes=[Region("dram:dbg")], dma=True)
                P.final_tokens.append(tok)

            bch, R_bch = A.tile(T, F32)
            for tt in ([] if DEBUG_STOP in ("down_nobch",) else range(2)):
                def fcs2(e, tt=tt):
                    ins = None
                    for c in range(L1HCH):
                        ins = e.matmul(
                            PB[6 + tt][0:1, :], ones_col,
                            hp_bf[:, c * T + tt * 512: c * T + tt * 512 + 512],
                            start=(c == 0), stop=(c == L1HCH - 1))
                    return ins
                P.op("tensor", fcs2, reads=[R_hp, R_onc], writes=[PBR[6 + tt]])
                csr, R_csr = rows(tt * 512, 512)
                P.op("scalar", lambda e, csr=csr, tt=tt: e.copy(
                    csr, PB[6 + tt][0:1, :]),
                    reads=[PBR[6 + tt]], writes=[R_csr])
                P.op("tensor", lambda e, csr=csr, tt=tt: e.matmul(
                    PB[6 + tt], c128f_row, csr, start=True, stop=True),
                    reads=[R_csr, R_c128], writes=[PBR[6 + tt]])
                P.op("scalar", lambda e, tt=tt: e.copy(
                    bch[:, tt * 512:tt * 512 + 512], PB[6 + tt]),
                    reads=[PBR[6 + tt]], writes=[R_bch])

            if DEBUG_STOP == "bch":
                return
            debug_tap("f1.bch", bch, R_bch, 1, F32)
            if DEBUG_STOP == "down_nobch":
                wdr0 = Rot(A, 2, L1HCH * 128, BF16)
                for o in range(16):
                    wd_, R_wd_ = wdr0.nxt()
                    P.op("gpsimd", lambda e, o=o, wd_=wd_: e.dma_start(
                        out=wd_, in_=dram["wdT1"][o]),
                        reads=[dreg("wdT1")], writes=[R_wd_], dma=True)
                return
            if DEBUG_STOP == "down_fresh":
                A.off = 183 * 1024
                wdrf = Rot(A, 2, L1HCH * 128, BF16)
                for o in range(1):
                    wd_, R_wd_ = wdrf.nxt()
                    P.op("gpsimd", lambda e, o=o, wd_=wd_: e.dma_start(
                        out=wd_, in_=dram["wdT1"][o]),
                        reads=[dreg("wdT1")], writes=[R_wd_], dma=True)
                return
            wdr = Rot(A, 2, L1HCH * 128, BF16)
            if DEBUG_STOP in ("down_dma1", "down_dmaonly2"):
                nloads = 1 if DEBUG_STOP == "down_dma1" else 16
                for o in range(nloads):
                    wd_, R_wd_ = wdr.nxt()
                    P.op("gpsimd", lambda e, o=o, wd_=wd_: e.dma_start(
                        out=wd_, in_=dram["wdT1"][o]),
                        reads=[dreg("wdT1")], writes=[R_wd_], dma=True)
                return
            tdr = Rot(A, 2, 512, F32)
            for o in range(NC):
                wd_, R_wd_ = wdr.nxt()
                for q4 in range(4):
                    qs = L1HID // 4
                    P.op("gpsimd", lambda e, o=o, wd_=wd_, q4=q4, qs=qs:
                         e.dma_start(out=wd_[:, q4 * qs:(q4 + 1) * qs],
                                     in_=dram["wdT1"][o][:, q4 * qs:(q4 + 1) * qs]),
                         reads=[dreg("wdT1")], writes=[R_wd_], dma=True)
                for tt in range(2):
                    if DEBUG_STOP == "down_dmaonly":
                        continue
                    bank = (o * 2 + tt) % 4

                    def mmd(e, wd_=wd_, tt=tt, bank=bank):
                        ins = None
                        for c in range(L1HCH):
                            ins = e.matmul(
                                PB[bank], wd_[:, c * 128:(c + 1) * 128],
                                hp_bf[:, c * T + tt * 512:
                                      c * T + tt * 512 + 512],
                                start=(c == 0), stop=(c == L1HCH - 1))
                        return ins
                    P.op("tensor", mmd, reads=[R_wd_, R_hp],
                         writes=[PBR[bank]], tag=f"f1.d{o}t{tt}")
                    td, R_td = tdr.nxt()
                    P.op("vector", lambda e, td=td, bank=bank, tt=tt:
                         e.tensor_sub(td, PB[bank],
                                      bch[:, tt * 512:tt * 512 + 512]),
                         reads=[PBR[bank], R_bch], writes=[R_td])
                    dsl = d_bf[:, o * T + tt * 512: o * T + tt * 512 + 512]
                    R_dsl = R_d.sub(2 * (o * T + tt * 512), 2 * 512)
                    P.op("vector", lambda e, dsl=dsl, td=td, o=o:
                         e.tensor_scalar(dsl, td, sd_sb[:, o:o + 1], None,
                                         ALU.mult),
                         reads=[R_td, R_sd], writes=[R_dsl])
            A.pop()
            debug_tap("f1.d", d_bf, R_d, NC, BF16)
            if DEBUG_STOP == "down":
                return

            emit_postres(src_ap, src_name, dst_ap, dst_name, d_bf, R_d,
                         "f1.post")
            A.pop()

        for pi, ph in enumerate(phases):
            first = (pi == 0)
            last = (pi == len(phases) - 1)
            if ph in ("a0", "a1"):
                phase_attn(ph, first, last)
            elif ph == "f0":
                phase_f0(first, last)
            elif ph == "f1":
                phase_f1(first, last)

        for reg, tok in P.writers:
            if reg.space == "dram:xout":
                P.final_tokens.append(tok)

        P.emit(nc)
    return nc, sorted(dram.keys()), A.peak, P.n_ops, P


def _to_native(WT, in_ch, n_out):
    W = np.asarray(WT)
    return np.ascontiguousarray(
        W.reshape(in_ch, 128, n_out, 128).transpose(2, 1, 0, 3)
        .reshape(n_out, 128, in_ch * 128))


def _prep_host(inp):
    H = {}
    f32 = np.float32

    def ct(a):
        return np.ascontiguousarray(np.asarray(a, dtype=f32))

    H["wqT0"] = ct(np.asarray(inp["L0_attn_wq"], f32).T)
    H["wkT0"] = ct(np.asarray(inp["L0_attn_wk"], f32).T)
    H["wvT0"] = ct(np.asarray(inp["L0_attn_wv"], f32).T)
    H["woT0"] = ct(np.asarray(inp["L0_attn_wo"], f32).T)
    H["qnw"] = ct(inp["L0_attn_q_norm_w"]).reshape(1, 128)
    H["knw"] = ct(inp["L0_attn_k_norm_w"]).reshape(1, 128)
    cos16 = ct(inp["L0_attn_rope_cos"])[0:16, :].T
    sin16 = ct(inp["L0_attn_rope_sin"])[0:16, :].T
    H["ropeHC"] = np.ascontiguousarray(np.concatenate([cos16, cos16], 0))
    H["ropeHS"] = np.ascontiguousarray(np.concatenate([sin16, sin16], 0))
    perm = np.concatenate([np.arange(0, 128, 2), np.arange(1, 128, 2)])
    full_perm = np.concatenate([h * 128 + perm for h in range(16)])
    H["wqT0"] = _to_native(H["wqT0"][:, full_perm], NC, 16)
    H["wkT0"] = _to_native(H["wkT0"][:, full_perm], NC, 16)
    H["wvT0"] = _to_native(H["wvT0"], NC, 16)
    H["woT0"] = _to_native(H["woT0"], 16, NC)
    H["qnw"] = np.ascontiguousarray(H["qnw"][:, perm])
    H["knw"] = np.ascontiguousarray(H["knw"][:, perm])

    wgt = ct(np.asarray(inp["L0_ffn_Wg"], f32).T)
    H["WgT"] = np.ascontiguousarray(
        wgt.reshape(NC, 128, 16).transpose(1, 0, 2).reshape(128, NC * 16))
    W1 = np.asarray(inp["L0_ffn_W1"], f32)
    W3 = np.asarray(inp["L0_ffn_W3"], f32)
    W2 = np.asarray(inp["L0_ffn_W2"], f32)
    W1T = np.empty((NEX, D, I_MOE), f32)
    W3T = np.empty((NEX, D, I_MOE), f32)
    W2T = np.empty((NEX, I_MOE, D), f32)
    for e in range(16):
        W1T[e] = W1[e].T
        W3T[e] = W3[e].T
        W2T[e] = W2[e].T
    Ws1T = np.asarray(inp["L0_ffn_Ws1"], f32).T
    Ws3T = np.asarray(inp["L0_ffn_Ws3"], f32).T
    Ws2T = np.asarray(inp["L0_ffn_Ws2"], f32).T
    W1T[16] = Ws1T[:, 0:I_MOE]
    W1T[17] = Ws1T[:, I_MOE:]
    W3T[16] = Ws3T[:, 0:I_MOE]
    W3T[17] = Ws3T[:, I_MOE:]
    W2T[16] = Ws2T[0:I_MOE, :]
    W2T[17] = Ws2T[I_MOE:, :]
    W1Tn = np.empty((NEX, ICH, 128, D), f32)
    W3Tn = np.empty((NEX, ICH, 128, D), f32)
    W2Tn = np.empty((NEX, 16, 128, ICH * 128), f32)
    for e in range(NEX):
        W1Tn[e] = _to_native(W1T[e], NC, ICH)
        W3Tn[e] = _to_native(W3T[e], NC, ICH)
        W2Tn[e] = _to_native(W2T[e], ICH, NC)
    H["W1T"], H["W3T"], H["W2T"] = W1Tn, W3Tn, W2Tn
    H["identf"] = np.eye(128, dtype=f32)
    H["identb"] = np.eye(128, dtype=BF)
    esel = np.zeros((16, 16 * 128), f32)
    for e in range(16):
        esel[e, e * 128:(e + 1) * 128] = 1.0
    H["esel"] = esel

    H["wq1"] = _to_native(ct(inp["L1_attn_wq"]), NC, 16)
    H["wk1"] = _to_native(ct(inp["L1_attn_wk"]), NC, 4)
    H["wv1"] = _to_native(ct(inp["L1_attn_wv"]), NC, 4)
    H["wo1"] = _to_native(ct(inp["L1_attn_wo"]), 16, NC)
    H["bq1"] = ct(inp["L1_attn_bq"]).reshape(16, 128).T.copy()
    H["bk1"] = ct(inp["L1_attn_bk"]).reshape(4, 128).T.copy()
    H["bv1"] = ct(inp["L1_attn_bv"]).reshape(4, 128).T.copy()
    H["bo1"] = ct(inp["L1_attn_bo"]).reshape(16, 128).T.copy()

    H["wgT1"] = _to_native(np.asarray(inp["L1_ffn_wq_gate"], np.int32).T,
                           NC, L1HCH)
    H["wuT1"] = _to_native(np.asarray(inp["L1_ffn_wq_up"], np.int32).T,
                           NC, L1HCH)
    H["wdT1"] = _to_native(np.asarray(inp["L1_ffn_wq_down"], np.int32).T,
                           L1HCH, NC)
    H["sg1"] = ct(inp["L1_ffn_s_gate"]).reshape(L1HCH, 128).T.copy()
    H["su1"] = ct(inp["L1_ffn_s_up"]).reshape(L1HCH, 128).T.copy()
    H["sd1"] = ct(inp["L1_ffn_s_down"]).reshape(16, 128).T.copy()
    return H


def _prep_core(inp, H, c):
    b, hf = c // 2, c % 2
    lo = hf * T
    per = {}
    x = np.asarray(inp["L0_attn_x"], np.float32)
    per["x"] = np.ascontiguousarray(x[b].T[:, lo:lo + T])
    tk = np.arange(SEQ).reshape(16, 128)
    tq = lo + np.arange(T)
    m = (tk[:, :, None] <= tq[None, None, :]).astype(BF)
    per["pmask"] = np.ascontiguousarray(m.transpose(1, 0, 2).reshape(128, 16 * T))
    return per


_CACHE = {}


def kernel(**inputs):
    H = _prep_host(inputs)
    per_core = [_prep_core(inputs, H, c) for c in range(8)]
    x_cur = [per_core[c]["x"] for c in range(8)]

    for phases in LAUNCH_PLAN:
        if phases not in _CACHE:
            _CACHE[phases] = build_program(phases)
        nc, names, peak, nops = _CACHE[phases][:4]
        in_maps = []
        for c in range(8):
            m = {}
            for n in names:
                if n == "x":
                    m[n] = x_cur[c]
                elif n in per_core[c]:
                    m[n] = per_core[c][n]
                else:
                    m[n] = H[n]
            in_maps.append(m)
        res = run_bass_kernel_spmd(nc, in_maps, core_ids=list(range(8)))
        x_cur = [np.asarray(res.results[c]["xout"]) for c in range(8)]

    out = np.empty((NB, SEQ, D), np.float32)
    for c in range(8):
        b, hf = c // 2, c % 2
        out[b, hf * T:(hf + 1) * T, :] = x_cur[c].T
    return out


# version 07
# speedup vs baseline: 1.0995x; 1.0646x over previous
import numpy as np
import ml_dtypes
from contextlib import ExitStack

import concourse.bass as bass
import concourse.mybir as mybir
from concourse.bass_utils import run_bass_kernel_spmd

F32 = mybir.dt.float32
BF16 = mybir.dt.bfloat16
I32 = mybir.dt.int32
AF = mybir.ActivationFunctionType
ALU = mybir.AluOpType
AX = mybir.AxisListType
BF = ml_dtypes.bfloat16

D = 2048
NC = 16
T = 1024
SEQ = 2048
DH = 128
NEX = 18
ICH = 11
I_MOE = 1408
L1HCH = 44
L1HID = 5632
NB = 4
EPS_LN = 1e-5
EPS_RMS = 1e-6
SOFTCAP = 50.0
SC_ATT = DH ** -0.5

ENGINES = ("sync", "tensor", "vector", "scalar", "gpsimd")
N_DMA_SEM = 12
GROUPS = [[0, 1], [2, 3], [4, 5], [6, 7]]
LAUNCH_PLAN = [("a0", "f0", "a1", "f1")]
DEBUG_TAP = None
DEBUG_STOP = None


class Region:
    __slots__ = ("space", "lo", "hi")

    def __init__(self, space, lo=0, hi=1 << 62):
        self.space, self.lo, self.hi = space, lo, hi

    def sub(self, lo_b, nb):
        return Region(self.space, self.lo + lo_b, self.lo + lo_b + nb)

    def overlaps(self, o):
        return self.space == o.space and self.lo < o.hi and o.lo < self.hi


class Prog:
    def __init__(self):
        self.ops = {e: [] for e in ENGINES}
        self.counts = {}
        self.writers = []
        self.readers = []
        self.dma_rr = {}
        self.sem_names = set()
        self.final_tokens = []
        self.oplog = []
        self.n_ops = 0

    def _sem(self, name):
        self.sem_names.add(name)
        return name

    def op(self, eng, fn, reads=(), writes=(), dma=False, coll=False, tag=""):
        waits = {}

        def need(tok):
            s, v = tok
            if waits.get(s, 0) < v:
                waits[s] = v

        for r in reads:
            for reg, tok in self.writers:
                if r.overlaps(reg):
                    need(tok)
        for w in writes:
            for reg, tok in self.writers:
                if w.overlaps(reg):
                    need(tok)
            for reg, tok in self.readers:
                if w.overlaps(reg):
                    need(tok)
        if dma:
            i = self.dma_rr.get(eng, 0)
            self.dma_rr[eng] = i + 1
            sem = self._sem(f"dq_{eng}_{i % N_DMA_SEM}")
            inc = 16
        elif coll:
            sem = self._sem("cc")
            inc = 1
        else:
            sem = self._sem(f"tk_{eng}")
            inc = 1
        val = self.counts.get(sem, 0) + inc
        self.counts[sem] = val
        tok = (sem, val)
        self.ops[eng].append((fn, waits, sem, inc, tag))
        self.oplog.append((eng, tag, [(w.space, w.lo, w.hi) for w in writes],
                           [(r.space, r.lo, r.hi) for r in reads]))
        self.n_ops += 1
        def contained(g, w):
            return g.space == w.space and g.lo >= w.lo and g.hi <= w.hi

        for w in writes:
            self.writers = [(g, t) for g, t in self.writers
                            if not contained(g, w)]
            self.readers = [(g, t) for g, t in self.readers
                            if not contained(g, w)]
            self.writers.append((w, tok))
        for r in reads:
            kept = []
            for g, t in self.readers:
                if (g.space == r.space and g.lo == r.lo and g.hi == r.hi
                        and t[0] == tok[0]):
                    continue
                kept.append((g, t))
            kept.append((r, tok))
            self.readers = kept
        return tok

    def emit(self, nc):
        with ExitStack() as st:
            sems = {n: st.enter_context(nc.semaphore(n))
                    for n in sorted(self.sem_names)}
            with nc.Block() as block:
                for ename in ENGINES:
                    ops = self.ops[ename]
                    fin = self.final_tokens if ename == "sync" else []
                    if not ops and not fin:
                        continue

                    def mkbody(ops=ops, fin=fin):
                        def body(eng):
                            water = {}
                            for fn, waits, sem, inc, tag in ops:
                                for s in sorted(waits):
                                    v = waits[s]
                                    if water.get(s, 0) < v:
                                        eng.wait_ge(sems[s], v)
                                        water[s] = v
                                ins = fn(eng)
                                ins.then_inc(sems[sem], inc)
                            for s, v in fin:
                                if water.get(s, 0) < v:
                                    eng.wait_ge(sems[s], v)
                                    water[s] = v
                        return body

                    getattr(block, ename)(mkbody())


class Arena:
    def __init__(self, ap_u8, nbytes):
        self.ap = ap_u8
        self.nbytes = nbytes
        self.off = 0
        self.marks = []
        self.peak = 0

    def tile(self, cols, dtype, parts=128):
        esz = mybir.dt.size(dtype)
        nb = cols * esz
        nb_al = (nb + 31) & ~31
        assert self.off + nb_al <= self.nbytes, \
            f"SBUF arena overflow: {self.off}+{nb_al} > {self.nbytes}"
        lo = self.off
        self.off += nb_al
        self.peak = max(self.peak, self.off)
        ap = self.ap[:, lo:lo + nb].bitcast(dtype)
        if parts != 128:
            ap = ap[0:parts, :]
        return ap, Region("sbuf", lo, lo + nb)

    def push(self):
        self.marks.append(self.off)

    def pop(self):
        self.off = self.marks.pop()


class Rot:
    def __init__(self, A, n, cols, dtype, parts=128):
        self.items = [A.tile(cols, dtype, parts) for _ in range(n)]
        self.i = 0

    def nxt(self):
        it = self.items[self.i % len(self.items)]
        self.i += 1
        return it


def build_program(phases):
    nc = bass.Bass("TRN2", target_bir_lowering=False)
    P = Prog()
    dram = {}

    def din(name, shape, dtype):
        dram[name] = nc.dram_tensor(name, list(shape), dtype,
                                    kind="ExternalInput").ap()
        return dram[name]

    def dreg(name):
        return Region("dram:" + name)

    x_in = din("x", (D, T), F32)
    x_out = nc.dram_tensor("xout", [D, T], F32, kind="ExternalOutput").ap()
    x_spill = nc.dram_tensor("x_spill", [D, T], F32).ap()
    dbg_state = {}

    def debug_tap(tag, ap, R, n_ch, dtype):
        if DEBUG_TAP != tag or "done" in dbg_state:
            return
        dbg_state["done"] = True
        dbg = nc.dram_tensor("dbg", [n_ch * 128, T], dtype,
                             kind="ExternalOutput").ap()
        tok = P.op("sync", lambda e: e.dma_start(
            out=dbg.rearrange("(c p) t -> p c t", p=128),
            in_=ap.rearrange("p (c t) -> p c t", c=n_ch)),
            reads=[R], writes=[Region("dram:dbg")], dma=True)
        P.final_tokens.append(tok)

    need = set(phases)
    if need & {"a0", "a1"}:
        din("pmask", (128, 16 * T), BF16)
        din("identb", (128, 128), BF16)
    if "a0" in need:
        din("wqT0", (16, 128, D), F32)
        din("wkT0", (16, 128, D), F32)
        din("wvT0", (16, 128, D), F32)
        din("woT0", (16, 128, D), F32)
        din("qnw", (1, 128), F32)
        din("knw", (1, 128), F32)
        din("ropeHC", (128, 16), F32)
        din("ropeHS", (128, 16), F32)
    if "f0" in need:
        din("WgT", (128, 256), F32)
        din("identf", (128, 128), F32)
        din("esel", (16, 16 * 128), F32)
        din("W1T", (NEX, ICH, 128, D), F32)
        din("W3T", (NEX, ICH, 128, D), F32)
        din("W2T", (NEX, 16, 128, ICH * 128), F32)
    if "a1" in need:
        din("wq1", (16, 128, D), F32)
        din("wk1", (4, 128, D), F32)
        din("wv1", (4, 128, D), F32)
        din("wo1", (16, 128, D), F32)
        din("bq1", (128, 16), F32)
        din("bk1", (128, 4), F32)
        din("bv1", (128, 4), F32)
        din("bo1", (128, 16), F32)
    if "f1" in need:
        din("wgT1", (L1HCH, 128, D), I32)
        din("wuT1", (L1HCH, 128, D), I32)
        din("wdT1", (16, 128, L1HID), I32)
        din("sg1", (128, L1HCH), F32)
        din("su1", (128, L1HCH), F32)
        din("sd1", (128, 16), F32)

    cc = {}
    if "a0" in need:
        cc["k0i"] = nc.dram_tensor("k0i", [D, T], BF16).ap()
        cc["k0g"] = nc.dram_tensor("k0g", [2 * D, T], BF16).ap()
        cc["v0i"] = nc.dram_tensor("v0i", [T, D], BF16).ap()
        cc["v0g"] = nc.dram_tensor("v0g", [2 * T, D], BF16).ap()
    if "a1" in need:
        cc["k1i"] = nc.dram_tensor("k1i", [512, T], BF16).ap()
        cc["k1g"] = nc.dram_tensor("k1g", [1024, T], BF16).ap()
        cc["v1i"] = nc.dram_tensor("v1i", [T, 512], BF16).ap()
        cc["v1g"] = nc.dram_tensor("v1g", [2 * T, 512], BF16).ap()

    with ExitStack() as st:
        ARENA = 212608
        arena_t = st.enter_context(
            nc.sbuf_tensor("arena", [128, ARENA], mybir.dt.uint8))
        A = Arena(arena_t[:, :], ARENA)
        PBT = st.enter_context(nc.psum_tensor("pball", [128, 4096], F32))
        PBALL = PBT[:, :]
        PB = [PBALL[:, 512 * k:512 * (k + 1)] for k in range(8)]
        PBR = [Region(f"ps{k}") for k in range(8)]
        PBP = [PBALL[:, 1024 * p:1024 * (p + 1)] for p in range(4)]
        PBbf = [PB[k].bitcast(BF16) for k in range(8)]

        ones_col, R_onc = A.tile(1, BF16)
        onesf_row, R_onfr = A.tile(128, F32, parts=1)
        c128f_row, R_c128 = A.tile(128, F32, parts=1)
        rowbuf, R_rows = A.tile(2048, F32, parts=1)

        def rows(off, n):
            return rowbuf[0:1, off:off + n], R_rows.sub(4 * off, 4 * n)

        eps_ln_t, R_epl = A.tile(1, F32, parts=1)
        eps_rms_t, R_epr = A.tile(1, F32, parts=1)
        P.op("gpsimd", lambda e: e.memset(ones_col, 1.0), writes=[R_onc])
        P.op("gpsimd", lambda e: e.memset(onesf_row, 1.0), writes=[R_onfr])
        P.op("gpsimd", lambda e: e.memset(c128f_row, 128.0), writes=[R_c128])
        P.op("gpsimd", lambda e: e.memset(eps_ln_t, EPS_LN), writes=[R_epl])
        P.op("gpsimd", lambda e: e.memset(eps_rms_t, EPS_RMS), writes=[R_epr])

        def ln_stats_bc(chunk_src, n_ch, n_feat, eps, tag):
            bc_r, R_bcr = A.tile(T, F32)
            bc_m, R_bcm = A.tile(T, F32)
            A.push()
            sqr = Rot(A, 2, T, BF16)
            for c in range(n_ch):
                src_ap, src_regs = chunk_src(c)
                sq, R_sq = sqr.nxt()
                P.op("scalar", lambda e, src_ap=src_ap, sq=sq: e.activation(
                    out=sq, in_=src_ap, func=AF.Square),
                    reads=list(src_regs), writes=[R_sq], tag=tag + f".sq{c}")
                for tt in range(2):
                    P.op("tensor", lambda e, src_ap=src_ap, tt=tt, c=c: e.matmul(
                        PB[4 + tt][0:1, :], ones_col,
                        src_ap[:, tt * 512:tt * 512 + 512],
                        start=(c == 0), stop=(c == n_ch - 1)),
                        reads=list(src_regs) + [R_onc], writes=[PBR[4 + tt]],
                        tag=tag + f".s1_{c}{tt}")
                    P.op("tensor", lambda e, sq=sq, tt=tt, c=c: e.matmul(
                        PB[6 + tt][0:1, :], ones_col,
                        sq[:, tt * 512:tt * 512 + 512],
                        start=(c == 0), stop=(c == n_ch - 1)),
                        reads=[R_sq, R_onc], writes=[PBR[6 + tt]],
                        tag=tag + f".s2_{c}{tt}")
            A.pop()
            m_ap, m_reg = rows(0, T)
            v_ap, v_reg = rows(T, T)
            inv = 1.0 / n_feat
            for tt in range(2):
                P.op("vector", lambda e, tt=tt: e.tensor_scalar(
                    m_ap[:, tt * 512:tt * 512 + 512], PB[4 + tt][0:1, :], inv,
                    None, ALU.mult), reads=[PBR[4 + tt]], writes=[m_reg])
                P.op("vector", lambda e, tt=tt: e.tensor_scalar(
                    v_ap[:, tt * 512:tt * 512 + 512], PB[6 + tt][0:1, :], inv,
                    None, ALU.mult), reads=[PBR[6 + tt]], writes=[v_reg])
            P.op("scalar", lambda e: e.activation(
                out=PB[4][0:1, 0:512], in_=m_ap[:, 0:512], func=AF.Square),
                reads=[m_reg], writes=[PBR[4]])
            P.op("scalar", lambda e: e.activation(
                out=PB[5][0:1, 0:512], in_=m_ap[:, 512:1024], func=AF.Square),
                reads=[m_reg], writes=[PBR[5]])
            P.op("vector", lambda e: e.tensor_sub(
                v_ap[:, 0:512], v_ap[:, 0:512], PB[4][0:1, 0:512]),
                reads=[v_reg, PBR[4]], writes=[v_reg])
            P.op("vector", lambda e: e.tensor_sub(
                v_ap[:, 512:1024], v_ap[:, 512:1024], PB[5][0:1, 0:512]),
                reads=[v_reg, PBR[5]], writes=[v_reg])
            P.op("scalar", lambda e: e.activation(out=v_ap, in_=v_ap, func=AF.Ln,
                                                  bias=eps_ln_t),
                 reads=[v_reg, R_epl], writes=[v_reg])
            P.op("scalar", lambda e: e.activation(out=v_ap, in_=v_ap,
                                                  func=AF.Exp, scale=-0.5),
                 reads=[v_reg], writes=[v_reg])
            P.op("vector", lambda e: e.tensor_mul(m_ap, m_ap, v_ap),
                 reads=[m_reg, v_reg], writes=[m_reg])
            for tt in range(2):
                P.op("tensor", lambda e, tt=tt: e.matmul(
                    PB[4 + tt], onesf_row, v_ap[:, tt * 512:tt * 512 + 512],
                    start=True, stop=True),
                    reads=[v_reg, R_onfr], writes=[PBR[4 + tt]])
                P.op("tensor", lambda e, tt=tt: e.matmul(
                    PB[6 + tt], onesf_row, m_ap[:, tt * 512:tt * 512 + 512],
                    start=True, stop=True),
                    reads=[m_reg, R_onfr], writes=[PBR[6 + tt]])
                P.op("scalar", lambda e, tt=tt: e.copy(
                    bc_r[:, tt * 512:tt * 512 + 512], PB[4 + tt]),
                    reads=[PBR[4 + tt]], writes=[R_bcr])
                P.op("scalar", lambda e, tt=tt: e.copy(
                    bc_m[:, tt * 512:tt * 512 + 512], PB[6 + tt]),
                    reads=[PBR[6 + tt]], writes=[R_bcm])
            return bc_r, R_bcr, bc_m, R_bcm

        def emit_preln(src_ap, src_name, xn_bf, R_xn, tag, f32_cb=None):
            A.push()
            xcr = Rot(A, 3, T, F32)
            xbr = Rot(A, 2, T, BF16)

            def load_chunk(c, xc, R_xc):
                P.op("sync", lambda e, c=c, xc=xc: e.dma_start(
                    out=xc, in_=src_ap[c * 128:(c + 1) * 128, :]),
                    reads=[Region("dram:" + src_name, c * 128 * T * 4,
                                  (c + 1) * 128 * T * 4)],
                    writes=[R_xc], dma=True, tag=tag + f".ldx{c}")

            chunk_cache = {}

            def chunk_src(c):
                xc, R_xc = xcr.nxt()
                load_chunk(c, xc, R_xc)
                xb, R_xb = xbr.nxt()
                P.op("vector", lambda e, xb=xb, xc=xc: e.tensor_copy(xb, xc),
                     reads=[R_xc], writes=[R_xb])
                return xb, [R_xb]

            bc_r, R_bcr, bc_m, R_bcm = ln_stats_bc(chunk_src, NC, D, EPS_LN, tag)
            tmp = Rot(A, 2, T, F32)
            for c in range(NC):
                xc, R_xc = xcr.nxt()
                load_chunk(c, xc, R_xc)
                tf, R_tf = tmp.nxt()
                R_nc = R_xn.sub(2 * c * T, 2 * T)
                P.op("vector", lambda e, xc=xc, tf=tf: e.tensor_mul(
                    tf, xc, bc_r), reads=[R_xc, R_bcr], writes=[R_tf])
                if f32_cb is None:
                    P.op("vector", lambda e, c=c, tf=tf: e.tensor_sub(
                        xn_bf[:, c * T:(c + 1) * T], tf, bc_m),
                        reads=[R_tf, R_bcm], writes=[R_nc])
                else:
                    P.op("vector", lambda e, tf=tf: e.tensor_sub(tf, tf, bc_m),
                         reads=[R_tf, R_bcm], writes=[R_tf])
                    P.op("gpsimd", lambda e, c=c, tf=tf: e.tensor_copy(
                        xn_bf[:, c * T:(c + 1) * T], tf),
                        reads=[R_tf], writes=[R_nc])
                    f32_cb(c, tf, R_tf)
            A.pop()

        def emit_postres(src_ap, src_name, dst_ap, dst_name, out_bf, R_out,
                         tag):
            A.push()

            def chunk_src(c):
                return (out_bf[:, c * T:(c + 1) * T],
                        [R_out.sub(2 * c * T, 2 * T)])

            bc_r, R_bcr, bc_m, R_bcm = ln_stats_bc(chunk_src, NC, D, EPS_LN,
                                                   tag)
            xcr = Rot(A, 3, T, F32)
            xnr = Rot(A, 3, T, F32)
            tmp = Rot(A, 2, T, F32)
            store_toks = []
            for c in range(NC):
                xc, R_xc = xcr.nxt()
                P.op("sync", lambda e, c=c, xc=xc: e.dma_start(
                    out=xc, in_=src_ap[c * 128:(c + 1) * 128, :]),
                    reads=[Region("dram:" + src_name, c * 128 * T * 4,
                                  (c + 1) * 128 * T * 4)],
                    writes=[R_xc], dma=True)
                tf, R_tf = tmp.nxt()
                P.op("vector", lambda e, c=c, tf=tf: e.tensor_mul(
                    tf, out_bf[:, c * T:(c + 1) * T], bc_r),
                    reads=[R_out.sub(2 * c * T, 2 * T), R_bcr], writes=[R_tf])
                P.op("vector", lambda e, tf=tf: e.tensor_sub(tf, tf, bc_m),
                     reads=[R_tf, R_bcm], writes=[R_tf])
                xo, R_xo = xnr.nxt()
                P.op("vector", lambda e, xo=xo, xc=xc, tf=tf: e.tensor_add(
                    xo, xc, tf), reads=[R_xc, R_tf], writes=[R_xo])
                tok = P.op("sync", lambda e, c=c, xo=xo: e.dma_start(
                    out=dst_ap[c * 128:(c + 1) * 128, :], in_=xo),
                    reads=[R_xo], writes=[dreg(dst_name) if False else
                                          Region("dram:" + dst_name,
                                                 c * 128 * T * 4,
                                                 (c + 1) * 128 * T * 4)],
                    dma=True)
                store_toks.append(tok)
            A.pop()
            return store_toks

        def emit_proj(w_hbm, wname, n_out_ch, src_bf, R_src, out_cb, tag,
                      in_ch=NC, nbuf=3, wpool=None):
            A.push()
            wt = wpool if wpool is not None else Rot(A, nbuf, in_ch * 128, BF16)
            for o in range(n_out_ch):
                w, R_w = wt.nxt()

                def ldw(e, o=o, w=w):
                    return e.dma_start(out=w, in_=w_hbm[o])
                P.op("gpsimd", ldw, reads=[dreg(wname)], writes=[R_w], dma=True,
                     tag=tag + f".w{o}")
                p = o % 2
                for tt in range(2):
                    bank = p * 2 + tt

                    def mm(e, tt=tt, w=w, bank=bank):
                        ins = None
                        for c in range(in_ch):
                            ins = e.matmul(
                                PB[bank], w[:, c * 128:(c + 1) * 128],
                                src_bf[:, c * T + tt * 512: c * T + tt * 512 + 512],
                                start=(c == 0), stop=(c == in_ch - 1))
                        return ins
                    P.op("tensor", mm, reads=[R_w, R_src], writes=[PBR[bank]],
                         tag=tag + f".mm{o}_{tt}")
                out_cb(o, PBP[p], [PBR[p * 2], PBR[p * 2 + 1]])
            A.pop()

        def phase_attn(layer, first, last):
            is0 = (layer == "a0")
            kvh_n = 16 if is0 else 4
            kv_dim = kvh_n * 128
            kgname, vgname = ("k0g", "v0g") if is0 else ("k1g", "v1g")
            kiname, viname = ("k0i", "v0i") if is0 else ("k1i", "v1i")
            src_name = "x" if first else "x_spill"
            src_ap = x_in if first else x_spill
            dst_name = "xout" if last else "x_spill"
            dst_ap = x_out if last else x_spill

            A.push()
            wpool = Rot(A, 4, NC * 128, BF16)
            q_bf, R_q = A.tile(16 * T, BF16)
            ao_bf, R_ao = A.tile(16 * T, BF16)
            idb_sb, R_idb = A.tile(128, BF16)
            P.op("sync", lambda e: e.dma_start(out=idb_sb, in_=dram["identb"]),
                 reads=[dreg("identb")], writes=[R_idb], dma=True)
            if is0:
                ropeC_sb, R_rc = A.tile(16, F32)
                ropeS_sb, R_rs = A.tile(16, F32)
                qnw_sb, R_qnw = A.tile(128, F32, parts=1)
                knw_sb, R_knw = A.tile(128, F32, parts=1)
                for nm, ap_, rg in (("ropeHC", ropeC_sb, R_rc),
                                    ("ropeHS", ropeS_sb, R_rs),
                                    ("qnw", qnw_sb, R_qnw),
                                    ("knw", knw_sb, R_knw)):
                    P.op("sync", lambda e, nm=nm, ap_=ap_: e.dma_start(
                        out=ap_, in_=dram[nm]),
                        reads=[dreg(nm)], writes=[rg], dma=True)
            else:
                bq_sb, R_bq = A.tile(16, F32)
                bk_sb, R_bk = A.tile(4, F32)
                bv_sb, R_bv = A.tile(4, F32)
                bo_sb, R_bo = A.tile(16, F32)
                for nm, ap_, rg in (("bq1", bq_sb, R_bq), ("bk1", bk_sb, R_bk),
                                    ("bv1", bv_sb, R_bv), ("bo1", bo_sb, R_bo)):
                    P.op("sync", lambda e, nm=nm, ap_=ap_: e.dma_start(
                        out=ap_, in_=dram[nm]),
                        reads=[dreg(nm)], writes=[rg], dma=True)

            A.push()
            xn_bf, R_xn = A.tile(NC * T, BF16)
            emit_preln(src_ap, src_name, xn_bf, R_xn, layer + ".ln")

            praw = Rot(A, 2, T, F32)
            sqr = Rot(A, 2, T, BF16)
            bctr0 = Rot(A, 2, T, F32)
            qnr = Rot(A, 2, T, F32)
            t0r = Rot(A, 2, T, F32)
            khead = Rot(A, 2, T, BF16)
            vhead = Rot(A, 2, T, BF16)
            vtmp = Rot(A, 2, T, BF16)

            def stage_k(o, kh, R_kh):
                P.op("sync", lambda e, o=o, kh=kh: e.dma_start(
                    out=cc[kiname][o * 128:(o + 1) * 128, :], in_=kh),
                    reads=[R_kh],
                    writes=[Region("dram:" + kiname, o * 128 * T * 2,
                                   (o + 1) * 128 * T * 2)], dma=True)

            def stage_v(o, vh, R_vh):
                P.op("sync", lambda e, o=o, vh=vh: e.dma_start(
                    out=cc[viname][:, o * 128:(o + 1) * 128].rearrange(
                        "(tc p) d -> p tc d", p=128),
                    in_=vh.rearrange("p (tc d) -> p tc d", d=128)),
                    reads=[R_vh],
                    writes=[Region("dram:" + viname, o, o + 1)], dma=True)

            def qk_post_l0(sink, nw_sb, R_nw):
                def cb(o, pp, pregs):
                    pr, R_pr = praw.nxt()
                    P.op("scalar", lambda e, pr=pr, pp=pp: e.copy(pr, pp),
                         reads=pregs, writes=[R_pr])
                    sqb, R_sqb = sqr.nxt()
                    P.op("scalar", lambda e, sqb=sqb, pr=pr: e.activation(
                        out=sqb, in_=pr, func=AF.Square),
                        reads=[R_pr], writes=[R_sqb])
                    roff = 1024 * (o % 2)
                    rs_ap, rs_reg = rows(roff, 1024)
                    for tt in range(2):
                        P.op("tensor", lambda e, sqb=sqb, tt=tt: e.matmul(
                            PB[4 + tt][0:1, :], ones_col,
                            sqb[:, tt * 512:tt * 512 + 512],
                            start=True, stop=True),
                            reads=[R_sqb, R_onc], writes=[PBR[4 + tt]])
                        P.op("scalar", lambda e, rs_ap=rs_ap, tt=tt:
                             e.activation(out=rs_ap[:, tt * 512:tt * 512 + 512],
                                          in_=PB[4 + tt][0:1, :], func=AF.Ln,
                                          scale=1.0 / DH, bias=eps_rms_t),
                             reads=[PBR[4 + tt], R_epr], writes=[rs_reg])
                    P.op("scalar", lambda e, rs_ap=rs_ap: e.activation(
                        out=rs_ap, in_=rs_ap, func=AF.Exp, scale=-0.5),
                        reads=[rs_reg], writes=[rs_reg])
                    for tt in range(2):
                        P.op("tensor", lambda e, rs_ap=rs_ap, tt=tt: e.matmul(
                            PB[4 + tt], nw_sb,
                            rs_ap[:, tt * 512:tt * 512 + 512],
                            start=True, stop=True),
                            reads=[rs_reg, R_nw], writes=[PBR[4 + tt]])
                    bct, R_bct = bctr0.nxt()
                    P.op("scalar", lambda e, bct=bct: e.copy(bct, PBP[2]),
                         reads=[PBR[4], PBR[5]], writes=[R_bct])
                    qn, R_qn = qnr.nxt()
                    P.op("vector", lambda e, qn=qn, pr=pr, bct=bct:
                         e.tensor_mul(qn, pr, bct),
                         reads=[R_pr, R_bct], writes=[R_qn])
                    t0, R_t0 = t0r.nxt()
                    P.op("vector", lambda e, t0=t0, qn=qn, o=o:
                         e.tensor_scalar(t0, qn, ropeC_sb[:, o:o + 1], None,
                                         ALU.mult),
                         reads=[R_qn, R_rc], writes=[R_t0])
                    P.op("vector", lambda e, qn=qn, o=o: e.tensor_scalar(
                        PBP[2], qn, ropeS_sb[:, o:o + 1], None, ALU.mult),
                        reads=[R_qn, R_rs], writes=[PBR[4], PBR[5]])
                    d_sl, R_dsl = sink(o)
                    P.op("vector", lambda e, d_sl=d_sl, t0=t0: e.tensor_sub(
                        d_sl[0:64, :], t0[0:64, :], PBP[2][64:128, :]),
                        reads=[R_t0, PBR[4], PBR[5]], writes=[R_dsl])
                    P.op("vector", lambda e, d_sl=d_sl, t0=t0: e.tensor_add(
                        d_sl[64:128, :], t0[64:128, :], PBP[2][0:64, :]),
                        reads=[R_t0, PBR[4], PBR[5]], writes=[R_dsl])
                return cb

            kstate = {}

            def k_sink(o):
                kstate["t"] = khead.nxt()
                return kstate["t"]

            def emit_k_ag(half):
                if kvh_n == 16:
                    rows_h = 8 * 128
                    P.op("gpsimd", lambda e, half=half: e.collective_compute(
                        "AllGather", ALU.bypass, replica_groups=GROUPS,
                        ins=[cc[kiname][half * rows_h:(half + 1) * rows_h, :]],
                        outs=[cc[kgname][half * 2 * rows_h:
                                         (half + 1) * 2 * rows_h, :]]),
                        reads=[Region("dram:" + kiname, half * rows_h * T * 2,
                                      (half + 1) * rows_h * T * 2)],
                        writes=[Region("dram:" + kgname,
                                       half * 2 * rows_h * T * 2,
                                       (half + 1) * 2 * rows_h * T * 2)],
                        coll=True)
                elif half == 1:
                    P.op("gpsimd", lambda e: e.collective_compute(
                        "AllGather", ALU.bypass, replica_groups=GROUPS,
                        ins=[cc[kiname]], outs=[cc[kgname]]),
                        reads=[Region("dram:" + kiname)],
                        writes=[dreg(kgname)], coll=True)

            def k_post_wrap(cb):
                def cb2(o, pp, pregs):
                    cb(o, pp, pregs)
                    kh, R_kh = kstate["t"]
                    stage_k(o, kh, R_kh)
                    if o == kvh_n // 2 - 1:
                        emit_k_ag(0)
                    if o == kvh_n - 1:
                        emit_k_ag(1)
                return cb2

            def bias_sink_cb(sink, b_sb, R_b, stage_fn=None):
                def cb(o, pp, pregs):
                    d_sl, R_dsl = sink(o)
                    if b_sb is None:
                        P.op("scalar", lambda e, d_sl=d_sl, pp=pp: e.copy(
                            d_sl, pp), reads=pregs, writes=[R_dsl])
                    else:
                        P.op("vector", lambda e, d_sl=d_sl, pp=pp, o=o:
                             e.tensor_scalar(d_sl, pp, b_sb[:, o:o + 1],
                                             None, ALU.add),
                             reads=pregs + [R_b], writes=[R_dsl])
                    if stage_fn is not None:
                        stage_fn(o)
                return cb

            def v_evac_cb(o, pp, pregs):
                vt, R_vtt = vtmp.nxt()
                if is0:
                    P.op("scalar", lambda e, vt=vt, pp=pp: e.copy(vt, pp),
                         reads=pregs, writes=[R_vtt])
                else:
                    P.op("vector", lambda e, vt=vt, pp=pp, o=o:
                         e.tensor_scalar(vt, pp, bv_sb[:, o:o + 1], None,
                                         ALU.add),
                         reads=pregs + [R_bv], writes=[R_vtt])
                vh, R_vh = vhead.nxt()
                for tc in range(8):
                    bank2 = 4 + tc % 2
                    P.op("tensor", lambda e, vt=vt, tc=tc, bank2=bank2:
                         e.transpose(PBbf[bank2][:, 0:128],
                                     vt[:, tc * 128:tc * 128 + 128], idb_sb),
                         reads=[R_vtt, R_idb], writes=[PBR[bank2]])
                    P.op("vector", lambda e, vh=vh, tc=tc, bank2=bank2:
                         e.tensor_copy(vh[:, tc * 128:tc * 128 + 128],
                                       PBbf[bank2][:, 0:128]),
                         reads=[PBR[bank2]], writes=[R_vh])
                stage_v(o, vh, R_vh)

            if is0:
                emit_proj(dram["wvT0"], "wvT0", kvh_n, xn_bf, R_xn, v_evac_cb,
                          layer + ".v", wpool=wpool)
            else:
                emit_proj(dram["wv1"], "wv1", kvh_n, xn_bf, R_xn, v_evac_cb,
                          layer + ".v", wpool=wpool)

            P.op("gpsimd", lambda e: e.collective_compute(
                "AllGather", ALU.bypass, replica_groups=GROUPS,
                ins=[cc[viname]], outs=[cc[vgname]]),
                reads=[Region("dram:" + viname)], writes=[dreg(vgname)],
                coll=True)

            if is0:
                emit_proj(dram["wkT0"], "wkT0", kvh_n, xn_bf, R_xn,
                          k_post_wrap(qk_post_l0(k_sink, knw_sb, R_knw)),
                          layer + ".k", wpool=wpool)
            else:
                def a1_k_stage(o):
                    stage_k(*((o,) + kstate["t"]))
                    if o == kvh_n - 1:
                        emit_k_ag(1)
                emit_proj(dram["wk1"], "wk1", kvh_n, xn_bf, R_xn,
                          bias_sink_cb(k_sink, bk_sb, R_bk, a1_k_stage),
                          layer + ".k", wpool=wpool)


            def q_sink(o):
                sl = q_bf[:, o * T:(o + 1) * T]
                return sl, R_q.sub(2 * o * T, 2 * T)

            if is0:
                emit_proj(dram["wqT0"], "wqT0", 16, xn_bf, R_xn,
                          qk_post_l0(q_sink, qnw_sb, R_qnw), layer + ".q", wpool=wpool)
            else:
                emit_proj(dram["wq1"], "wq1", 16, xn_bf, R_xn,
                          bias_sink_cb(q_sink, bq_sb, R_bq), layer + ".q", wpool=wpool)
            A.pop()

            A.push()
            mask_sb, R_mask = A.tile(16 * T, BF16)
            P.op("sync", lambda e: e.dma_start(out=mask_sb, in_=dram["pmask"]),
                reads=[dreg("pmask")], writes=[R_mask], dma=True)
            GH = 2
            kbuf = Rot(A, 2, GH * SEQ, BF16)
            vbuf = Rot(A, 2, GH * SEQ, BF16)
            ptr = Rot(A, 6, T, BF16)
            utr = Rot(A, 3, T, F32)
            bctr = Rot(A, 2, T, F32)

            def load_group(g):
                kt, R_kt = kbuf.nxt()
                vt, R_vt2 = vbuf.nxt()
                for hl in range(GH):
                    h = g * GH + hl
                    kvh = h * kvh_n // 16
                    for r in range(2):
                        if kvh_n == 16:
                            krow = ((kvh // 8) * 2048 + r * 1024
                                    + (kvh % 8) * 128)
                        else:
                            krow = r * kv_dim + 128 * kvh
                        P.op("sync", lambda e, hl=hl, r=r, krow=krow, kt=kt:
                             e.dma_start(
                                 out=kt[:, hl * SEQ + r * T: hl * SEQ + r * T + T],
                                 in_=cc[kgname][krow:krow + 128, :]),
                             reads=[Region("dram:" + kgname, krow * T * 2,
                                           (krow + 128) * T * 2)],
                             writes=[R_kt], dma=True)
                        P.op("sync", lambda e, hl=hl, r=r, kvh=kvh, vt=vt:
                             e.dma_start(
                                 out=vt[:, hl * SEQ + r * T: hl * SEQ + r * T + T]
                                 .rearrange("p (tc d) -> p tc d", d=128),
                                 in_=cc[vgname][r * T:(r + 1) * T,
                                                128 * kvh:128 * kvh + 128]
                                 .rearrange("(tc p) d -> p tc d", p=128)),
                             reads=[dreg(vgname)], writes=[R_vt2], dma=True)
                return kt, R_kt, vt, R_vt2

            cur = load_group(0)
            for g in range(16 // GH):
                kt, R_kt, vt, R_vt2 = cur
                nxt = load_group(g + 1) if g < 16 // GH - 1 else None
                for hl in range(GH):
                    h = g * GH + hl
                    ptiles = {}

                    def emit_sums_O(j):
                        pt, R_pt = ptiles.pop(j)
                        for tt in range(2):
                            P.op("tensor", lambda e, pt=pt, j=j, tt=tt:
                                 e.matmul(PB[4 + tt][0:1, :], ones_col,
                                          pt[:, tt * 512:tt * 512 + 512],
                                          start=(j == 0), stop=(j == 15)),
                                 reads=[R_pt, R_onc], writes=[PBR[4 + tt]])
                            vsl = vt[:, hl * SEQ + j * 128:
                                     hl * SEQ + j * 128 + 128]
                            P.op("tensor", lambda e, vsl=vsl, pt=pt, j=j, tt=tt:
                                 e.matmul(PB[6 + tt], vsl,
                                          pt[:, tt * 512:tt * 512 + 512],
                                          start=(j == 0), stop=(j == 15)),
                                 reads=[R_vt2, R_pt], writes=[PBR[6 + tt]])

                    R_qsl = R_q.sub(2 * h * T, 2 * T)
                    for j in range(16):
                        sp = j % 2
                        ksl = kt[:, hl * SEQ + j * 128: hl * SEQ + j * 128 + 128]
                        for tt in range(2):
                            qsl = q_bf[:, h * T + tt * 512: h * T + tt * 512 + 512]
                            P.op("tensor", lambda e, ksl=ksl, qsl=qsl, sp=sp,
                                 tt=tt: e.matmul(PB[sp * 2 + tt], ksl, qsl,
                                                 start=True, stop=True),
                                 reads=[R_kt, R_qsl], writes=[PBR[sp * 2 + tt]])
                        pt, R_pt = ptr.nxt()
                        ptiles[j] = (pt, R_pt)
                        if is0:
                            ut, R_ut = utr.nxt()
                            P.op("scalar", lambda e, sp=sp, ut=ut: e.activation(
                                out=ut, in_=PBP[sp], func=AF.Tanh,
                                scale=SC_ATT / SOFTCAP),
                                reads=[PBR[sp * 2], PBR[sp * 2 + 1]],
                                writes=[R_ut])
                            P.op("scalar", lambda e, ut=ut, pt=pt: e.activation(
                                out=pt, in_=ut, func=AF.Exp, scale=SOFTCAP),
                                reads=[R_ut], writes=[R_pt])
                        else:
                            P.op("scalar", lambda e, sp=sp, pt=pt: e.activation(
                                out=pt, in_=PBP[sp], func=AF.Exp, scale=SC_ATT),
                                reads=[PBR[sp * 2], PBR[sp * 2 + 1]],
                                writes=[R_pt])
                        msl = mask_sb[:, j * T:(j + 1) * T]
                        P.op("vector", lambda e, pt=pt, msl=msl: e.tensor_mul(
                            pt, pt, msl), reads=[R_mask, R_pt], writes=[R_pt])
                        if j >= 3:
                            emit_sums_O(j - 3)
                    emit_sums_O(13)
                    emit_sums_O(14)
                    emit_sums_O(15)
                    roff = T + 0
                    inv_ap, inv_reg = rows(roff, 1024)
                    for tt in range(2):
                        P.op("vector", lambda e, tt=tt, inv_ap=inv_ap:
                             e.reciprocal(inv_ap[:, tt * 512:tt * 512 + 512],
                                          PB[4 + tt][0:1, :]),
                             reads=[PBR[4 + tt]], writes=[inv_reg])
                        P.op("tensor", lambda e, tt=tt, inv_ap=inv_ap: e.matmul(
                            PB[4 + tt], onesf_row,
                            inv_ap[:, tt * 512:tt * 512 + 512],
                            start=True, stop=True),
                            reads=[inv_reg, R_onfr], writes=[PBR[4 + tt]])
                    bct, R_bct = bctr.nxt()
                    P.op("scalar", lambda e, bct=bct: e.copy(bct, PBP[2]),
                         reads=[PBR[4], PBR[5]], writes=[R_bct])
                    osl = ao_bf[:, h * T:(h + 1) * T]
                    R_osl = R_ao.sub(2 * h * T, 2 * T)
                    P.op("vector", lambda e, osl=osl, bct=bct: e.tensor_mul(
                        osl, PBP[3], bct),
                        reads=[PBR[6], PBR[7], R_bct], writes=[R_osl])
                cur = nxt
            A.pop()

            aoo_bf, R_aoo = A.tile(NC * T, BF16)

            def o_sink(o):
                sl = aoo_bf[:, o * T:(o + 1) * T]
                return sl, R_aoo.sub(2 * o * T, 2 * T)

            if is0:
                emit_proj(dram["woT0"], "woT0", NC, ao_bf, R_ao,
                          bias_sink_cb(o_sink, None, None), layer + ".o",
                          in_ch=16, wpool=wpool)
            else:
                emit_proj(dram["wo1"], "wo1", NC, ao_bf, R_ao,
                          bias_sink_cb(o_sink, bo_sb, R_bo), layer + ".o",
                          in_ch=16, wpool=wpool)
            emit_postres(src_ap, src_name, dst_ap, dst_name, aoo_bf, R_aoo,
                         layer + ".post")
            A.pop()

        def phase_f0(first, last):
            src_name = "x" if first else "x_spill"
            src_ap = x_in if first else x_spill
            dst_name = "xout" if last else "x_spill"
            dst_ap = x_out if last else x_spill

            A.push()
            w13 = Rot(A, 4, NC * 128, BF16)
            w2r = Rot(A, 2, ICH * 128, BF16)
            y_sb, R_y = A.tile(NC * T, F32)
            A.push()
            hp_bf, R_hp = A.tile(ICH * T, BF16)
            xn_bf, R_xn = A.tile(NC * T, BF16)
            wg_sb, R_wg = A.tile(16 * 16, F32)
            idf_t, R_idft = A.tile(128, F32)
            P.op("sync", lambda e: e.dma_start(out=wg_sb, in_=dram["WgT"]),
                reads=[dreg("WgT")], writes=[R_wg], dma=True)
            P.op("sync", lambda e: e.dma_start(out=idf_t, in_=dram["identf"]),
                 reads=[dreg("identf")], writes=[R_idft], dma=True)

            def router_cb(c, tf, R_tf):
                for tt in range(2):
                    P.op("tensor", lambda e, c=c, tf=tf, tt=tt: e.matmul(
                        PB[2 + tt][0:16, :], wg_sb[:, c * 16:(c + 1) * 16],
                        tf[:, tt * 512:tt * 512 + 512],
                        start=(c == 0), stop=(c == NC - 1)),
                        reads=[R_tf, R_wg], writes=[PBR[2 + tt]])

            emit_preln(src_ap, src_name, xn_bf, R_xn, "f0.ln",
                       f32_cb=router_cb)

            lg_sb, R_lg = A.tile(T, F32, parts=16)
            for tt in range(2):
                P.op("scalar", lambda e, tt=tt: e.copy(
                    lg_sb[:, tt * 512:tt * 512 + 512], PB[2 + tt][0:16, :]),
                    reads=[PBR[2 + tt]], writes=[R_lg])
            ltm, R_ltm = A.tile(8 * 16, F32)
            for tc in range(8):
                bank = tc % 2
                P.op("tensor", lambda e, tc=tc, bank=bank: e.transpose(
                    PB[bank][:, 0:16], lg_sb[:, tc * 128:(tc + 1) * 128],
                    idf_t[0:16, 0:16]),
                    reads=[R_lg, R_idft], writes=[PBR[bank]])
                P.op("vector", lambda e, tc=tc, bank=bank: e.tensor_copy(
                    ltm[:, tc * 16:(tc + 1) * 16], PB[bank][:, 0:16]),
                    reads=[PBR[bank]], writes=[R_ltm])

            cmb_tm, R_cmb = A.tile(8 * 16, F32)
            sc_t, R_sct = A.tile(16, F32)
            gs_t, R_gst = A.tile(4, F32)
            gt_t, R_gt = A.tile(16, F32)
            r1_t, R_r1 = A.tile(1, F32)
            r2_t, R_r2 = A.tile(1, F32)
            ex_t, R_ex = A.tile(16, F32)
            for tc in range(8):
                lt = ltm[:, tc * 16:(tc + 1) * 16]
                R_lt = R_ltm.sub(4 * tc * 16, 4 * 16)
                P.op("scalar", lambda e, lt=lt: e.activation(
                    out=ex_t, in_=lt, func=AF.Exp),
                    reads=[R_lt], writes=[R_ex])
                P.op("vector", lambda e: e.reduce_sum(r1_t, ex_t, axis=AX.X),
                     reads=[R_ex], writes=[R_r1])
                P.op("vector", lambda e: e.reciprocal(r2_t, r1_t),
                     reads=[R_r1], writes=[R_r2])
                P.op("vector", lambda e: e.tensor_scalar(
                    sc_t, ex_t, r2_t, None, ALU.mult),
                    reads=[R_ex, R_r2], writes=[R_sct])
                for gg in range(4):
                    P.op("vector", lambda e, gg=gg: e.reduce_max(
                        gs_t[:, gg:gg + 1], sc_t[:, gg * 4:(gg + 1) * 4],
                        axis=AX.X), reads=[R_sct], writes=[R_gst])
                for gg in range(4):
                    P.op("vector", lambda e, gg=gg: e.tensor_scalar(
                        gt_t[:, 0:4], gs_t, gs_t[:, gg:gg + 1], None,
                        ALU.is_gt), reads=[R_gst], writes=[R_gt])
                    P.op("vector", lambda e: e.reduce_sum(
                        r1_t, gt_t[:, 0:4], axis=AX.X),
                        reads=[R_gt], writes=[R_r1])
                    P.op("vector", lambda e: e.tensor_scalar(
                        r1_t, r1_t, 1.5, None, ALU.is_le),
                        reads=[R_r1], writes=[R_r1])
                    P.op("vector", lambda e: e.tensor_scalar(
                        r2_t, r1_t, 1e9, -1e9, ALU.mult, ALU.add),
                        reads=[R_r1], writes=[R_r2])
                    P.op("vector", lambda e, gg=gg: e.tensor_scalar(
                        ex_t[:, gg * 4:(gg + 1) * 4],
                        sc_t[:, gg * 4:(gg + 1) * 4], r1_t, r2_t,
                        ALU.mult, ALU.add),
                        reads=[R_sct, R_r1, R_r2], writes=[R_ex])
                for ee in range(16):
                    P.op("vector", lambda e, ee=ee: e.tensor_scalar(
                        gt_t, ex_t, ex_t[:, ee:ee + 1], None, ALU.is_gt),
                        reads=[R_ex], writes=[R_gt])
                    P.op("vector", lambda e: e.reduce_sum(
                        r1_t, gt_t, axis=AX.X), reads=[R_gt], writes=[R_r1])
                    P.op("vector", lambda e: e.tensor_scalar(
                        r1_t, r1_t, 3.5, None, ALU.is_le),
                        reads=[R_r1], writes=[R_r1])
                    dsl = cmb_tm[:, tc * 16 + ee: tc * 16 + ee + 1]
                    R_dsl = R_cmb.sub(4 * (tc * 16 + ee), 4)
                    P.op("vector", lambda e, dsl=dsl, ee=ee: e.tensor_mul(
                        dsl, sc_t[:, ee:ee + 1], r1_t),
                        reads=[R_sct, R_r1], writes=[R_dsl])
            cmb_fm, R_cfm = A.tile(T, F32, parts=16)
            esel_sb, R_esel = A.tile(16 * 128, F32, parts=16)

            def emit_cmb_fm():
                P.op("sync", lambda e: e.dma_start(out=esel_sb,
                                                   in_=dram["esel"]),
                     reads=[dreg("esel")], writes=[R_esel], dma=True)
                for tc in range(8):
                    bank = tc % 2
                    P.op("tensor", lambda e, tc=tc, bank=bank: e.transpose(
                        PB[bank][0:16, 0:128],
                        cmb_tm[:, tc * 16:(tc + 1) * 16], idf_t),
                        reads=[R_cmb, R_idft], writes=[PBR[bank]])
                    P.op("vector", lambda e, tc=tc, bank=bank: e.tensor_copy(
                        cmb_fm[:, tc * 128:(tc + 1) * 128],
                        PB[bank][0:16, 0:128]),
                        reads=[PBR[bank]], writes=[R_cfm])

            bce, R_bce = A.tile(T, F32)
            sar = Rot(A, 2, 512, F32)
            tbr = Rot(A, 2, 512, F32)

            for ei, ex in enumerate([16] + list(range(16)) + [17]):
                first_ex = (ei == 0)
                if ei == 1:
                    emit_cmb_fm()
                routed = ex < 16
                if routed:
                    for tt in range(2):
                        P.op("tensor", lambda e, ex=ex, tt=tt: e.matmul(
                            PB[6 + tt], esel_sb[:, ex * 128:(ex + 1) * 128],
                            cmb_fm[:, tt * 512:tt * 512 + 512],
                            start=True, stop=True),
                            reads=[R_cfm, R_esel], writes=[PBR[6 + tt]])
                        P.op("scalar", lambda e, tt=tt: e.copy(
                            bce[:, tt * 512:tt * 512 + 512], PB[6 + tt]),
                            reads=[PBR[6 + tt]], writes=[R_bce])
                for ic in range(ICH):
                    bs = 0 if ic % 2 == 0 else 4
                    w1, R_w1 = w13.nxt()
                    w3, R_w3 = w13.nxt()
                    P.op("gpsimd", lambda e, ex=ex, ic=ic, w1=w1: e.dma_start(
                        out=w1, in_=dram["W1T"][ex, ic]),
                        reads=[dreg("W1T")], writes=[R_w1], dma=True)
                    P.op("gpsimd", lambda e, ex=ex, ic=ic, w3=w3: e.dma_start(
                        out=w3, in_=dram["W3T"][ex, ic]),
                        reads=[dreg("W3T")], writes=[R_w3], dma=True)
                    for tt in range(2):
                        for mi, (w, R_w) in enumerate(((w1, R_w1), (w3, R_w3))):
                            bank = bs + mi * 2 + tt

                            def mm(e, w=w, tt=tt, bank=bank):
                                ins = None
                                for c in range(NC):
                                    ins = e.matmul(
                                        PB[bank], w[:, c * 128:(c + 1) * 128],
                                        xn_bf[:, c * T + tt * 512:
                                              c * T + tt * 512 + 512],
                                        start=(c == 0), stop=(c == NC - 1))
                                return ins
                            P.op("tensor", mm, reads=[R_w, R_xn],
                                 writes=[PBR[bank]],
                                 tag=f"f0.e{ex}i{ic}t{tt}m{mi}")
                        sa, R_sa = sar.nxt()
                        P.op("scalar", lambda e, sa=sa, bs=bs, tt=tt:
                             e.activation(out=sa, in_=PB[bs + tt],
                                          func=AF.Silu),
                             reads=[PBR[bs + tt]], writes=[R_sa])
                        hsl = hp_bf[:, ic * T + tt * 512: ic * T + tt * 512 + 512]
                        R_hsl = R_hp.sub(2 * (ic * T + tt * 512), 2 * 512)
                        if routed:
                            tb, R_tb = tbr.nxt()
                            P.op("vector", lambda e, tb=tb, bs=bs, tt=tt:
                                 e.tensor_mul(tb, PB[bs + 2 + tt],
                                              bce[:, tt * 512:tt * 512 + 512]),
                                 reads=[PBR[bs + 2 + tt], R_bce],
                                 writes=[R_tb])
                            P.op("vector", lambda e, hsl=hsl, tb=tb, sa=sa:
                                 e.tensor_mul(hsl, tb, sa),
                                 reads=[R_tb, R_sa], writes=[R_hsl])
                        else:
                            P.op("vector", lambda e, hsl=hsl, bs=bs, tt=tt,
                                 sa=sa: e.tensor_mul(hsl, PB[bs + 2 + tt], sa),
                                 reads=[PBR[bs + 2 + tt], R_sa],
                                 writes=[R_hsl])
                for o in range(NC):
                    w2, R_w2 = w2r.nxt()
                    P.op("gpsimd", lambda e, ex=ex, o=o, w2=w2: e.dma_start(
                        out=w2, in_=dram["W2T"][ex, o]),
                        reads=[dreg("W2T")], writes=[R_w2], dma=True)
                    for tt in range(2):
                        bank = (o * 2 + tt) % 4

                        def mm2(e, w2=w2, tt=tt, bank=bank):
                            ins = None
                            for ci in range(ICH):
                                ins = e.matmul(
                                    PB[bank], w2[:, ci * 128:(ci + 1) * 128],
                                    hp_bf[:, ci * T + tt * 512:
                                          ci * T + tt * 512 + 512],
                                    start=(ci == 0), stop=(ci == ICH - 1))
                            return ins
                        P.op("tensor", mm2, reads=[R_w2, R_hp],
                             writes=[PBR[bank]], tag=f"f0.w2e{ex}o{o}t{tt}")
                        ysl = y_sb[:, o * T + tt * 512: o * T + tt * 512 + 512]
                        R_ysl = R_y.sub(4 * (o * T + tt * 512), 4 * 512)
                        if first_ex:
                            P.op("vector", lambda e, ysl=ysl, bank=bank:
                                 e.tensor_copy(ysl, PB[bank]),
                                 reads=[PBR[bank]], writes=[R_ysl])
                        else:
                            P.op("vector", lambda e, ysl=ysl, bank=bank:
                                 e.tensor_add(ysl, ysl, PB[bank]),
                                 reads=[PBR[bank], R_ysl], writes=[R_ysl])
            A.pop()

            ybr = Rot(A, 2, T, BF16)
            ycast = {}

            def ychunk(c):
                yb, R_yb = ybr.nxt()
                P.op("vector", lambda e, c=c, yb=yb: e.tensor_copy(
                    yb, y_sb[:, c * T:(c + 1) * T]),
                    reads=[R_y.sub(4 * c * T, 4 * T)], writes=[R_yb])
                return yb, [R_yb]

            bc_r, R_bcr, bc_m, R_bcm = ln_stats_bc(ychunk, NC, D, EPS_LN,
                                                   "f0.post")
            xcr = Rot(A, 3, T, F32)
            xnr = Rot(A, 3, T, F32)
            tmp = Rot(A, 2, T, F32)
            for c in range(NC):
                xc, R_xc = xcr.nxt()
                P.op("sync", lambda e, c=c, xc=xc: e.dma_start(
                    out=xc, in_=src_ap[c * 128:(c + 1) * 128, :]),
                    reads=[Region("dram:" + src_name, c * 128 * T * 4,
                                  (c + 1) * 128 * T * 4)],
                    writes=[R_xc], dma=True)
                tf, R_tf = tmp.nxt()
                P.op("vector", lambda e, c=c, tf=tf: e.tensor_mul(
                    tf, y_sb[:, c * T:(c + 1) * T], bc_r),
                    reads=[R_y.sub(4 * c * T, 4 * T), R_bcr], writes=[R_tf])
                P.op("vector", lambda e, tf=tf: e.tensor_sub(tf, tf, bc_m),
                     reads=[R_tf, R_bcm], writes=[R_tf])
                xo, R_xo = xnr.nxt()
                P.op("vector", lambda e, xo=xo, xc=xc, tf=tf: e.tensor_add(
                    xo, xc, tf), reads=[R_xc, R_tf], writes=[R_xo])
                P.op("sync", lambda e, c=c, xo=xo: e.dma_start(
                    out=dst_ap[c * 128:(c + 1) * 128, :], in_=xo),
                    reads=[R_xo],
                    writes=[Region("dram:" + dst_name, c * 128 * T * 4,
                                   (c + 1) * 128 * T * 4)], dma=True)
            A.pop()

        def phase_f1(first, last):
            src_name = "x" if first else "x_spill"
            src_ap = x_in if first else x_spill
            dst_name = "xout" if last else "x_spill"
            dst_ap = x_out if last else x_spill

            A.push()
            d_bf, R_d = A.tile(NC * T, BF16)
            sg_sb, R_sg = A.tile(L1HCH, F32)
            su_sb, R_su = A.tile(L1HCH, F32)
            sd_sb, R_sd = A.tile(16, F32)
            for nm, ap_, rg in (("sg1", sg_sb, R_sg), ("su1", su_sb, R_su),
                                ("sd1", sd_sb, R_sd)):
                P.op("sync", lambda e, nm=nm, ap_=ap_: e.dma_start(
                    out=ap_, in_=dram[nm]),
                    reads=[dreg(nm)], writes=[rg], dma=True)
            A.push()
            hp_bf, R_hp = A.tile(L1HCH * T, BF16)
            A.push()
            xn_bf, R_xn = A.tile(NC * T, BF16)
            emit_preln(src_ap, src_name, xn_bf, R_xn, "f1.ln")
            debug_tap("f1.xn", xn_bf, R_xn, NC, BF16)

            bcx, R_bcx = A.tile(T, F32)
            for tt in range(2):
                def fcs(e, tt=tt):
                    ins = None
                    for c in range(NC):
                        ins = e.matmul(
                            PB[6 + tt][0:1, :], ones_col,
                            xn_bf[:, c * T + tt * 512: c * T + tt * 512 + 512],
                            start=(c == 0), stop=(c == NC - 1))
                    return ins
                P.op("tensor", fcs, reads=[R_xn, R_onc], writes=[PBR[6 + tt]])
                csr, R_csr = rows(tt * 512, 512)
                P.op("scalar", lambda e, csr=csr, tt=tt: e.copy(
                    csr, PB[6 + tt][0:1, :]),
                    reads=[PBR[6 + tt]], writes=[R_csr])
                P.op("tensor", lambda e, csr=csr, tt=tt: e.matmul(
                    PB[6 + tt], c128f_row, csr, start=True, stop=True),
                    reads=[R_csr, R_c128], writes=[PBR[6 + tt]])
                P.op("scalar", lambda e, tt=tt: e.copy(
                    bcx[:, tt * 512:tt * 512 + 512], PB[6 + tt]),
                    reads=[PBR[6 + tt]], writes=[R_bcx])

            wgr = Rot(A, 2, NC * 128, BF16)
            wur = Rot(A, 2, NC * 128, BF16)
            tgr = Rot(A, 2, 512, F32)
            tur = Rot(A, 2, 512, F32)
            for o in range(L1HCH):
                wg_, R_wg_ = wgr.nxt()
                wu_, R_wu_ = wur.nxt()
                P.op("gpsimd", lambda e, o=o, wg_=wg_: e.dma_start(
                    out=wg_, in_=dram["wgT1"][o]),
                    reads=[dreg("wgT1")], writes=[R_wg_], dma=True)
                P.op("gpsimd", lambda e, o=o, wu_=wu_: e.dma_start(
                    out=wu_, in_=dram["wuT1"][o]),
                    reads=[dreg("wuT1")], writes=[R_wu_], dma=True)
                for tt in range(2):
                    bs = (o % 2) * 4
                    for mi, (w, R_w) in enumerate(((wg_, R_wg_), (wu_, R_wu_))):
                        bank = bs + mi * 2 + tt

                        def mm(e, w=w, tt=tt, bank=bank):
                            ins = None
                            for c in range(NC):
                                ins = e.matmul(
                                    PB[bank], w[:, c * 128:(c + 1) * 128],
                                    xn_bf[:, c * T + tt * 512:
                                          c * T + tt * 512 + 512],
                                    start=(c == 0), stop=(c == NC - 1))
                            return ins
                        P.op("tensor", mm, reads=[R_w, R_xn],
                             writes=[PBR[bank]], tag=f"f1.gu{o}t{tt}m{mi}")
                    tg, R_tg = tgr.nxt()
                    tu, R_tu = tur.nxt()
                    P.op("vector", lambda e, tg=tg, bs=bs, tt=tt: e.tensor_sub(
                        tg, PB[bs + tt], bcx[:, tt * 512:tt * 512 + 512]),
                        reads=[PBR[bs + tt], R_bcx], writes=[R_tg])
                    P.op("scalar", lambda e, tg=tg, o=o: e.activation(
                        out=tg, in_=tg, func=AF.Silu, scale=sg_sb[:, o:o + 1]),
                        reads=[R_tg, R_sg], writes=[R_tg])
                    P.op("vector", lambda e, tu=tu, bs=bs, tt=tt: e.tensor_sub(
                        tu, PB[bs + 2 + tt], bcx[:, tt * 512:tt * 512 + 512]),
                        reads=[PBR[bs + 2 + tt], R_bcx], writes=[R_tu])
                    P.op("vector", lambda e, tu=tu, o=o: e.tensor_scalar(
                        tu, tu, su_sb[:, o:o + 1], None, ALU.mult),
                        reads=[R_tu, R_su], writes=[R_tu])
                    hsl = hp_bf[:, o * T + tt * 512: o * T + tt * 512 + 512]
                    R_hsl = R_hp.sub(2 * (o * T + tt * 512), 2 * 512)
                    P.op("vector", lambda e, hsl=hsl, tg=tg, tu=tu:
                         e.tensor_mul(hsl, tg, tu),
                         reads=[R_tg, R_tu], writes=[R_hsl])
                    if DEBUG_TAP in (f"f1.tg{o}_{tt}", f"f1.tu{o}_{tt}"):
                        dsrc, drg = (tg, R_tg) if "tg" in DEBUG_TAP else (tu, R_tu)
                        dbg2 = nc.dram_tensor("dbg", [128, 512], F32,
                                              kind="ExternalOutput").ap()
                        tok = P.op("sync", lambda e, dsrc=dsrc: e.dma_start(
                            out=dbg2, in_=dsrc), reads=[drg],
                            writes=[Region("dram:dbg")], dma=True)
                        P.final_tokens.append(tok)
                    if DEBUG_TAP == f"f1.hp{o}_{tt}":
                        dbg3 = nc.dram_tensor("dbg", [128, 512], BF16,
                                              kind="ExternalOutput").ap()
                        tok = P.op("sync", lambda e, hsl=hsl: e.dma_start(
                            out=dbg3, in_=hsl), reads=[R_hsl],
                            writes=[Region("dram:dbg")], dma=True)
                        P.final_tokens.append(tok)
            debug_tap("f1.bcx", bcx, R_bcx, 1, F32)
            A.pop()
            debug_tap("f1.hp", hp_bf, R_hp, L1HCH, BF16)
            if DEBUG_STOP == "gate":
                return
            if DEBUG_TAP == "f1.hptail":
                dbg4 = nc.dram_tensor("dbg", [4 * 128, T], BF16,
                                      kind="ExternalOutput").ap()
                tok = P.op("sync", lambda e: e.dma_start(
                    out=dbg4.rearrange("(c p) t -> p c t", p=128),
                    in_=hp_bf[:, 40 * T:44 * T].rearrange(
                        "p (c t) -> p c t", c=4)),
                    reads=[R_hp], writes=[Region("dram:dbg")], dma=True)
                P.final_tokens.append(tok)

            bch, R_bch = A.tile(T, F32)
            for tt in ([] if DEBUG_STOP in ("down_nobch",) else range(2)):
                def fcs2(e, tt=tt):
                    ins = None
                    for c in range(L1HCH):
                        ins = e.matmul(
                            PB[6 + tt][0:1, :], ones_col,
                            hp_bf[:, c * T + tt * 512: c * T + tt * 512 + 512],
                            start=(c == 0), stop=(c == L1HCH - 1))
                    return ins
                P.op("tensor", fcs2, reads=[R_hp, R_onc], writes=[PBR[6 + tt]])
                csr, R_csr = rows(tt * 512, 512)
                P.op("scalar", lambda e, csr=csr, tt=tt: e.copy(
                    csr, PB[6 + tt][0:1, :]),
                    reads=[PBR[6 + tt]], writes=[R_csr])
                P.op("tensor", lambda e, csr=csr, tt=tt: e.matmul(
                    PB[6 + tt], c128f_row, csr, start=True, stop=True),
                    reads=[R_csr, R_c128], writes=[PBR[6 + tt]])
                P.op("scalar", lambda e, tt=tt: e.copy(
                    bch[:, tt * 512:tt * 512 + 512], PB[6 + tt]),
                    reads=[PBR[6 + tt]], writes=[R_bch])

            debug_tap("f1.bch", bch, R_bch, 1, F32)
            wdr = Rot(A, 2, L1HCH * 128, BF16)
            tdr = Rot(A, 2, 512, F32)
            for o in range(NC):
                wd_, R_wd_ = wdr.nxt()
                for q4 in range(4):
                    qs = L1HID // 4
                    P.op("gpsimd", lambda e, o=o, wd_=wd_, q4=q4, qs=qs:
                         e.dma_start(out=wd_[:, q4 * qs:(q4 + 1) * qs],
                                     in_=dram["wdT1"][o][:, q4 * qs:(q4 + 1) * qs]),
                         reads=[dreg("wdT1")], writes=[R_wd_], dma=True)
                for tt in range(2):
                    if DEBUG_STOP == "down_dmaonly":
                        continue
                    bank = (o * 2 + tt) % 4

                    def mmd(e, wd_=wd_, tt=tt, bank=bank):
                        ins = None
                        for c in range(L1HCH):
                            ins = e.matmul(
                                PB[bank], wd_[:, c * 128:(c + 1) * 128],
                                hp_bf[:, c * T + tt * 512:
                                      c * T + tt * 512 + 512],
                                start=(c == 0), stop=(c == L1HCH - 1))
                        return ins
                    P.op("tensor", mmd, reads=[R_wd_, R_hp],
                         writes=[PBR[bank]], tag=f"f1.d{o}t{tt}")
                    td, R_td = tdr.nxt()
                    P.op("vector", lambda e, td=td, bank=bank, tt=tt:
                         e.tensor_sub(td, PB[bank],
                                      bch[:, tt * 512:tt * 512 + 512]),
                         reads=[PBR[bank], R_bch], writes=[R_td])
                    dsl = d_bf[:, o * T + tt * 512: o * T + tt * 512 + 512]
                    R_dsl = R_d.sub(2 * (o * T + tt * 512), 2 * 512)
                    P.op("vector", lambda e, dsl=dsl, td=td, o=o:
                         e.tensor_scalar(dsl, td, sd_sb[:, o:o + 1], None,
                                         ALU.mult),
                         reads=[R_td, R_sd], writes=[R_dsl])
            A.pop()
            debug_tap("f1.d", d_bf, R_d, NC, BF16)
            if DEBUG_STOP == "down":
                return

            emit_postres(src_ap, src_name, dst_ap, dst_name, d_bf, R_d,
                         "f1.post")
            A.pop()

        for pi, ph in enumerate(phases):
            first = (pi == 0)
            last = (pi == len(phases) - 1)
            if ph in ("a0", "a1"):
                phase_attn(ph, first, last)
            elif ph == "f0":
                phase_f0(first, last)
            elif ph == "f1":
                phase_f1(first, last)

        for reg, tok in P.writers:
            if reg.space == "dram:xout":
                P.final_tokens.append(tok)

        P.emit(nc)
    return nc, sorted(dram.keys()), A.peak, P.n_ops, P


def _to_native(WT, in_ch, n_out):
    W = np.asarray(WT)
    return np.ascontiguousarray(
        W.reshape(in_ch, 128, n_out, 128).transpose(2, 1, 0, 3)
        .reshape(n_out, 128, in_ch * 128))


def _prep_host(inp):
    H = {}
    f32 = np.float32

    def ct(a):
        return np.ascontiguousarray(np.asarray(a, dtype=f32))

    H["wqT0"] = ct(np.asarray(inp["L0_attn_wq"], f32).T)
    H["wkT0"] = ct(np.asarray(inp["L0_attn_wk"], f32).T)
    H["wvT0"] = ct(np.asarray(inp["L0_attn_wv"], f32).T)
    H["woT0"] = ct(np.asarray(inp["L0_attn_wo"], f32).T)
    H["qnw"] = ct(inp["L0_attn_q_norm_w"]).reshape(1, 128)
    H["knw"] = ct(inp["L0_attn_k_norm_w"]).reshape(1, 128)
    cos16 = ct(inp["L0_attn_rope_cos"])[0:16, :].T
    sin16 = ct(inp["L0_attn_rope_sin"])[0:16, :].T
    H["ropeHC"] = np.ascontiguousarray(np.concatenate([cos16, cos16], 0))
    H["ropeHS"] = np.ascontiguousarray(np.concatenate([sin16, sin16], 0))
    perm = np.concatenate([np.arange(0, 128, 2), np.arange(1, 128, 2)])
    full_perm = np.concatenate([h * 128 + perm for h in range(16)])
    H["wqT0"] = _to_native(H["wqT0"][:, full_perm], NC, 16)
    H["wkT0"] = _to_native(H["wkT0"][:, full_perm], NC, 16)
    H["wvT0"] = _to_native(H["wvT0"], NC, 16)
    H["woT0"] = _to_native(H["woT0"], 16, NC)
    H["qnw"] = np.ascontiguousarray(H["qnw"][:, perm])
    H["knw"] = np.ascontiguousarray(H["knw"][:, perm])

    wgt = ct(np.asarray(inp["L0_ffn_Wg"], f32).T)
    H["WgT"] = np.ascontiguousarray(
        wgt.reshape(NC, 128, 16).transpose(1, 0, 2).reshape(128, NC * 16))
    W1 = np.asarray(inp["L0_ffn_W1"], f32)
    W3 = np.asarray(inp["L0_ffn_W3"], f32)
    W2 = np.asarray(inp["L0_ffn_W2"], f32)
    W1T = np.empty((NEX, D, I_MOE), f32)
    W3T = np.empty((NEX, D, I_MOE), f32)
    W2T = np.empty((NEX, I_MOE, D), f32)
    for e in range(16):
        W1T[e] = W1[e].T
        W3T[e] = W3[e].T
        W2T[e] = W2[e].T
    Ws1T = np.asarray(inp["L0_ffn_Ws1"], f32).T
    Ws3T = np.asarray(inp["L0_ffn_Ws3"], f32).T
    Ws2T = np.asarray(inp["L0_ffn_Ws2"], f32).T
    W1T[16] = Ws1T[:, 0:I_MOE]
    W1T[17] = Ws1T[:, I_MOE:]
    W3T[16] = Ws3T[:, 0:I_MOE]
    W3T[17] = Ws3T[:, I_MOE:]
    W2T[16] = Ws2T[0:I_MOE, :]
    W2T[17] = Ws2T[I_MOE:, :]
    W1Tn = np.empty((NEX, ICH, 128, D), f32)
    W3Tn = np.empty((NEX, ICH, 128, D), f32)
    W2Tn = np.empty((NEX, 16, 128, ICH * 128), f32)
    for e in range(NEX):
        W1Tn[e] = _to_native(W1T[e], NC, ICH)
        W3Tn[e] = _to_native(W3T[e], NC, ICH)
        W2Tn[e] = _to_native(W2T[e], ICH, NC)
    H["W1T"], H["W3T"], H["W2T"] = W1Tn, W3Tn, W2Tn
    H["identf"] = np.eye(128, dtype=f32)
    H["identb"] = np.eye(128, dtype=BF)
    esel = np.zeros((16, 16 * 128), f32)
    for e in range(16):
        esel[e, e * 128:(e + 1) * 128] = 1.0
    H["esel"] = esel

    H["wq1"] = _to_native(ct(inp["L1_attn_wq"]), NC, 16)
    H["wk1"] = _to_native(ct(inp["L1_attn_wk"]), NC, 4)
    H["wv1"] = _to_native(ct(inp["L1_attn_wv"]), NC, 4)
    H["wo1"] = _to_native(ct(inp["L1_attn_wo"]), 16, NC)
    H["bq1"] = ct(inp["L1_attn_bq"]).reshape(16, 128).T.copy()
    H["bk1"] = ct(inp["L1_attn_bk"]).reshape(4, 128).T.copy()
    H["bv1"] = ct(inp["L1_attn_bv"]).reshape(4, 128).T.copy()
    H["bo1"] = ct(inp["L1_attn_bo"]).reshape(16, 128).T.copy()

    H["wgT1"] = _to_native(np.asarray(inp["L1_ffn_wq_gate"], np.int32).T,
                           NC, L1HCH)
    H["wuT1"] = _to_native(np.asarray(inp["L1_ffn_wq_up"], np.int32).T,
                           NC, L1HCH)
    H["wdT1"] = _to_native(np.asarray(inp["L1_ffn_wq_down"], np.int32).T,
                           L1HCH, NC)
    H["sg1"] = ct(inp["L1_ffn_s_gate"]).reshape(L1HCH, 128).T.copy()
    H["su1"] = ct(inp["L1_ffn_s_up"]).reshape(L1HCH, 128).T.copy()
    H["sd1"] = ct(inp["L1_ffn_s_down"]).reshape(16, 128).T.copy()
    return H


def _prep_core(inp, H, c):
    b, hf = c // 2, c % 2
    lo = hf * T
    per = {}
    x = np.asarray(inp["L0_attn_x"], np.float32)
    per["x"] = np.ascontiguousarray(x[b].T[:, lo:lo + T])
    tk = np.arange(SEQ).reshape(16, 128)
    tq = lo + np.arange(T)
    m = (tk[:, :, None] <= tq[None, None, :]).astype(BF)
    per["pmask"] = np.ascontiguousarray(m.transpose(1, 0, 2).reshape(128, 16 * T))
    return per


_CACHE = {}


def kernel(**inputs):
    H = _prep_host(inputs)
    per_core = [_prep_core(inputs, H, c) for c in range(8)]
    x_cur = [per_core[c]["x"] for c in range(8)]

    for phases in LAUNCH_PLAN:
        if phases not in _CACHE:
            _CACHE[phases] = build_program(phases)
        nc, names, peak, nops = _CACHE[phases][:4]
        in_maps = []
        for c in range(8):
            m = {}
            for n in names:
                if n == "x":
                    m[n] = x_cur[c]
                elif n in per_core[c]:
                    m[n] = per_core[c][n]
                else:
                    m[n] = H[n]
            in_maps.append(m)
        res = run_bass_kernel_spmd(nc, in_maps, core_ids=list(range(8)))
        x_cur = [np.asarray(res.results[c]["xout"]) for c in range(8)]

    out = np.empty((NB, SEQ, D), np.float32)
    for c in range(8):
        b, hf = c // 2, c % 2
        out[b, hf * T:(hf + 1) * T, :] = x_cur[c].T
    return out
